# version 07
# baseline (speedup 1.0000x reference)
import math
from contextlib import ExitStack

import numpy as np

import concourse.bass as bass
import concourse.mybir as mybir
from concourse.bass_utils import run_bass_kernel_spmd

F32 = mybir.dt.float32
I32 = mybir.dt.int32
U32 = mybir.dt.uint32
ALU = mybir.AluOpType
ACT = mybir.ActivationFunctionType
AX = mybir.AxisListType

D = 2048
TLOC = 1024
NT = TLOC // 128
NH = 16
DH = 128
EPS_LN = 1e-5
FH = 5632
NCH = FH // 128
RSC = 1.0 / math.sqrt(DH)

ENGS = ("sync", "tensor", "vector", "scalar", "gpsimd")


class Sem:
    def __init__(self, handle, name):
        self.h = handle
        self.name = name
        self.n = 0

    def bump(self, k=1):
        self.n += k
        return self.n


class KB:
    def __init__(self, nc, sems):
        self.nc = nc
        self.q = {e: [] for e in ENGS}
        self.s = sems
        self.pe_pending = []

    def wait(self, eng, sem, val):
        if val <= 0:
            return
        self.q[eng].append(lambda e, s=sem.h, v=val: e.wait_ge(s, v))

    def op(self, eng, fn, sem=None, k=1):
        if sem is None:
            self.q[eng].append(fn)
            return None
        n = sem.bump(k)
        self.q[eng].append(lambda e, f=fn, s=sem.h, kk=k: f(e).then_inc(s, kk))
        if sem is self.s["pe"]:
            for cb in self.pe_pending:
                cb(n)
            self.pe_pending = []
        return n

    def V(self, fn, waits=()):
        dv = self.s["dv"]
        for (sem, val) in waits:
            self.wait("vector", sem, val)
        self.wait("vector", dv, dv.n)
        return self.op("vector", fn, dv)

    def C(self, fn, waits=()):
        ac = self.s["ac"]
        for (sem, val) in waits:
            self.wait("scalar", sem, val)
        self.wait("scalar", ac, ac.n)
        return self.op("scalar", fn, ac)

    def G(self, fn, waits=()):
        gp = self.s["gp"]
        for (sem, val) in waits:
            self.wait("gpsimd", sem, val)
        self.wait("gpsimd", gp, gp.n)
        return self.op("gpsimd", fn, gp)

    def T(self, fn, waits=(), inc=False):
        for (sem, val) in waits:
            self.wait("tensor", sem, val)
        return self.op("tensor", fn, self.s["pe"] if inc else None)

    def barrier(self):
        for ch in getattr(self, "channels", []):
            if ch.open:
                ch.close()
        marks = {nm: self.s[nm].n for nm in self.s}
        for eng in ENGS:
            for nm, v in marks.items():
                self.wait(eng, self.s[nm], v)


class LoadChan:

    def __init__(self, kb, sems, eng="sync"):
        self.kb = kb
        self.sems = sems
        self.eng = eng
        self.cur = 0
        self.closed_total = [0] * len(sems)
        self.open = False

    def load(self, out_ap, in_ap):
        kb = self.kb
        s = self.sems[self.cur]
        if not self.open:
            kb.wait(self.eng, s, self.closed_total[self.cur])
            self.open = True
        kb.op(self.eng, lambda e, o=out_ap, i=in_ap: e.dma_start(out=o, in_=i),
              s, 16)
        return s, s.n

    def close(self):
        s = self.sems[self.cur]
        tot = s.n
        self.closed_total[self.cur] = tot
        self.cur = (self.cur + 1) % len(self.sems)
        self.open = False
        return s, tot

    def load1(self, out_ap, in_ap):
        assert not self.open
        sem, n = self.load(out_ap, in_ap)
        self.close()
        return sem, n

    def totals(self):
        assert not self.open
        return [(sm, sm.n) for sm in self.sems]


def build_nc(phases=4):
    nc = bass.Bass("TRN2", target_bir_lowering=False)

    x_in = nc.dram_tensor("x_in", [TLOC, D], F32, kind="ExternalInput").ap()
    out_d = nc.dram_tensor("out", [TLOC, D], F32, kind="ExternalOutput").ap()

    l0_wq = nc.dram_tensor("l0_wq", [NH, 128, D], F32, kind="ExternalInput").ap()
    l0_wk = nc.dram_tensor("l0_wk", [NH, 128, D], F32, kind="ExternalInput").ap()
    l0_wv = nc.dram_tensor("l0_wv", [NH, 128, D], F32, kind="ExternalInput").ap()
    l0_wo = nc.dram_tensor("l0_wo", [D, D], F32, kind="ExternalInput").ap()
    l0_bq = nc.dram_tensor("l0_bq", [128, 16], F32, kind="ExternalInput").ap()
    l0_bk = nc.dram_tensor("l0_bk", [128, 16], F32, kind="ExternalInput").ap()
    l0_bv = nc.dram_tensor("l0_bv", [128, 16], F32, kind="ExternalInput").ap()
    l0_bo = nc.dram_tensor("l0_bo", [1, D], F32, kind="ExternalInput").ap()

    f0_wq = nc.dram_tensor("f0_wq", [4, 128, D], F32, kind="ExternalInput").ap()
    f0_k1t = nc.dram_tensor("f0_k1t", [4, 64, 128], F32, kind="ExternalInput").ap()
    f0_k2t = nc.dram_tensor("f0_k2t", [4, 64, 128], F32, kind="ExternalInput").ap()
    f0_down = nc.dram_tensor("f0_down", [16384, D], F32, kind="ExternalInput").ap()
    f0_up = nc.dram_tensor("f0_up", [16384, D], F32, kind="ExternalInput").ap()

    l1_wqkv = nc.dram_tensor("l1_wqkv", [48, 128, D], F32, kind="ExternalInput").ap()
    l1_wo = nc.dram_tensor("l1_wo", [D, D], F32, kind="ExternalInput").ap()

    f1_w1t = nc.dram_tensor("f1_w1t", [NCH, 128, D], F32, kind="ExternalInput").ap()
    f1_w2t = nc.dram_tensor("f1_w2t", [NCH, 128, D], F32, kind="ExternalInput").ap()
    f1_w3t = nc.dram_tensor("f1_w3t", [FH, D], F32, kind="ExternalInput").ap()

    c_ident = nc.dram_tensor("c_ident", [128, 128], F32, kind="ExternalInput").ap()
    c_iota = nc.dram_tensor("c_iota", [128, 64], F32, kind="ExternalInput").ap()
    cs_rows = nc.dram_tensor("cs_rows", [NH, 1], I32, kind="ExternalInput").ap()

    dr_kt = nc.dram_tensor("dr_kt", [D, TLOC], F32, kind="Internal").ap()
    dr_kt_ag = nc.dram_tensor("dr_kt_ag", [2 * D, TLOC], F32, kind="Internal").ap()
    dr_v = nc.dram_tensor("dr_v", [TLOC, D], F32, kind="Internal").ap()
    dr_v_ag = nc.dram_tensor("dr_v_ag", [2 * TLOC, D], F32, kind="Internal").ap()
    dr_qt = nc.dram_tensor("dr_qt", [D, TLOC], F32, kind="Internal").ap()
    dr_attn = nc.dram_tensor("dr_attn", [D, TLOC], F32, kind="Internal").ap()
    dr_xn = nc.dram_tensor("dr_xn", [TLOC, D], F32, kind="Internal").ap()
    dr_kt1 = nc.dram_tensor("dr_kt1", [D, TLOC], F32, kind="Internal").ap()
    dr_kt1_ag = nc.dram_tensor("dr_kt1_ag", [2 * D, TLOC], F32, kind="Internal").ap()
    dr_cs = nc.dram_tensor("dr_cs", [NH, 2048], F32, kind="Internal").ap()
    dr_cs_ar = nc.dram_tensor("dr_cs_ar", [NH, 2048], F32, kind="Internal").ap()
    dr_h = nc.dram_tensor("dr_h", [FH, TLOC], F32, kind="Internal").ap()
    dr_w3q = nc.dram_tensor("dr_w3q", [FH, D], F32, kind="Internal").ap()

    ctx = ExitStack()
    with ctx:
        def sbuf(name, shape, dt=F32):
            return ctx.enter_context(nc.sbuf_tensor(name, shape, dt))

        def psum(name, shape, dt=F32):
            return ctx.enter_context(nc.psum_tensor(name, shape, dt))

        RES = sbuf("RES", [128, NT * D])
        A = sbuf("A", [128, 16384])
        B = sbuf("B", [128, 8192])
        SM = sbuf("SM", [128, 1536])
        WR1 = sbuf("WR1", [128, 4096])
        WR2 = sbuf("WR2", [128, 2048])
        SBIAS = sbuf("SBIAS", [1, D])
        IDENT = sbuf("IDENT", [128, 128])
        IOTA = sbuf("IOTA", [128, 64])
        KEYS = sbuf("KEYS", [128, 1024])
        ONES1 = sbuf("ONES1", [1, 128])
        ONES128 = sbuf("ONES128", [128, 1])
        U1 = sbuf("U1", [128, 8], U32)
        U2 = sbuf("U2", [128, 8], U32)
        UP = sbuf("UPOS", [128, 8], U32)
        IDXI = sbuf("IDXI", [128, 32], I32)
        CSR = sbuf("CSR", [NH, 1], I32)

        PB = psum("PB", [128, 2048])
        PQ = psum("PQ", [128, 1024])
        PT1 = psum("PT1", [128, 512])
        PT2 = psum("PT2", [128, 512])

        sems = {}
        for nm in ["pe", "dv", "ac", "gp", "ld0", "ld1", "ld2", "ld3",
                   "st", "st2", "gd0", "gd1", "cc", "lw0", "lw1"]:
            sems[nm] = Sem(ctx.enter_context(nc.semaphore(nm)), nm)

        kb = KB(nc, sems)
        LD = LoadChan(kb, [sems["ld0"], sems["ld1"], sems["ld2"], sems["ld3"]])
        ST = LoadChan(kb, [sems["st"], sems["st2"]])
        LDW = LoadChan(kb, [sems["lw0"], sems["lw1"]], eng="scalar")
        kb.channels = [LD, ST, LDW]

        def STW(dst, src):
            sm_, tt_ = ST.load1(dst, src)
            kb.wait("vector", sm_, tt_)
            kb.wait("scalar", sm_, tt_)
            return sm_, tt_

        BQ = SM[:, 0:16]
        BK = SM[:, 16:32]
        BV = SM[:, 32:48]
        STM = 48

        def smc(i, w=1):
            return SM[:, STM + i:STM + i + w]

        SCO = SM[:, 80:112]
        SCOR = SM[:, 112:144]
        IDXF = SM[:, 144:176]
        HID = SM[:, 176:208]
        GG = SM[:, 208:240]
        S1 = SM[:, 240:248]
        S2 = SM[:, 248:256]
        FS = SM[:, 256:264]
        I1F = SM[:, 264:272]
        I2F = SM[:, 272:280]
        PF = SM[:, 280:288]
        I64 = SM[:, 288:352]
        S64 = SM[:, 352:416]
        EQB = SM[:, 416:928]
        SIM1 = SM[:, 928:1056]
        SIM2 = SM[:, 1056:1184]
        GAC1 = SM[:, 1184:1185]
        GAC2 = SM[:, 1185:1186]
        GAC3 = SM[:, 1186:1187]
        GPART = SM[:, 1187:1188]
        GTOT = SM[0:1, 1190:1191]
        CSML = SM[:, 1200:1216]

        V, C, G, T, W = kb.V, kb.C, kb.G, kb.T, kb.wait
        pe, dv, ac, gp, cc = (sems["pe"], sems["dv"], sems["ac"], sems["gp"],
                              sems["cc"])
        rg = [[0, 1], [2, 3], [4, 5], [6, 7]]

        for t in range(NT):
            LD.load(RES[:, t * D:(t + 1) * D], x_in[t * 128:(t + 1) * 128, :])
        LD.load(IDENT[:], c_ident)
        LD.load(IOTA[:], c_iota)
        LD.load(BQ, l0_bq)
        LD.load(BK, l0_bk)
        LD.load(BV, l0_bv)
        LD.load(SBIAS[:], l0_bo)
        LD.load(CSR[:], cs_rows)
        for h in range(4):
            LD.load(KEYS[0:64, h * 128:(h + 1) * 128], f0_k1t[h])
            LD.load(KEYS[64:128, h * 128:(h + 1) * 128], f0_k2t[h])
        s_pro, n_pro = LD.close()
        for eng in ("vector", "tensor", "scalar", "gpsimd"):
            kb.wait(eng, s_pro, n_pro)
        V(lambda e: e.memset(ONES1[:], 1.0))
        V(lambda e: e.memset(ONES128[:], 1.0))

        def layernorm_tile(t, xn_ap):
            rt = RES[:, t * D:(t + 1) * D]
            m = smc(0)
            vv = smc(1)
            inv = smc(2)
            t1 = smc(3)
            V(lambda e: e.reduce_sum(m, rt, AX.X))
            V(lambda e: e.tensor_scalar_mul(m, m, 1.0 / D))
            nc_ = V(lambda e: e.tensor_scalar(xn_ap, rt, m, None, ALU.subtract))
            nsq = C(lambda e: e.activation(B[:, 6144:8192], xn_ap, ACT.Square,
                                           accum_out=vv), waits=[(dv, nc_)])
            V(lambda e: e.tensor_scalar(vv, vv, 1.0 / D, EPS_LN,
                                        ALU.mult, ALU.add), waits=[(ac, nsq)])
            nv = sems["dv"].n
            na = C(lambda e: e.activation(inv, vv, ACT.Sqrt), waits=[(dv, nv)])
            V(lambda e: e.reciprocal(inv, inv), waits=[(ac, na)])
            V(lambda e: e.tensor_tensor(t1, inv, inv, ALU.mult))
            V(lambda e: e.tensor_tensor(t1, t1, vv, ALU.mult))
            V(lambda e: e.tensor_scalar(t1, t1, -0.5, 1.5, ALU.mult, ALU.add))
            V(lambda e: e.tensor_tensor(inv, inv, t1, ALU.mult))
            return V(lambda e: e.tensor_scalar_mul(xn_ap, xn_ap, inv))

        def ln_and_transpose(store_xn=None):
            tile_pe = {}
            bank_evac = {0: 0, 1: 0}
            for t in range(NT):
                xn = B[:, (t % 2) * D:((t % 2) + 1) * D]
                if (t - 2) in tile_pe:
                    kb.wait("vector", pe, tile_pe[t - 2])
                nln = layernorm_tile(t, xn)
                if store_xn is not None:
                    kb.wait("sync", dv, nln)
                    STW(store_xn[t * 128:(t + 1) * 128, :], xn)
                for c in range(16):
                    bk = c % 2
                    pb = PT1 if bk == 0 else PT2
                    waits = [(dv, nln)] if c < 2 else []
                    if bank_evac[bk]:
                        waits = waits + [(dv, bank_evac[bk])]
                    n = T(lambda e, s=xn[:, c * 128:(c + 1) * 128], p=pb:
                          e.transpose(p[:, 0:128], s, IDENT[:]),
                          waits=waits, inc=True)
                    dst = A[:, c * 1024 + t * 128: c * 1024 + (t + 1) * 128]
                    bank_evac[bk] = V(lambda e, p=pb, d=dst:
                                      e.tensor_copy(d, p[:, 0:128]),
                                      waits=[(pe, n)])
                tile_pe[t] = sems["pe"].n
            return sems["dv"].n

        class WRing:
            def __init__(self, area, slot_cols, nslots):
                self.area = area
                self.w = slot_cols
                self.n = nslots
                self.i = 0
                self.last_use = [0] * nslots

            def __init2__(self):
                pass

            def get(self, src_ap, rows=128, ch=None):
                ch = ch or LD
                sl = self.i % self.n
                self.i += 1
                ap = self.area[0:rows, sl * self.w:sl * self.w + self.w]
                kb.wait(ch.eng, pe, self.last_use[sl])
                sem, n = ch.load1(ap, src_ap)
                return ap, sl, (sem, n)

            def used(self, sl):
                def cb(v, sl=sl):
                    self.last_use[sl] = v
                kb.pe_pending.append(cb)

        wr1 = WRing(WR1, 2048, 2)
        wr2 = WRing(WR2, 512, 4)

        def proj_fm(w_dram, h, psum_ap):
            wt, sl, (sm, ln) = wr1.get(w_dram[h], ch=LDW)
            for half in range(2):
                for di in range(16):
                    rhs = A[:, di * 1024 + half * 512:
                            di * 1024 + half * 512 + 512]
                    T(lambda e, p=psum_ap, w=wt[:, di * 128:(di + 1) * 128],
                      r=rhs, st=(di == 0), sp=(di == 15), hh=half:
                      e.matmul(p[:, hh * 512:(hh + 1) * 512], w, r,
                               start=st, stop=sp),
                      waits=[(sm, ln)] if (di == 0 and half == 0) else [],
                      inc=(di == 15))
            wr1.used(sl)
            return sems["pe"].n

        def attn_softmax(qt_sb, kt_sb, probs, qt):
            nsc = None
            for kc in range(4):
                nsc = T(lambda e, q=qt_sb[:, qt * 128:(qt + 1) * 128],
                        r=kt_sb[:, kc * 512:(kc + 1) * 512], kc=kc:
                        e.matmul(PB[:, kc * 512:(kc + 1) * 512], q, r,
                                 start=True, stop=True),
                        inc=(kc == 3))
            nm = smc(4)
            rs = smc(5)
            ri = smc(6)
            V(lambda e: e.reduce_max(nm, PB[:], AX.X, negate=True),
              waits=[(pe, nsc)])
            nvm = V(lambda e: e.tensor_scalar_mul(nm, nm, RSC))
            ne = C(lambda e: e.activation(probs, PB[:], ACT.Exp, bias=nm,
                                          scale=RSC, accum_out=rs),
                   waits=[(dv, nvm)])
            nri = V(lambda e: e.reciprocal(ri, rs), waits=[(ac, ne)])
            nsc2 = C(lambda e: e.activation(probs, probs, ACT.Copy, scale=ri),
                     waits=[(dv, nri)])
            return nsc2

        gsteps = []
        if phases >= 4:
            gpar = [0]
            gmarkp = {0: 0, 1: 0}
            gfirst = {}

            def g_step(src, gac):
                def run():
                    p = gpar[0] % 2
                    gpar[0] += 1
                    wdt = B[:, p * 2048:(p + 1) * 2048]
                    kb.wait("sync", dv, gmarkp[p])
                    s_w, n_w = LD.load1(wdt, src)
                    V(lambda e: e.reduce_sum(GPART, wdt, AX.X,
                                             apply_absolute_value=True),
                      waits=[(s_w, n_w)])
                    if gac.offset not in gfirst:
                        gfirst[gac.offset] = True
                        nr = V(lambda e: e.tensor_copy(gac, GPART))
                    else:
                        nr = V(lambda e: e.tensor_tensor(gac, gac, GPART,
                                                         ALU.add))
                    gmarkp[p] = nr
                return run

            for c_ in range(NCH):
                gsteps.append(g_step(f1_w1t[c_], GAC1))
            for c_ in range(NCH):
                gsteps.append(g_step(f1_w2t[c_], GAC2))
            for c_ in range(NCH):
                gsteps.append(g_step(f1_w3t[c_ * 128:(c_ + 1) * 128, :], GAC3))

        nxt = ln_and_transpose()
        kb.wait("tensor", dv, nxt)
        if phases >= 4:
            gmarkp[0] = nxt
            gmarkp[1] = nxt

        for h in range(NH):
            for _ in range(9):
                if gsteps:
                    gsteps.pop(0)()
            nq = proj_fm(l0_wq, h, PQ)
            qsb = B[:, 4096:5120]
            nv = V(lambda e, b=BQ[:, h:h + 1], d=qsb:
                   e.tensor_scalar_add(d, PQ[:], b), waits=[(pe, nq)])
            kb.wait("sync", dv, nv)
            STW(dr_qt[h * 128:(h + 1) * 128, :], qsb)
            kb.wait("tensor", dv, nv)
            nk = proj_fm(l0_wk, h, PQ)
            ksb = B[:, 5120:6144]
            nv = V(lambda e, b=BK[:, h:h + 1], d=ksb:
                   e.tensor_scalar_add(d, PQ[:], b), waits=[(pe, nk)])
            kb.wait("sync", dv, nv)
            STW(dr_kt[h * 128:(h + 1) * 128, :], ksb)
            kb.wait("tensor", dv, nv)
            nvm = proj_fm(l0_wv, h, PQ)
            vsb = B[:, 4096:5120]
            nv = V(lambda e, b=BV[:, h:h + 1], d=vsb:
                   e.tensor_scalar_add(d, PQ[:], b), waits=[(pe, nvm)])
            stage = B[:, 6144:6272]
            prev_ev = 0
            for t in range(NT):
                pb = PT1 if t % 2 == 0 else PT2
                waits = [(dv, nv)]
                if prev_ev:
                    waits.append((dv, prev_ev))
                n = T(lambda e, s=vsb[:, t * 128:(t + 1) * 128], p=pb:
                      e.transpose(p[:, 0:128], s, IDENT[:]),
                      waits=waits, inc=True)
                nve = V(lambda e, p=pb, d=stage: e.tensor_copy(d, p[:, 0:128]),
                        waits=[(pe, n)])
                kb.wait("sync", dv, nve)
                s_s, n_s = ST.load1(
                    dr_v[t * 128:(t + 1) * 128, h * 128:(h + 1) * 128], stage)
                kb.wait("vector", s_s, n_s)
                prev_ev = nve
            kb.wait("tensor", dv, nv)
        while gsteps:
            gsteps.pop(0)()
        n_gfin = 0
        if phases >= 4:
            T1P = smc(10)
            T1N = smc(12)
            T2P = smc(14)
            T2N = smc(16)
            T3P = smc(18)
            T3N = smc(20)

            def g_fin(gac, tp_ap, tn_ap):
                ng = sems["dv"].n
                ntt = T(lambda e, g=gac: e.matmul(PT2[0:1, 0:1], g, ONES128[:],
                                                  start=True, stop=True),
                        waits=[(dv, ng)], inc=True)
                V(lambda e: e.tensor_copy(GTOT, PT2[0:1, 0:1]),
                  waits=[(pe, ntt)])
                V(lambda e: e.tensor_scalar(GTOT, GTOT, 0.5 / (FH * D),
                                            0.5e-8, ALU.mult, ALU.add))
                ngt = sems["dv"].n
                nrep = T(lambda e: e.matmul(PT2[:, 0:1], ONES1[:], GTOT,
                                            start=True, stop=True),
                         waits=[(dv, ngt)], inc=True)
                V(lambda e, d=tp_ap: e.tensor_copy(d, PT2[:, 0:1]),
                  waits=[(pe, nrep)])
                return V(lambda e, d=tn_ap, sr=tp_ap:
                         e.tensor_scalar_mul(d, sr, -1.0))

            g_fin(GAC1, T1P, T1N)
            g_fin(GAC2, T2P, T2N)
            n_gfin = g_fin(GAC3, T3P, T3N)

        for (sm_, tt_) in ST.totals():
            kb.wait("gpsimd", sm_, tt_)
        n_cc_k = kb.op("gpsimd", lambda e: e.collective_compute(
            "AllGather", ALU.bypass, replica_groups=rg,
            ins=[dr_kt], outs=[dr_kt_ag]), cc)
        n_cc_v = kb.op("gpsimd", lambda e: e.collective_compute(
            "AllGather", ALU.bypass, replica_groups=rg,
            ins=[dr_v], outs=[dr_v_ag]), cc)

        kb.wait("sync", cc, n_cc_v)
        for h in range(NH):
            kt_sb = A[:, 0:2048]
            v_sb = A[:, 2048:4096]
            qt_sb = A[:, 4096:5120]
            probs = A[:, 5120:7168]
            pt_sb = A[:, 7168:9216]
            att_h = B[:, 0:1024]
            kb.wait("sync", pe, sems["pe"].n)
            kb.wait("sync", dv, sems["dv"].n)
            kb.wait("sync", ac, sems["ac"].n)
            LD.load(kt_sb[:, 0:1024], dr_kt_ag[h * 128:(h + 1) * 128, :])
            LD.load(kt_sb[:, 1024:2048],
                    dr_kt_ag[D + h * 128:D + (h + 1) * 128, :])
            LD.load(v_sb.rearrange("p (kt d) -> p kt d", d=128),
                    dr_v_ag[:, h * 128:(h + 1) * 128]
                    .rearrange("(kt p) d -> p kt d", p=128))
            LD.load(qt_sb[:], dr_qt[h * 128:(h + 1) * 128, :])
            s_h, n_h = LD.close()
            for eng in ("tensor", "vector", "scalar"):
                kb.wait(eng, s_h, n_h)
            for qt in range(NT):
                nsm = attn_softmax(qt_sb, kt_sb, probs, qt)
                lastpt = []
                bank_evac = {0: 0, 1: 0}
                for kc16 in range(16):
                    bk = kc16 % 2
                    pb = PT1 if bk == 0 else PT2
                    waits = [(ac, nsm)] if kc16 < 2 else []
                    if bank_evac[bk]:
                        waits = waits + [(dv, bank_evac[bk])]
                    n = T(lambda e, s=probs[:, kc16 * 128:(kc16 + 1) * 128],
                          p=pb: e.transpose(p[:, 0:128], s, IDENT[:]),
                          waits=waits, inc=True)
                    nve = V(lambda e, p=pb,
                            d=pt_sb[:, kc16 * 128:(kc16 + 1) * 128]:
                            e.tensor_copy(d, p[:, 0:128]), waits=[(pe, n)])
                    bank_evac[bk] = nve
                    lastpt.append(nve)
                nav = None
                for kc16 in range(16):
                    nav = T(lambda e, v=v_sb[:, kc16 * 128:(kc16 + 1) * 128],
                            r=pt_sb[:, kc16 * 128:(kc16 + 1) * 128],
                            st=(kc16 == 0), sp=(kc16 == 15):
                            e.matmul(PQ[:, 0:128], v, r, start=st, stop=sp),
                            waits=[(dv, lastpt[15])] if kc16 == 0 else [],
                            inc=(kc16 == 15))
                nve = V(lambda e, d=att_h[:, qt * 128:(qt + 1) * 128]:
                        e.tensor_copy(d, PQ[:, 0:128]), waits=[(pe, nav)])
                kb.wait("tensor", dv, nve)
                kb.wait("tensor", ac, nsm)
            kb.wait("sync", dv, sems["dv"].n)
            STW(dr_attn[h * 128:(h + 1) * 128, :], att_h)
        for (sm_, tt_) in ST.totals():
            kb.wait("sync", sm_, tt_)
        atr = WRing(B, 128, 16)
        for t in range(NT):
            for blk in range(4):
                T(lambda e, blk=blk:
                  e.matmul(PB[:, blk * 512:(blk + 1) * 512], ONES1[:],
                           SBIAS[:, blk * 512:(blk + 1) * 512],
                           start=True, stop=False))
            for hc in range(16):
                at, sl, (sm, ln) = atr.get(
                    dr_attn[hc * 128:(hc + 1) * 128, t * 128:(t + 1) * 128])
                for blk in range(4):
                    wt, sl2, (sm2, ln2) = wr2.get(
                        l0_wo[hc * 128:(hc + 1) * 128,
                              blk * 512:(blk + 1) * 512])
                    T(lambda e, a=at, w=wt, blk=blk, sp=(hc == 15):
                      e.matmul(PB[:, blk * 512:(blk + 1) * 512], a, w,
                               start=False, stop=sp),
                      waits=[(sm, ln), (sm2, ln2)],
                      inc=(blk == 3))
                    wr2.used(sl2)
                atr.used(sl)
            npj = sems["pe"].n
            nve = V(lambda e, t=t: e.tensor_tensor(
                RES[:, t * D:(t + 1) * D], PB[:],
                RES[:, t * D:(t + 1) * D], ALU.add), waits=[(pe, npj)])
            kb.wait("tensor", dv, nve)

        if phases >= 2:
            kb.barrier()
            nxt = ln_and_transpose(store_xn=dr_xn)
            kb.wait("tensor", dv, nxt)
            for hq in range(4):
                nq = proj_fm(f0_wq, hq, PQ)
                nv = V(lambda e, d=B[:, hq * 1024:(hq + 1) * 1024]:
                       e.tensor_copy(d, PQ[:]), waits=[(pe, nq)])
                kb.wait("tensor", dv, nv)
            sim_ev = {0: 0, 1: 0}
            for t in range(NT):
                xn2 = B[:, 4096:6144]
                outacc = B[:, 6144:8192]
                kb.wait("sync", dv, sems["dv"].n)
                kb.wait("sync", gp, sems["gp"].n)
                LD.load(xn2, dr_xn[t * 128:(t + 1) * 128, :])
                s_t, n_t = LD.close()
                kb.wait("vector", s_t, n_t)
                for h in range(4):
                    q1 = B[0:64, h * 1024 + t * 128:h * 1024 + (t + 1) * 128]
                    q2 = B[64:128, h * 1024 + t * 128:h * 1024 + (t + 1) * 128]
                    w1 = [(dv, sim_ev[0])] if sim_ev[0] else []
                    n1 = T(lambda e, q=q1,
                           k=KEYS[0:64, h * 128:(h + 1) * 128]:
                           e.matmul(PT1[:, 0:128], q, k, start=True,
                                    stop=True), waits=w1, inc=True)
                    nv1 = V(lambda e: e.tensor_copy(SIM1, PT1[:, 0:128]),
                            waits=[(pe, n1)])
                    sim_ev[0] = nv1
                    w2 = [(dv, sim_ev[1])] if sim_ev[1] else []
                    n2 = T(lambda e, q=q2,
                           k=KEYS[64:128, h * 128:(h + 1) * 128]:
                           e.matmul(PT2[:, 0:128], q, k, start=True,
                                    stop=True), waits=w2, inc=True)
                    nv2 = V(lambda e: e.tensor_copy(SIM2, PT2[:, 0:128]),
                            waits=[(pe, n2)])
                    sim_ev[1] = nv2
                    V(lambda e: e.max(out=S1, in_=SIM1))
                    V(lambda e: e.max_index(out=U1[:], in_max=S1, in_values=SIM1))
                    V(lambda e: e.max(out=S2, in_=SIM2))
                    V(lambda e: e.max_index(out=U2[:], in_max=S2, in_values=SIM2))
                    V(lambda e: e.tensor_copy(I1F, U1[:]))
                    V(lambda e: e.tensor_copy(I2F, U2[:]))
                    V(lambda e: e.tensor_tensor(
                        S64.rearrange("p (i j) -> p i j", j=8),
                        S1.unsqueeze(2).to_broadcast([128, 8, 8]),
                        S2.unsqueeze(1).to_broadcast([128, 8, 8]), ALU.add))
                    V(lambda e: e.tensor_scalar_mul(I1F, I1F, 128.0))
                    V(lambda e: e.tensor_tensor(
                        I64.rearrange("p (i j) -> p i j", j=8),
                        I1F.unsqueeze(2).to_broadcast([128, 8, 8]),
                        I2F.unsqueeze(1).to_broadcast([128, 8, 8]), ALU.add))
                    V(lambda e: e.max(out=FS, in_=S64))
                    V(lambda e: e.max_index(out=UP[:], in_max=FS, in_values=S64))
                    V(lambda e: e.tensor_copy(PF, UP[:]))
                    V(lambda e, h=h: e.tensor_copy(SCO[:, h * 8:(h + 1) * 8],
                                                   FS))
                    V(lambda e: e.tensor_tensor(
                        EQB.rearrange("p (s j) -> p s j", j=64),
                        PF.unsqueeze(2).to_broadcast([128, 8, 64]),
                        IOTA[:].unsqueeze(1).to_broadcast([128, 8, 64]),
                        ALU.is_equal))
                    V(lambda e: e.tensor_tensor(
                        EQB.rearrange("p (s j) -> p s j", j=64),
                        EQB.rearrange("p (s j) -> p s j", j=64),
                        I64.unsqueeze(1).to_broadcast([128, 8, 64]), ALU.mult))
                    V(lambda e, h=h: e.tensor_reduce(
                        IDXF[:, h * 8:(h + 1) * 8],
                        EQB.rearrange("p (s j) -> p s j", j=64),
                        AX.X, ALU.add))
                kb.wait("vector", sems["gd0"], sems["gd0"].n)
                kb.wait("vector", sems["gd1"], sems["gd1"].n)
                nidx = V(lambda e: e.tensor_copy(IDXI[:], IDXF))
                nrel = V(lambda e: e.tensor_scalar_max(SCOR, SCO, 0.0))
                gmark = {}
                for g in range(16):
                    par = g % 2
                    gdsem = sems["gd0"] if par == 0 else sems["gd1"]
                    kb.wait("gpsimd", dv, nidx)
                    if g >= 2:
                        kb.wait("gpsimd", dv, gmark[g - 2])
                    dslot = [A[:, par * 4096 + j2 * 2048:
                               par * 4096 + (j2 + 1) * 2048] for j2 in range(2)]
                    uslot = [A[:, 8192 + par * 4096 + j2 * 2048:
                               8192 + par * 4096 + (j2 + 1) * 2048]
                             for j2 in range(2)]
                    for j2 in range(2):
                        j = g * 2 + j2
                        kb.op("gpsimd",
                              lambda e, j=j, d=dslot[j2]: e.indirect_dma_start(
                                  out=d, out_offset=None, in_=f0_down,
                                  in_offset=bass.IndirectOffsetOnAxis(
                                      IDXI[:, j:j + 1], 0)), gdsem, 16)
                        kb.op("gpsimd",
                              lambda e, j=j, u=uslot[j2]: e.indirect_dma_start(
                                  out=u, out_offset=None, in_=f0_up,
                                  in_offset=bass.IndirectOffsetOnAxis(
                                      IDXI[:, j:j + 1], 0)), gdsem, 16)
                    ngd = gdsem.n
                    nh4 = None
                    for j2 in range(2):
                        j = g * 2 + j2
                        nmul = kb.G(lambda e, d=dslot[j2]: e.tensor_tensor(
                            d, d, xn2, ALU.mult),
                            waits=[(gdsem, ngd), (s_t, n_t)])
                        nh4 = V(lambda e, d=dslot[j2], j=j: e.tensor_reduce(
                            HID[:, j:j + 1], d, AX.X, ALU.add),
                            waits=[(gp, nmul)])
                    nsl = C(lambda e, g=g: e.activation(
                        GG[:, g * 2:(g + 1) * 2],
                        HID[:, g * 2:(g + 1) * 2], ACT.Silu),
                        waits=[(dv, nh4)])
                    ngg = V(lambda e, g=g: e.tensor_tensor(
                        GG[:, g * 2:(g + 1) * 2], GG[:, g * 2:(g + 1) * 2],
                        SCOR[:, g * 2:(g + 1) * 2], ALU.mult),
                        waits=[(ac, nsl), (dv, nrel)])
                    for j2 in range(2):
                        j = g * 2 + j2
                        up = uslot[j2]
                        gcol = GG[:, j:j + 1]
                        if j == 0:
                            V(lambda e, u=up, gc=gcol: e.tensor_scalar(
                                outacc, u, gc, None, ALU.mult))
                        else:
                            V(lambda e, u=up, gc=gcol: e.scalar_tensor_tensor(
                                outacc, u, gc, outacc, ALU.mult, ALU.add))
                    gmark[g] = sems["dv"].n
                V(lambda e, t=t: e.tensor_tensor(
                    RES[:, t * D:(t + 1) * D], outacc,
                    RES[:, t * D:(t + 1) * D], ALU.add))

        if phases >= 3:
            kb.barrier()
            nxt = ln_and_transpose()
            kb.wait("tensor", dv, nxt)
            for h in range(NH):
                nq = proj_fm(l1_wqkv, h, PQ)
                qsb = B[:, 0:1024]
                nv = V(lambda e, d=qsb: e.tensor_copy(d, PQ[:]),
                       waits=[(pe, nq)])
                kb.wait("sync", dv, nv)
                STW(dr_qt[h * 128:(h + 1) * 128, :], qsb)
                kb.wait("tensor", dv, nv)
                nk = proj_fm(l1_wqkv, NH + h, PQ)
                ksb = B[:, 1024:2048]
                nv = V(lambda e, d=ksb: e.tensor_copy(d, PQ[:]),
                       waits=[(pe, nk)])
                kb.wait("sync", dv, nv)
                STW(dr_kt1[h * 128:(h + 1) * 128, :], ksb)
                kb.wait("tensor", dv, nv)
            for t in range(NT):
                for dc in range(16):
                    wt, sl, (sm, ln) = wr1.get(l1_wqkv[32 + dc], ch=LDW)
                    for di in range(16):
                        T(lambda e, w=wt[:, di * 128:(di + 1) * 128], dc=dc,
                          x=A[:, di * 1024 + t * 128:
                              di * 1024 + (t + 1) * 128],
                          st=(di == 0), sp=(di == 15):
                          e.matmul(PB[:, dc * 128:(dc + 1) * 128], x, w,
                                   start=st, stop=sp),
                          waits=[(sm, ln)] if di == 0 else [],
                          inc=(di == 15))
                    wr1.used(sl)
                nvt = sems["pe"].n
                vsb = B[:, 2048:4096]
                nv = V(lambda e, d=vsb: e.tensor_copy(d, PB[:]),
                       waits=[(pe, nvt)])
                kb.wait("sync", dv, nv)
                STW(dr_v[t * 128:(t + 1) * 128, :], vsb)
                kb.wait("tensor", dv, nv)
            for (sm_, tt_) in ST.totals():
                kb.wait("gpsimd", sm_, tt_)
            n_cc_k1 = kb.op("gpsimd", lambda e: e.collective_compute(
                "AllGather", ALU.bypass, replica_groups=rg,
                ins=[dr_kt1], outs=[dr_kt1_ag]), cc)
            kb.wait("sync", cc, n_cc_k1)
            rows_acc = B[:, 0:2048]
            for h in range(NH):
                kt_sb = A[:, 0:2048]
                qt_sb = A[:, 4096:5120]
                probs = A[:, 5120:7168]
                kb.wait("sync", pe, sems["pe"].n)
                kb.wait("sync", dv, sems["dv"].n)
                kb.wait("sync", ac, sems["ac"].n)
                LD.load(kt_sb[:, 0:1024], dr_kt1_ag[h * 128:(h + 1) * 128, :])
                LD.load(kt_sb[:, 1024:2048],
                        dr_kt1_ag[D + h * 128:D + (h + 1) * 128, :])
                LD.load(qt_sb[:], dr_qt[h * 128:(h + 1) * 128, :])
                s_h, n_h = LD.close()
                for eng in ("tensor", "vector", "scalar"):
                    kb.wait(eng, s_h, n_h)
                nacc = 0
                for qt in range(NT):
                    nsm = attn_softmax(qt_sb, kt_sb, probs, qt)
                    if qt == 0:
                        nacc = V(lambda e: e.tensor_copy(rows_acc, probs),
                                 waits=[(ac, nsm)])
                    else:
                        nacc = V(lambda e: e.tensor_tensor(
                            rows_acc, rows_acc, probs, ALU.add),
                            waits=[(ac, nsm)])
                    kb.wait("tensor", ac, nsm)
                for kc in range(4):
                    ncs = T(lambda e, kc=kc:
                            e.matmul(PT1[0:1, 0:512], ONES128[:],
                                     rows_acc[:, kc * 512:(kc + 1) * 512],
                                     start=True, stop=True),
                            waits=[(dv, nacc)] if kc == 0 else [], inc=True)
                    nve = V(lambda e, kc=kc: e.tensor_copy(
                        SBIAS[0:1, kc * 512:(kc + 1) * 512],
                        PT1[0:1, 0:512]), waits=[(pe, ncs)])
                    kb.wait("tensor", dv, nve)
                kb.wait("sync", dv, sems["dv"].n)
                STW(dr_cs[h:h + 1, :], SBIAS[0:1, :])
            for (sm_, tt_) in ST.totals():
                kb.wait("gpsimd", sm_, tt_)
            n_cc_cs = kb.op("gpsimd", lambda e: e.collective_compute(
                "AllReduce", ALU.add, replica_groups=rg,
                ins=[dr_cs], outs=[dr_cs_ar]), cc)
            csl = B[:, 4096:5120]
            kb.wait("gpsimd", cc, n_cc_cs)
            kb.wait("gpsimd", sems["gd0"], sems["gd0"].n)
            ngsel_sem = sems["gd0"]
            ngsel = kb.op("gpsimd", lambda e: e.indirect_dma_start(
                out=csl[0:16, :], out_offset=None,
                in_=dr_cs_ar.rearrange("h (s t) -> (h s) t", s=2),
                in_offset=bass.IndirectOffsetOnAxis(CSR[:], 0)),
                ngsel_sem, 16)
            for t in range(NT):
                v_sb = B[:, 0:2048]
                kb.wait("sync", pe, sems["pe"].n)
                kb.wait("sync", dv, sems["dv"].n)
                LD.load(v_sb, dr_v[t * 128:(t + 1) * 128, :])
                s_v, n_v = LD.close()
                nct = T(lambda e, s=csl[0:16, t * 128:(t + 1) * 128]:
                        e.transpose(PT1[:, 0:16], s, IDENT[0:16, 0:16]),
                        waits=[(ngsel_sem, ngsel)], inc=True)
                nve = V(lambda e: e.tensor_copy(CSML, PT1[:, 0:16]),
                        waits=[(pe, nct)])
                nsv = V(lambda e: e.tensor_tensor(
                    v_sb.rearrange("p (h d) -> p h d", d=128),
                    v_sb.rearrange("p (h d) -> p h d", d=128),
                    CSML.unsqueeze(2).to_broadcast([128, 16, 128]),
                    ALU.mult), waits=[(s_v, n_v)])
                bank_evac = {0: 0, 1: 0}
                for hc in range(16):
                    bk = hc % 2
                    pb = PT1 if bk == 0 else PT2
                    waits = [(dv, nsv)] if hc < 2 else []
                    if bank_evac[bk]:
                        waits = waits + [(dv, bank_evac[bk])]
                    n = T(lambda e, s=v_sb[:, hc * 128:(hc + 1) * 128], p=pb:
                          e.transpose(p[:, 0:128], s, IDENT[:]),
                          waits=waits, inc=True)
                    bank_evac[bk] = V(
                        lambda e, p=pb, d=A[:, hc * 128:(hc + 1) * 128]:
                        e.tensor_copy(d, p[:, 0:128]), waits=[(pe, n)])
                nvals = sems["dv"].n
                kb.wait("tensor", dv, nvals)
                for hc in range(16):
                    for blk in range(4):
                        wt, sl2, (sm2, ln2) = wr2.get(
                            l1_wo[hc * 128:(hc + 1) * 128,
                                  blk * 512:(blk + 1) * 512])
                        T(lambda e, a=A[:, hc * 128:(hc + 1) * 128], w=wt,
                          blk=blk, st=(hc == 0), sp=(hc == 15):
                          e.matmul(PB[:, blk * 512:(blk + 1) * 512], a, w,
                                   start=st, stop=sp),
                          waits=[(sm2, ln2)], inc=(blk == 3))
                        wr2.used(sl2)
                npj = sems["pe"].n
                nve = V(lambda e, t=t: e.tensor_tensor(
                    RES[:, t * D:(t + 1) * D], PB[:],
                    RES[:, t * D:(t + 1) * D], ALU.add), waits=[(pe, npj)])
                kb.wait("tensor", dv, nve)

        if phases >= 4:
            kb.barrier()
            nxt = ln_and_transpose()
            kb.wait("tensor", dv, nxt)

            def quantize(dst_ap, src_ap, tp, tn, tmp_ap, wl):
                V(lambda e: e.tensor_scalar(dst_ap, src_ap, tp, None,
                                            ALU.is_gt), waits=wl)
                V(lambda e: e.tensor_scalar(tmp_ap, src_ap, tn, None,
                                            ALU.is_lt))
                return V(lambda e: e.tensor_tensor(dst_ap, dst_ap, tmp_ap,
                                                   ALU.subtract))

            QWa = B[:, 4096:6144]
            QWb = B[:, 6144:8192]
            QTMP = B[:, 0:2048]
            qw_mark = {0: 0, 1: 0}
            for c in range(NCH):
                for wi, (wsrc, psm, tp, tn) in enumerate(
                        ((f1_w1t, PQ, T1P, T1N), (f1_w2t, PB, T2P, T2N))):
                    wt, sl, (sm, ln) = wr1.get(wsrc[c], ch=LDW)
                    par = wi
                    qw = QWa if par == 0 else QWb
                    kb.wait("vector", pe, qw_mark[par])
                    nq = quantize(qw, wt, tp, tn, QTMP,
                                  [(sm, ln), (dv, n_gfin)])
                    for di in range(16):
                        for half in range(2):
                            rhs = A[:, di * 1024 + half * 512:
                                    di * 1024 + half * 512 + 512]
                            T(lambda e, p=psm, di=di,
                              w=qw[:, di * 128:(di + 1) * 128], r=rhs,
                              st=(di == 0), sp=(di == 15), hh=half:
                              e.matmul(p[:, hh * 512:(hh + 1) * 512], w, r,
                                       start=st, stop=sp),
                              waits=[(dv, nq)] if di == 0 else [],
                              inc=(half == 1))
                    def _cb(v, par=par):
                        qw_mark[par] = v
                    kb.pe_pending.append(_cb)
                    wr1.used(sl)
                w3raw = QTMP
                w3tmp = B[:, 2048:3072 + 1024][:, 0:1024]
                kb.wait("sync", dv, sems["dv"].n)
                sw3, lw3 = LD.load1(w3raw, f1_w3t[c * 128:(c + 1) * 128, :])
                V(lambda e, w=w3raw: e.tensor_scalar(
                    B[:, 2048:4096], w, T3P, None, ALU.is_gt),
                  waits=[(sw3, lw3), (dv, n_gfin)])
                V(lambda e, w=w3raw: e.tensor_scalar(w, w, T3N, None,
                                                     ALU.is_lt))
                nw3q = V(lambda e, w=w3raw: e.tensor_tensor(
                    w, B[:, 2048:4096], w, ALU.subtract))
                kb.wait("sync", dv, nw3q)
                STW(dr_w3q[c * 128:(c + 1) * 128, :], w3raw)
                npb = sems["pe"].n
                hsb = B[:, 2048:3072]
                na = C(lambda e: e.activation(hsb, PQ[:], ACT.Sigmoid),
                       waits=[(pe, npb), (dv, nw3q)])
                nh = V(lambda e: e.tensor_tensor(hsb, hsb, PB[:, 0:1024],
                                                 ALU.mult), waits=[(ac, na)])
                kb.wait("sync", dv, nh)
                STW(dr_h[c * 128:(c + 1) * 128, :], hsb)
                kb.wait("tensor", dv, nh)
                kb.wait("tensor", ac, na)
            for (sm_, tt_) in ST.totals():
                kb.wait("sync", sm_, tt_)

            htr = WRing(B, 128, 16)
            w3mark = {0: 0, 1: 0}
            for tp_i in range(4):
                t0, t1 = 2 * tp_i, 2 * tp_i + 1
                for c in range(NCH):
                    p3 = c % 2
                    w3raw = B[:, 2048 + p3 * 2048:2048 + (p3 + 1) * 2048]
                    kb.wait("scalar", pe, w3mark[p3])
                    sm3, ln3 = LDW.load1(w3raw,
                                         dr_w3q[c * 128:(c + 1) * 128, :])
                    nq3 = 0
                    hA, slA, (smA, lnA) = htr.get(
                        dr_h[c * 128:(c + 1) * 128, t0 * 128:(t0 + 1) * 128])
                    hB, slB, (smB, lnB) = htr.get(
                        dr_h[c * 128:(c + 1) * 128, t1 * 128:(t1 + 1) * 128])
                    for blk in range(4):
                        T(lambda e, a=hA,
                          w=w3raw[:, blk * 512:(blk + 1) * 512],
                          blk=blk, st=(c == 0), sp=(c == NCH - 1):
                          e.matmul(PB[:, blk * 512:(blk + 1) * 512], a, w,
                                   start=st, stop=sp),
                          waits=[(smA, lnA), (sm3, ln3)] if blk == 0
                          else [], inc=True)
                        pqs = (PQ[:, 0:512], PQ[:, 512:1024],
                               PT1[:, 0:512], PT2[:, 0:512])[blk]
                        T(lambda e, a=hB,
                          w=w3raw[:, blk * 512:(blk + 1) * 512],
                          p=pqs, st=(c == 0), sp=(c == NCH - 1):
                          e.matmul(p, a, w, start=st, stop=sp),
                          waits=[(smB, lnB)] if blk == 0 else [],
                          inc=(blk == 3))
                    def _cb3(v, p3=p3):
                        w3mark[p3] = v
                    kb.pe_pending.append(_cb3)
                    htr.used(slA)
                    htr.used(slB)
                npj = sems["pe"].n
                nv0 = V(lambda e, t=t0: e.tensor_tensor(
                    RES[:, t * D:(t + 1) * D], PB[:],
                    RES[:, t * D:(t + 1) * D], ALU.add), waits=[(pe, npj)])
                V(lambda e, t=t1: e.tensor_tensor(
                    RES[:, t * D:t * D + 512], PQ[:, 0:512],
                    RES[:, t * D:t * D + 512], ALU.add))
                V(lambda e, t=t1: e.tensor_tensor(
                    RES[:, t * D + 512:t * D + 1024], PQ[:, 512:1024],
                    RES[:, t * D + 512:t * D + 1024], ALU.add))
                V(lambda e, t=t1: e.tensor_tensor(
                    RES[:, t * D + 1024:t * D + 1536], PT1[:, 0:512],
                    RES[:, t * D + 1024:t * D + 1536], ALU.add))
                nv1 = V(lambda e, t=t1: e.tensor_tensor(
                    RES[:, t * D + 1536:t * D + 2048], PT2[:, 0:512],
                    RES[:, t * D + 1536:t * D + 2048], ALU.add))
                kb.wait("tensor", dv, nv1)

        kb.barrier()
        for t in range(NT):
            ST.load1(out_d[t * 128:(t + 1) * 128, :], RES[:, t * D:(t + 1) * D])
        for (sm_, tt_) in ST.totals():
            kb.wait("sync", sm_, tt_)

        with nc.Block() as block:
            @block.sync
            def _(e):
                for fn in kb.q["sync"]:
                    fn(e)

            @block.tensor
            def _(e):
                for fn in kb.q["tensor"]:
                    fn(e)

            @block.vector
            def _(e):
                for fn in kb.q["vector"]:
                    fn(e)

            @block.scalar
            def _(e):
                for fn in kb.q["scalar"]:
                    fn(e)

            @block.gpsimd
            def _(e):
                for fn in kb.q["gpsimd"]:
                    fn(e)

    return nc


def _vtile(v):
    v = np.asarray(v, dtype=np.float32)
    ndi, ndo = v.shape[0] // 128, v.shape[1] // 128
    t = v.reshape(ndi, 128, ndo, 128).transpose(2, 1, 0, 3)
    return np.ascontiguousarray(t.reshape(ndo, 128, ndi * 128))


def prep_inputs(kw):
    X = np.asarray(kw["L0_attn_x"], dtype=np.float32)
    keys = np.asarray(kw["L0_ffn_keys"], dtype=np.float32)
    common = {
        "l0_wq": _vtile(np.asarray(kw["L0_attn_Wq"]).T),
        "l0_wk": _vtile(np.asarray(kw["L0_attn_Wk"]).T),
        "l0_wv": _vtile(np.asarray(kw["L0_attn_Wv"]).T),
        "l0_wo": np.ascontiguousarray(np.asarray(kw["L0_attn_Wo"]).T),
        "l0_bq": np.ascontiguousarray(np.asarray(kw["L0_attn_bq"],
                                                 dtype=np.float32)
                                      .reshape(16, 128).T),
        "l0_bk": np.ascontiguousarray(np.asarray(kw["L0_attn_bk"],
                                                 dtype=np.float32)
                                      .reshape(16, 128).T),
        "l0_bv": np.ascontiguousarray(np.asarray(kw["L0_attn_bv"],
                                                 dtype=np.float32)
                                      .reshape(16, 128).T),
        "l0_bo": np.ascontiguousarray(np.asarray(kw["L0_attn_bo"],
                                                 dtype=np.float32)
                                      .reshape(1, D)),
        "f0_wq": _vtile(np.asarray(kw["L0_ffn_Wq"])),
        "f0_k1t": np.ascontiguousarray(keys[:, :, 0, :].transpose(0, 2, 1)),
        "f0_k2t": np.ascontiguousarray(keys[:, :, 1, :].transpose(0, 2, 1)),
        "f0_down": np.ascontiguousarray(np.asarray(kw["L0_ffn_e_down"],
                                                   dtype=np.float32)),
        "f0_up": np.ascontiguousarray(np.asarray(kw["L0_ffn_e_up"],
                                                 dtype=np.float32)),
        "l1_wqkv": _vtile(np.asarray(kw["L1_attn_w_qkv"])),
        "l1_wo": np.ascontiguousarray(np.asarray(kw["L1_attn_w_o"],
                                                 dtype=np.float32)),
        "f1_w1t": _vtile(np.asarray(kw["L1_ffn_w1"]).T),
        "f1_w2t": _vtile(np.asarray(kw["L1_ffn_w2"]).T),
        "f1_w3t": np.ascontiguousarray(np.asarray(kw["L1_ffn_w3"],
                                                  dtype=np.float32).T),
        "c_ident": np.eye(128, dtype=np.float32),
        "c_iota": np.broadcast_to(np.arange(64, dtype=np.float32),
                                  (128, 64)).copy(),
    }
    in_maps = []
    for c in range(8):
        b, half = c // 2, c % 2
        m = dict(common)
        m["x_in"] = np.ascontiguousarray(X[b, half * TLOC:(half + 1) * TLOC, :])
        m["cs_rows"] = np.arange(NH, dtype=np.int32).reshape(NH, 1) * 2 + half
        in_maps.append(m)
    return in_maps


def kernel(**kw):
    in_maps = prep_inputs(kw)
    nc = build_nc(phases=4)
    res = run_bass_kernel_spmd(nc, in_maps, core_ids=list(range(8)))
    out = np.empty((4, 2048, D), dtype=np.float32)
    for c in range(8):
        b, half = c // 2, c % 2
        out[b, half * TLOC:(half + 1) * TLOC, :] = res.results[c]["out"]
    return out


# version 08
# speedup vs baseline: 1.0330x; 1.0330x over previous
import math
from contextlib import ExitStack

import numpy as np

import concourse.bass as bass
import concourse.mybir as mybir
from concourse.bass_utils import run_bass_kernel_spmd

F32 = mybir.dt.float32
I32 = mybir.dt.int32
U32 = mybir.dt.uint32
ALU = mybir.AluOpType
ACT = mybir.ActivationFunctionType
AX = mybir.AxisListType

D = 2048
TLOC = 1024
NT = TLOC // 128
NH = 16
DH = 128
EPS_LN = 1e-5
FH = 5632
NCH = FH // 128
RSC = 1.0 / math.sqrt(DH)

ENGS = ("sync", "tensor", "vector", "scalar", "gpsimd")


class Sem:
    def __init__(self, handle, name):
        self.h = handle
        self.name = name
        self.n = 0

    def bump(self, k=1):
        self.n += k
        return self.n


class KB:
    def __init__(self, nc, sems):
        self.nc = nc
        self.q = {e: [] for e in ENGS}
        self.s = sems
        self.pe_pending = []

    def wait(self, eng, sem, val):
        if val <= 0:
            return
        self.q[eng].append(lambda e, s=sem.h, v=val: e.wait_ge(s, v))

    def op(self, eng, fn, sem=None, k=1):
        if sem is None:
            self.q[eng].append(fn)
            return None
        n = sem.bump(k)
        self.q[eng].append(lambda e, f=fn, s=sem.h, kk=k: f(e).then_inc(s, kk))
        if sem is self.s["pe"]:
            for cb in self.pe_pending:
                cb(n)
            self.pe_pending = []
        return n

    def V(self, fn, waits=()):
        dv = self.s["dv"]
        for (sem, val) in waits:
            self.wait("vector", sem, val)
        self.wait("vector", dv, dv.n)
        return self.op("vector", fn, dv)

    def C(self, fn, waits=()):
        ac = self.s["ac"]
        for (sem, val) in waits:
            self.wait("scalar", sem, val)
        self.wait("scalar", ac, ac.n)
        return self.op("scalar", fn, ac)

    def G(self, fn, waits=()):
        gp = self.s["gp"]
        for (sem, val) in waits:
            self.wait("gpsimd", sem, val)
        self.wait("gpsimd", gp, gp.n)
        return self.op("gpsimd", fn, gp)

    def T(self, fn, waits=(), inc=False):
        for (sem, val) in waits:
            self.wait("tensor", sem, val)
        return self.op("tensor", fn, self.s["pe"] if inc else None)

    def barrier(self):
        for ch in getattr(self, "channels", []):
            if ch.open:
                ch.close()
        marks = {nm: self.s[nm].n for nm in self.s}
        for eng in ENGS:
            for nm, v in marks.items():
                self.wait(eng, self.s[nm], v)


class LoadChan:

    def __init__(self, kb, sems, eng="sync"):
        self.kb = kb
        self.sems = sems
        self.eng = eng
        self.cur = 0
        self.closed_total = [0] * len(sems)
        self.open = False

    def load(self, out_ap, in_ap):
        kb = self.kb
        s = self.sems[self.cur]
        if not self.open:
            kb.wait(self.eng, s, self.closed_total[self.cur])
            self.open = True
        kb.op(self.eng, lambda e, o=out_ap, i=in_ap: e.dma_start(out=o, in_=i),
              s, 16)
        return s, s.n

    def close(self):
        s = self.sems[self.cur]
        tot = s.n
        self.closed_total[self.cur] = tot
        self.cur = (self.cur + 1) % len(self.sems)
        self.open = False
        return s, tot

    def load1(self, out_ap, in_ap):
        assert not self.open
        sem, n = self.load(out_ap, in_ap)
        self.close()
        return sem, n

    def totals(self):
        assert not self.open
        return [(sm, sm.n) for sm in self.sems]


def build_nc(phases=4):
    nc = bass.Bass("TRN2", target_bir_lowering=False)

    x_in = nc.dram_tensor("x_in", [TLOC, D], F32, kind="ExternalInput").ap()
    out_d = nc.dram_tensor("out", [TLOC, D], F32, kind="ExternalOutput").ap()

    l0_wq = nc.dram_tensor("l0_wq", [NH, 128, D], F32, kind="ExternalInput").ap()
    l0_wk = nc.dram_tensor("l0_wk", [NH, 128, D], F32, kind="ExternalInput").ap()
    l0_wv = nc.dram_tensor("l0_wv", [NH, 128, D], F32, kind="ExternalInput").ap()
    l0_wo = nc.dram_tensor("l0_wo", [D, D], F32, kind="ExternalInput").ap()
    l0_bq = nc.dram_tensor("l0_bq", [128, 16], F32, kind="ExternalInput").ap()
    l0_bk = nc.dram_tensor("l0_bk", [128, 16], F32, kind="ExternalInput").ap()
    l0_bv = nc.dram_tensor("l0_bv", [128, 16], F32, kind="ExternalInput").ap()
    l0_bo = nc.dram_tensor("l0_bo", [1, D], F32, kind="ExternalInput").ap()

    f0_wq = nc.dram_tensor("f0_wq", [4, 128, D], F32, kind="ExternalInput").ap()
    f0_k1t = nc.dram_tensor("f0_k1t", [4, 64, 128], F32, kind="ExternalInput").ap()
    f0_k2t = nc.dram_tensor("f0_k2t", [4, 64, 128], F32, kind="ExternalInput").ap()
    f0_down = nc.dram_tensor("f0_down", [16384, D], F32, kind="ExternalInput").ap()
    f0_up = nc.dram_tensor("f0_up", [16384, D], F32, kind="ExternalInput").ap()

    l1_wqkv = nc.dram_tensor("l1_wqkv", [48, 128, D], F32, kind="ExternalInput").ap()
    l1_wo = nc.dram_tensor("l1_wo", [D, D], F32, kind="ExternalInput").ap()

    f1_w1t = nc.dram_tensor("f1_w1t", [NCH, 128, D], F32, kind="ExternalInput").ap()
    f1_w2t = nc.dram_tensor("f1_w2t", [NCH, 128, D], F32, kind="ExternalInput").ap()
    f1_w3t = nc.dram_tensor("f1_w3t", [FH, D], F32, kind="ExternalInput").ap()

    c_ident = nc.dram_tensor("c_ident", [128, 128], F32, kind="ExternalInput").ap()
    c_iota = nc.dram_tensor("c_iota", [128, 64], F32, kind="ExternalInput").ap()
    cs_rows = nc.dram_tensor("cs_rows", [NH, 1], I32, kind="ExternalInput").ap()

    dr_kt = nc.dram_tensor("dr_kt", [D, TLOC], F32, kind="Internal").ap()
    dr_kt_ag = nc.dram_tensor("dr_kt_ag", [2 * D, TLOC], F32, kind="Internal").ap()
    dr_v = nc.dram_tensor("dr_v", [TLOC, D], F32, kind="Internal").ap()
    dr_v_ag = nc.dram_tensor("dr_v_ag", [2 * TLOC, D], F32, kind="Internal").ap()
    dr_qt = nc.dram_tensor("dr_qt", [D, TLOC], F32, kind="Internal").ap()
    dr_attn = nc.dram_tensor("dr_attn", [D, TLOC], F32, kind="Internal").ap()
    dr_xn = nc.dram_tensor("dr_xn", [TLOC, D], F32, kind="Internal").ap()
    dr_kt1 = nc.dram_tensor("dr_kt1", [D, TLOC], F32, kind="Internal").ap()
    dr_kt1_ag = nc.dram_tensor("dr_kt1_ag", [2 * D, TLOC], F32, kind="Internal").ap()
    dr_cs = nc.dram_tensor("dr_cs", [NH, 2048], F32, kind="Internal").ap()
    dr_cs_ar = nc.dram_tensor("dr_cs_ar", [NH, 2048], F32, kind="Internal").ap()
    dr_h = nc.dram_tensor("dr_h", [FH, TLOC], F32, kind="Internal").ap()
    dr_w3q = nc.dram_tensor("dr_w3q", [FH, D], F32, kind="Internal").ap()

    ctx = ExitStack()
    with ctx:
        def sbuf(name, shape, dt=F32):
            return ctx.enter_context(nc.sbuf_tensor(name, shape, dt))

        def psum(name, shape, dt=F32):
            return ctx.enter_context(nc.psum_tensor(name, shape, dt))

        RES = sbuf("RES", [128, NT * D])
        A = sbuf("A", [128, 16384])
        B = sbuf("B", [128, 8192])
        SM = sbuf("SM", [128, 1536])
        WR1 = sbuf("WR1", [128, 4096])
        WR2 = sbuf("WR2", [128, 2048])
        SBIAS = sbuf("SBIAS", [1, D])
        IDENT = sbuf("IDENT", [128, 128])
        IOTA = sbuf("IOTA", [128, 64])
        KEYS = sbuf("KEYS", [128, 1024])
        ONES1 = sbuf("ONES1", [1, 128])
        ONES128 = sbuf("ONES128", [128, 1])
        U1 = sbuf("U1", [128, 8], U32)
        U2 = sbuf("U2", [128, 8], U32)
        UP = sbuf("UPOS", [128, 8], U32)
        IDXI = sbuf("IDXI", [128, 32], I32)
        CSR = sbuf("CSR", [NH, 1], I32)

        PB = psum("PB", [128, 2048])
        PQ = psum("PQ", [128, 1024])
        PT1 = psum("PT1", [128, 512])
        PT2 = psum("PT2", [128, 512])

        sems = {}
        for nm in ["pe", "dv", "ac", "gp", "ld0", "ld1", "ld2", "ld3",
                   "st", "st2", "gd0", "gd1", "cc", "lw0", "lw1"]:
            sems[nm] = Sem(ctx.enter_context(nc.semaphore(nm)), nm)

        kb = KB(nc, sems)
        LD = LoadChan(kb, [sems["ld0"], sems["ld1"], sems["ld2"], sems["ld3"]])
        ST = LoadChan(kb, [sems["st"], sems["st2"]])
        LDW = LoadChan(kb, [sems["lw0"], sems["lw1"]], eng="scalar")
        kb.channels = [LD, ST, LDW]

        def STW(dst, src):
            sm_, tt_ = ST.load1(dst, src)
            kb.wait("vector", sm_, tt_)
            kb.wait("scalar", sm_, tt_)
            return sm_, tt_

        BQ = SM[:, 0:16]
        BK = SM[:, 16:32]
        BV = SM[:, 32:48]
        STM = 48

        def smc(i, w=1):
            return SM[:, STM + i:STM + i + w]

        SCO = SM[:, 80:112]
        SCOR = SM[:, 112:144]
        IDXF = SM[:, 144:176]
        HID = SM[:, 176:208]
        GG = SM[:, 208:240]
        S1 = SM[:, 240:248]
        S2 = SM[:, 248:256]
        FS = SM[:, 256:264]
        I1F = SM[:, 264:272]
        I2F = SM[:, 272:280]
        PF = SM[:, 280:288]
        I64 = SM[:, 288:352]
        S64 = SM[:, 352:416]
        EQB = SM[:, 416:928]
        SIM1 = SM[:, 928:1056]
        SIM2 = SM[:, 1056:1184]
        GAC1 = SM[:, 1184:1185]
        GAC2 = SM[:, 1185:1186]
        GAC3 = SM[:, 1186:1187]
        GPART = SM[:, 1187:1188]
        GTOT = SM[0:1, 1190:1191]
        CSML = SM[:, 1200:1216]

        V, C, G, T, W = kb.V, kb.C, kb.G, kb.T, kb.wait
        pe, dv, ac, gp, cc = (sems["pe"], sems["dv"], sems["ac"], sems["gp"],
                              sems["cc"])
        rg = [[0, 1], [2, 3], [4, 5], [6, 7]]

        for t in range(NT):
            LD.load(RES[:, t * D:(t + 1) * D], x_in[t * 128:(t + 1) * 128, :])
        LD.load(IDENT[:], c_ident)
        LD.load(IOTA[:], c_iota)
        LD.load(BQ, l0_bq)
        LD.load(BK, l0_bk)
        LD.load(BV, l0_bv)
        LD.load(SBIAS[:], l0_bo)
        LD.load(CSR[:], cs_rows)
        for h in range(4):
            LD.load(KEYS[0:64, h * 128:(h + 1) * 128], f0_k1t[h])
            LD.load(KEYS[64:128, h * 128:(h + 1) * 128], f0_k2t[h])
        s_pro, n_pro = LD.close()
        for eng in ("vector", "tensor", "scalar", "gpsimd"):
            kb.wait(eng, s_pro, n_pro)
        V(lambda e: e.memset(ONES1[:], 1.0))
        V(lambda e: e.memset(ONES128[:], 1.0))

        def layernorm_tile(t, xn_ap):
            rt = RES[:, t * D:(t + 1) * D]
            m = smc(0)
            vv = smc(1)
            inv = smc(2)
            t1 = smc(3)
            V(lambda e: e.reduce_sum(m, rt, AX.X))
            V(lambda e: e.tensor_scalar_mul(m, m, 1.0 / D))
            nc_ = V(lambda e: e.tensor_scalar(xn_ap, rt, m, None, ALU.subtract))
            nsq = C(lambda e: e.activation(B[:, 6144:8192], xn_ap, ACT.Square,
                                           accum_out=vv), waits=[(dv, nc_)])
            V(lambda e: e.tensor_scalar(vv, vv, 1.0 / D, EPS_LN,
                                        ALU.mult, ALU.add), waits=[(ac, nsq)])
            nv = sems["dv"].n
            na = C(lambda e: e.activation(inv, vv, ACT.Sqrt), waits=[(dv, nv)])
            V(lambda e: e.reciprocal(inv, inv), waits=[(ac, na)])
            V(lambda e: e.tensor_tensor(t1, inv, inv, ALU.mult))
            V(lambda e: e.tensor_tensor(t1, t1, vv, ALU.mult))
            V(lambda e: e.tensor_scalar(t1, t1, -0.5, 1.5, ALU.mult, ALU.add))
            V(lambda e: e.tensor_tensor(inv, inv, t1, ALU.mult))
            return V(lambda e: e.tensor_scalar_mul(xn_ap, xn_ap, inv))

        def ln_and_transpose(store_xn=None):
            tile_pe = {}
            bank_evac = {0: 0, 1: 0}
            for t in range(NT):
                xn = B[:, (t % 2) * D:((t % 2) + 1) * D]
                if (t - 2) in tile_pe:
                    kb.wait("vector", pe, tile_pe[t - 2])
                nln = layernorm_tile(t, xn)
                if store_xn is not None:
                    kb.wait("sync", dv, nln)
                    STW(store_xn[t * 128:(t + 1) * 128, :], xn)
                for c in range(16):
                    bk = c % 2
                    pb = PT1 if bk == 0 else PT2
                    waits = [(dv, nln)] if c < 2 else []
                    if bank_evac[bk]:
                        waits = waits + [(dv, bank_evac[bk])]
                    n = T(lambda e, s=xn[:, c * 128:(c + 1) * 128], p=pb:
                          e.transpose(p[:, 0:128], s, IDENT[:]),
                          waits=waits, inc=True)
                    dst = A[:, c * 1024 + t * 128: c * 1024 + (t + 1) * 128]
                    bank_evac[bk] = V(lambda e, p=pb, d=dst:
                                      e.tensor_copy(d, p[:, 0:128]),
                                      waits=[(pe, n)])
                tile_pe[t] = sems["pe"].n
            return sems["dv"].n

        class WRing:
            def __init__(self, area, slot_cols, nslots):
                self.area = area
                self.w = slot_cols
                self.n = nslots
                self.i = 0
                self.last_use = [0] * nslots

            def __init2__(self):
                pass

            def get(self, src_ap, rows=128, ch=None):
                ch = ch or LD
                sl = self.i % self.n
                self.i += 1
                ap = self.area[0:rows, sl * self.w:sl * self.w + self.w]
                kb.wait(ch.eng, pe, self.last_use[sl])
                sem, n = ch.load1(ap, src_ap)
                return ap, sl, (sem, n)

            def used(self, sl):
                def cb(v, sl=sl):
                    self.last_use[sl] = v
                kb.pe_pending.append(cb)

        wr1 = WRing(WR1, 2048, 2)
        wr2 = WRing(WR2, 512, 4)

        def proj_fm(w_dram, h, psum_ap):
            wt, sl, (sm, ln) = wr1.get(w_dram[h], ch=LDW)
            for half in range(2):
                for di in range(16):
                    rhs = A[:, di * 1024 + half * 512:
                            di * 1024 + half * 512 + 512]
                    T(lambda e, p=psum_ap, w=wt[:, di * 128:(di + 1) * 128],
                      r=rhs, st=(di == 0), sp=(di == 15), hh=half:
                      e.matmul(p[:, hh * 512:(hh + 1) * 512], w, r,
                               start=st, stop=sp),
                      waits=[(sm, ln)] if (di == 0 and half == 0) else [],
                      inc=(di == 15))
            wr1.used(sl)
            return sems["pe"].n

        def attn_scores(qt_sb, kt_sb, qt, waits0=()):
            nsc = None
            for kc in range(4):
                nsc = T(lambda e, q=qt_sb[:, qt * 128:(qt + 1) * 128],
                        r=kt_sb[:, kc * 512:(kc + 1) * 512], kc=kc:
                        e.matmul(PB[:, kc * 512:(kc + 1) * 512], q, r,
                                 start=True, stop=True),
                        waits=waits0 if kc == 0 else (), inc=(kc == 3))
            return nsc

        def attn_softmax(probs, nsc, par):
            nm = smc(4 + 3 * par)
            rs = smc(5 + 3 * par)
            ri = smc(6 + 3 * par)
            V(lambda e, nm=nm: e.reduce_max(nm, PB[:], AX.X, negate=True),
              waits=[(pe, nsc)])
            nvm = V(lambda e, nm=nm: e.tensor_scalar_mul(nm, nm, RSC))
            ne = C(lambda e, nm=nm, rs=rs, p=probs:
                   e.activation(p, PB[:], ACT.Exp, bias=nm,
                                scale=RSC, accum_out=rs),
                   waits=[(dv, nvm)])
            nri = V(lambda e, ri=ri, rs=rs: e.reciprocal(ri, rs),
                    waits=[(ac, ne)])
            nsm = C(lambda e, ri=ri, p=probs:
                    e.activation(p, p, ACT.Copy, scale=ri),
                    waits=[(dv, nri)])
            return ne, nsm

        gsteps = []
        if phases >= 4:
            gpar = [0]
            gmarkp = {0: 0, 1: 0}
            gfirst = {}

            def g_step(src, gac):
                def run():
                    p = gpar[0] % 2
                    gpar[0] += 1
                    wdt = B[:, p * 2048:(p + 1) * 2048]
                    kb.wait("sync", dv, gmarkp[p])
                    s_w, n_w = LD.load1(wdt, src)
                    V(lambda e: e.reduce_sum(GPART, wdt, AX.X,
                                             apply_absolute_value=True),
                      waits=[(s_w, n_w)])
                    if gac.offset not in gfirst:
                        gfirst[gac.offset] = True
                        nr = V(lambda e: e.tensor_copy(gac, GPART))
                    else:
                        nr = V(lambda e: e.tensor_tensor(gac, gac, GPART,
                                                         ALU.add))
                    gmarkp[p] = nr
                return run

            for c_ in range(NCH):
                gsteps.append(g_step(f1_w1t[c_], GAC1))
            for c_ in range(NCH):
                gsteps.append(g_step(f1_w2t[c_], GAC2))
            for c_ in range(NCH):
                gsteps.append(g_step(f1_w3t[c_ * 128:(c_ + 1) * 128, :], GAC3))

        nxt = ln_and_transpose()
        kb.wait("tensor", dv, nxt)
        if phases >= 4:
            gmarkp[0] = nxt
            gmarkp[1] = nxt

        for h in range(NH):
            for _ in range(9):
                if gsteps:
                    gsteps.pop(0)()
            nq = proj_fm(l0_wq, h, PQ)
            qsb = B[:, 4096:5120]
            nv = V(lambda e, b=BQ[:, h:h + 1], d=qsb:
                   e.tensor_scalar_add(d, PQ[:], b), waits=[(pe, nq)])
            kb.wait("sync", dv, nv)
            STW(dr_qt[h * 128:(h + 1) * 128, :], qsb)
            kb.wait("tensor", dv, nv)
            nk = proj_fm(l0_wk, h, PQ)
            ksb = B[:, 5120:6144]
            nv = V(lambda e, b=BK[:, h:h + 1], d=ksb:
                   e.tensor_scalar_add(d, PQ[:], b), waits=[(pe, nk)])
            kb.wait("sync", dv, nv)
            STW(dr_kt[h * 128:(h + 1) * 128, :], ksb)
            kb.wait("tensor", dv, nv)
            nvm = proj_fm(l0_wv, h, PQ)
            vsb = B[:, 4096:5120]
            nv = V(lambda e, b=BV[:, h:h + 1], d=vsb:
                   e.tensor_scalar_add(d, PQ[:], b), waits=[(pe, nvm)])
            stage = B[:, 6144:6272]
            prev_ev = 0
            for t in range(NT):
                pb = PT1 if t % 2 == 0 else PT2
                waits = [(dv, nv)]
                if prev_ev:
                    waits.append((dv, prev_ev))
                n = T(lambda e, s=vsb[:, t * 128:(t + 1) * 128], p=pb:
                      e.transpose(p[:, 0:128], s, IDENT[:]),
                      waits=waits, inc=True)
                nve = V(lambda e, p=pb, d=stage: e.tensor_copy(d, p[:, 0:128]),
                        waits=[(pe, n)])
                kb.wait("sync", dv, nve)
                s_s, n_s = ST.load1(
                    dr_v[t * 128:(t + 1) * 128, h * 128:(h + 1) * 128], stage)
                kb.wait("vector", s_s, n_s)
                prev_ev = nve
            kb.wait("tensor", dv, nv)
        while gsteps:
            gsteps.pop(0)()
        n_gfin = 0
        if phases >= 4:
            T1P = smc(10)
            T1N = smc(12)
            T2P = smc(14)
            T2N = smc(16)
            T3P = smc(18)
            T3N = smc(20)

            def g_fin(gac, tp_ap, tn_ap):
                ng = sems["dv"].n
                ntt = T(lambda e, g=gac: e.matmul(PT2[0:1, 0:1], g, ONES128[:],
                                                  start=True, stop=True),
                        waits=[(dv, ng)], inc=True)
                V(lambda e: e.tensor_copy(GTOT, PT2[0:1, 0:1]),
                  waits=[(pe, ntt)])
                V(lambda e: e.tensor_scalar(GTOT, GTOT, 0.5 / (FH * D),
                                            0.5e-8, ALU.mult, ALU.add))
                ngt = sems["dv"].n
                nrep = T(lambda e: e.matmul(PT2[:, 0:1], ONES1[:], GTOT,
                                            start=True, stop=True),
                         waits=[(dv, ngt)], inc=True)
                V(lambda e, d=tp_ap: e.tensor_copy(d, PT2[:, 0:1]),
                  waits=[(pe, nrep)])
                return V(lambda e, d=tn_ap, sr=tp_ap:
                         e.tensor_scalar_mul(d, sr, -1.0))

            g_fin(GAC1, T1P, T1N)
            g_fin(GAC2, T2P, T2N)
            n_gfin = g_fin(GAC3, T3P, T3N)

        for (sm_, tt_) in ST.totals():
            kb.wait("gpsimd", sm_, tt_)
        n_cc_k = kb.op("gpsimd", lambda e: e.collective_compute(
            "AllGather", ALU.bypass, replica_groups=rg,
            ins=[dr_kt], outs=[dr_kt_ag]), cc)
        n_cc_v = kb.op("gpsimd", lambda e: e.collective_compute(
            "AllGather", ALU.bypass, replica_groups=rg,
            ins=[dr_v], outs=[dr_v_ag]), cc)

        kb.wait("sync", cc, n_cc_v)
        for h in range(NH):
            kt_sb = A[:, 0:2048]
            v_sb = A[:, 2048:4096]
            qt_sb = A[:, 4096:5120]
            probs = A[:, 5120:7168]
            probs2 = A[:, 9216:11264]
            pt_sb = A[:, 7168:9216]
            att_h = B[:, 0:1024]
            kb.wait("sync", pe, sems["pe"].n)
            kb.wait("sync", dv, sems["dv"].n)
            kb.wait("sync", ac, sems["ac"].n)
            LD.load(kt_sb[:, 0:1024], dr_kt_ag[h * 128:(h + 1) * 128, :])
            LD.load(kt_sb[:, 1024:2048],
                    dr_kt_ag[D + h * 128:D + (h + 1) * 128, :])
            LD.load(v_sb.rearrange("p (kt d) -> p kt d", d=128),
                    dr_v_ag[:, h * 128:(h + 1) * 128]
                    .rearrange("(kt p) d -> p kt d", p=128))
            LD.load(qt_sb[:], dr_qt[h * 128:(h + 1) * 128, :])
            s_h, n_h = LD.close()
            for eng in ("tensor", "vector", "scalar"):
                kb.wait(eng, s_h, n_h)
            nexp = {}
            nsm = {}
            for qt in range(NT + 1):
                if qt < NT:
                    par = qt % 2
                    pr = probs if par == 0 else probs2
                    w0 = [(ac, nexp[qt - 1])] if qt >= 1 else []
                    nsc = attn_scores(qt_sb, kt_sb, qt, waits0=w0)
                    nexp[qt], nsm[qt] = attn_softmax(pr, nsc, par)
                if qt == 0:
                    continue
                qp = qt - 1
                prp = probs if qp % 2 == 0 else probs2
                lastpt = []
                bank_evac = {0: 0, 1: 0}
                for kc16 in range(16):
                    bk = kc16 % 2
                    pb = PT1 if bk == 0 else PT2
                    waits = [(ac, nsm[qp])] if kc16 < 2 else []
                    if bank_evac[bk]:
                        waits = waits + [(dv, bank_evac[bk])]
                    n = T(lambda e, sp=prp[:, kc16 * 128:(kc16 + 1) * 128],
                          p=pb: e.transpose(p[:, 0:128], sp, IDENT[:]),
                          waits=waits, inc=True)
                    nve = V(lambda e, p=pb,
                            d=pt_sb[:, kc16 * 128:(kc16 + 1) * 128]:
                            e.tensor_copy(d, p[:, 0:128]), waits=[(pe, n)])
                    bank_evac[bk] = nve
                    lastpt.append(nve)
                nav = None
                for kc16 in range(16):
                    nav = T(lambda e, v=v_sb[:, kc16 * 128:(kc16 + 1) * 128],
                            r=pt_sb[:, kc16 * 128:(kc16 + 1) * 128],
                            st=(kc16 == 0), sp=(kc16 == 15):
                            e.matmul(PQ[:, 0:128], v, r, start=st, stop=sp),
                            waits=[(dv, lastpt[15])] if kc16 == 0 else [],
                            inc=(kc16 == 15))
                nve = V(lambda e, d=att_h[:, qp * 128:(qp + 1) * 128]:
                        e.tensor_copy(d, PQ[:, 0:128]), waits=[(pe, nav)])
                kb.wait("tensor", dv, nve)
            kb.wait("sync", dv, sems["dv"].n)
            STW(dr_attn[h * 128:(h + 1) * 128, :], att_h)
        for (sm_, tt_) in ST.totals():
            kb.wait("sync", sm_, tt_)
        atr = WRing(B, 128, 16)
        for t in range(NT):
            for blk in range(4):
                T(lambda e, blk=blk:
                  e.matmul(PB[:, blk * 512:(blk + 1) * 512], ONES1[:],
                           SBIAS[:, blk * 512:(blk + 1) * 512],
                           start=True, stop=False))
            for hc in range(16):
                at, sl, (sm, ln) = atr.get(
                    dr_attn[hc * 128:(hc + 1) * 128, t * 128:(t + 1) * 128])
                for blk in range(4):
                    wt, sl2, (sm2, ln2) = wr2.get(
                        l0_wo[hc * 128:(hc + 1) * 128,
                              blk * 512:(blk + 1) * 512])
                    T(lambda e, a=at, w=wt, blk=blk, sp=(hc == 15):
                      e.matmul(PB[:, blk * 512:(blk + 1) * 512], a, w,
                               start=False, stop=sp),
                      waits=[(sm, ln), (sm2, ln2)],
                      inc=(blk == 3))
                    wr2.used(sl2)
                atr.used(sl)
            npj = sems["pe"].n
            nve = V(lambda e, t=t: e.tensor_tensor(
                RES[:, t * D:(t + 1) * D], PB[:],
                RES[:, t * D:(t + 1) * D], ALU.add), waits=[(pe, npj)])
            kb.wait("tensor", dv, nve)

        if phases >= 2:
            kb.barrier()
            nxt = ln_and_transpose(store_xn=dr_xn)
            kb.wait("tensor", dv, nxt)
            for hq in range(4):
                nq = proj_fm(f0_wq, hq, PQ)
                nv = V(lambda e, d=B[:, hq * 1024:(hq + 1) * 1024]:
                       e.tensor_copy(d, PQ[:]), waits=[(pe, nq)])
                kb.wait("tensor", dv, nv)
            sim_ev = {0: 0, 1: 0}
            for t in range(NT):
                xn2 = B[:, 4096:6144]
                outacc = B[:, 6144:8192]
                kb.wait("sync", dv, sems["dv"].n)
                kb.wait("sync", gp, sems["gp"].n)
                LD.load(xn2, dr_xn[t * 128:(t + 1) * 128, :])
                s_t, n_t = LD.close()
                kb.wait("vector", s_t, n_t)
                for h in range(4):
                    q1 = B[0:64, h * 1024 + t * 128:h * 1024 + (t + 1) * 128]
                    q2 = B[64:128, h * 1024 + t * 128:h * 1024 + (t + 1) * 128]
                    w1 = [(dv, sim_ev[0])] if sim_ev[0] else []
                    n1 = T(lambda e, q=q1,
                           k=KEYS[0:64, h * 128:(h + 1) * 128]:
                           e.matmul(PT1[:, 0:128], q, k, start=True,
                                    stop=True), waits=w1, inc=True)
                    nv1 = V(lambda e: e.tensor_copy(SIM1, PT1[:, 0:128]),
                            waits=[(pe, n1)])
                    sim_ev[0] = nv1
                    w2 = [(dv, sim_ev[1])] if sim_ev[1] else []
                    n2 = T(lambda e, q=q2,
                           k=KEYS[64:128, h * 128:(h + 1) * 128]:
                           e.matmul(PT2[:, 0:128], q, k, start=True,
                                    stop=True), waits=w2, inc=True)
                    nv2 = V(lambda e: e.tensor_copy(SIM2, PT2[:, 0:128]),
                            waits=[(pe, n2)])
                    sim_ev[1] = nv2
                    V(lambda e: e.max(out=S1, in_=SIM1))
                    V(lambda e: e.max_index(out=U1[:], in_max=S1, in_values=SIM1))
                    V(lambda e: e.max(out=S2, in_=SIM2))
                    V(lambda e: e.max_index(out=U2[:], in_max=S2, in_values=SIM2))
                    V(lambda e: e.tensor_copy(I1F, U1[:]))
                    V(lambda e: e.tensor_copy(I2F, U2[:]))
                    V(lambda e: e.tensor_tensor(
                        S64.rearrange("p (i j) -> p i j", j=8),
                        S1.unsqueeze(2).to_broadcast([128, 8, 8]),
                        S2.unsqueeze(1).to_broadcast([128, 8, 8]), ALU.add))
                    V(lambda e: e.tensor_scalar_mul(I1F, I1F, 128.0))
                    V(lambda e: e.tensor_tensor(
                        I64.rearrange("p (i j) -> p i j", j=8),
                        I1F.unsqueeze(2).to_broadcast([128, 8, 8]),
                        I2F.unsqueeze(1).to_broadcast([128, 8, 8]), ALU.add))
                    V(lambda e: e.max(out=FS, in_=S64))
                    V(lambda e: e.max_index(out=UP[:], in_max=FS, in_values=S64))
                    V(lambda e: e.tensor_copy(PF, UP[:]))
                    V(lambda e, h=h: e.tensor_copy(SCO[:, h * 8:(h + 1) * 8],
                                                   FS))
                    V(lambda e: e.tensor_tensor(
                        EQB.rearrange("p (s j) -> p s j", j=64),
                        PF.unsqueeze(2).to_broadcast([128, 8, 64]),
                        IOTA[:].unsqueeze(1).to_broadcast([128, 8, 64]),
                        ALU.is_equal))
                    V(lambda e: e.tensor_tensor(
                        EQB.rearrange("p (s j) -> p s j", j=64),
                        EQB.rearrange("p (s j) -> p s j", j=64),
                        I64.unsqueeze(1).to_broadcast([128, 8, 64]), ALU.mult))
                    V(lambda e, h=h: e.tensor_reduce(
                        IDXF[:, h * 8:(h + 1) * 8],
                        EQB.rearrange("p (s j) -> p s j", j=64),
                        AX.X, ALU.add))
                kb.wait("vector", sems["gd0"], sems["gd0"].n)
                kb.wait("vector", sems["gd1"], sems["gd1"].n)
                nidx = V(lambda e: e.tensor_copy(IDXI[:], IDXF))
                nrel = V(lambda e: e.tensor_scalar_max(SCOR, SCO, 0.0))
                gmark = {}
                for g in range(16):
                    par = g % 2
                    gdsem = sems["gd0"] if par == 0 else sems["gd1"]
                    kb.wait("gpsimd", dv, nidx)
                    if g >= 2:
                        kb.wait("gpsimd", dv, gmark[g - 2])
                    dslot = [A[:, par * 4096 + j2 * 2048:
                               par * 4096 + (j2 + 1) * 2048] for j2 in range(2)]
                    uslot = [A[:, 8192 + par * 4096 + j2 * 2048:
                               8192 + par * 4096 + (j2 + 1) * 2048]
                             for j2 in range(2)]
                    for j2 in range(2):
                        j = g * 2 + j2
                        kb.op("gpsimd",
                              lambda e, j=j, d=dslot[j2]: e.indirect_dma_start(
                                  out=d, out_offset=None, in_=f0_down,
                                  in_offset=bass.IndirectOffsetOnAxis(
                                      IDXI[:, j:j + 1], 0)), gdsem, 16)
                        kb.op("gpsimd",
                              lambda e, j=j, u=uslot[j2]: e.indirect_dma_start(
                                  out=u, out_offset=None, in_=f0_up,
                                  in_offset=bass.IndirectOffsetOnAxis(
                                      IDXI[:, j:j + 1], 0)), gdsem, 16)
                    ngd = gdsem.n
                    nh4 = None
                    for j2 in range(2):
                        j = g * 2 + j2
                        nmul = kb.G(lambda e, d=dslot[j2]: e.tensor_tensor(
                            d, d, xn2, ALU.mult),
                            waits=[(gdsem, ngd), (s_t, n_t)])
                        nh4 = V(lambda e, d=dslot[j2], j=j: e.tensor_reduce(
                            HID[:, j:j + 1], d, AX.X, ALU.add),
                            waits=[(gp, nmul)])
                    nsl = C(lambda e, g=g: e.activation(
                        GG[:, g * 2:(g + 1) * 2],
                        HID[:, g * 2:(g + 1) * 2], ACT.Silu),
                        waits=[(dv, nh4)])
                    ngg = V(lambda e, g=g: e.tensor_tensor(
                        GG[:, g * 2:(g + 1) * 2], GG[:, g * 2:(g + 1) * 2],
                        SCOR[:, g * 2:(g + 1) * 2], ALU.mult),
                        waits=[(ac, nsl), (dv, nrel)])
                    for j2 in range(2):
                        j = g * 2 + j2
                        up = uslot[j2]
                        gcol = GG[:, j:j + 1]
                        if j == 0:
                            V(lambda e, u=up, gc=gcol: e.tensor_scalar(
                                outacc, u, gc, None, ALU.mult))
                        else:
                            V(lambda e, u=up, gc=gcol: e.scalar_tensor_tensor(
                                outacc, u, gc, outacc, ALU.mult, ALU.add))
                    gmark[g] = sems["dv"].n
                V(lambda e, t=t: e.tensor_tensor(
                    RES[:, t * D:(t + 1) * D], outacc,
                    RES[:, t * D:(t + 1) * D], ALU.add))

        if phases >= 3:
            kb.barrier()
            nxt = ln_and_transpose()
            kb.wait("tensor", dv, nxt)
            for h in range(NH):
                nq = proj_fm(l1_wqkv, h, PQ)
                qsb = B[:, 0:1024]
                nv = V(lambda e, d=qsb: e.tensor_copy(d, PQ[:]),
                       waits=[(pe, nq)])
                kb.wait("sync", dv, nv)
                STW(dr_qt[h * 128:(h + 1) * 128, :], qsb)
                kb.wait("tensor", dv, nv)
                nk = proj_fm(l1_wqkv, NH + h, PQ)
                ksb = B[:, 1024:2048]
                nv = V(lambda e, d=ksb: e.tensor_copy(d, PQ[:]),
                       waits=[(pe, nk)])
                kb.wait("sync", dv, nv)
                STW(dr_kt1[h * 128:(h + 1) * 128, :], ksb)
                kb.wait("tensor", dv, nv)
            for t in range(NT):
                for dc in range(16):
                    wt, sl, (sm, ln) = wr1.get(l1_wqkv[32 + dc], ch=LDW)
                    for di in range(16):
                        T(lambda e, w=wt[:, di * 128:(di + 1) * 128], dc=dc,
                          x=A[:, di * 1024 + t * 128:
                              di * 1024 + (t + 1) * 128],
                          st=(di == 0), sp=(di == 15):
                          e.matmul(PB[:, dc * 128:(dc + 1) * 128], x, w,
                                   start=st, stop=sp),
                          waits=[(sm, ln)] if di == 0 else [],
                          inc=(di == 15))
                    wr1.used(sl)
                nvt = sems["pe"].n
                vsb = B[:, 2048:4096]
                nv = V(lambda e, d=vsb: e.tensor_copy(d, PB[:]),
                       waits=[(pe, nvt)])
                kb.wait("sync", dv, nv)
                STW(dr_v[t * 128:(t + 1) * 128, :], vsb)
                kb.wait("tensor", dv, nv)
            for (sm_, tt_) in ST.totals():
                kb.wait("gpsimd", sm_, tt_)
            n_cc_k1 = kb.op("gpsimd", lambda e: e.collective_compute(
                "AllGather", ALU.bypass, replica_groups=rg,
                ins=[dr_kt1], outs=[dr_kt1_ag]), cc)
            kb.wait("sync", cc, n_cc_k1)
            rows_acc = B[:, 0:2048]
            for h in range(NH):
                kt_sb = A[:, 0:2048]
                qt_sb = A[:, 4096:5120]
                probs = A[:, 5120:7168]
                probs2 = A[:, 9216:11264]
                kb.wait("sync", pe, sems["pe"].n)
                kb.wait("sync", dv, sems["dv"].n)
                kb.wait("sync", ac, sems["ac"].n)
                LD.load(kt_sb[:, 0:1024], dr_kt1_ag[h * 128:(h + 1) * 128, :])
                LD.load(kt_sb[:, 1024:2048],
                        dr_kt1_ag[D + h * 128:D + (h + 1) * 128, :])
                LD.load(qt_sb[:], dr_qt[h * 128:(h + 1) * 128, :])
                s_h, n_h = LD.close()
                for eng in ("tensor", "vector", "scalar"):
                    kb.wait(eng, s_h, n_h)
                nacc = 0
                nexp = {}
                nsm = {}
                for qt in range(NT + 1):
                    if qt < NT:
                        par = qt % 2
                        pr = probs if par == 0 else probs2
                        w0 = [(ac, nexp[qt - 1])] if qt >= 1 else []
                        nsc = attn_scores(qt_sb, kt_sb, qt, waits0=w0)
                        nexp[qt], nsm[qt] = attn_softmax(pr, nsc, par)
                    if qt == 0:
                        continue
                    qp = qt - 1
                    prp = probs if qp % 2 == 0 else probs2
                    if qp == 0:
                        nacc = V(lambda e, p=prp: e.tensor_copy(rows_acc, p),
                                 waits=[(ac, nsm[qp])])
                    else:
                        nacc = V(lambda e, p=prp: e.tensor_tensor(
                            rows_acc, rows_acc, p, ALU.add),
                            waits=[(ac, nsm[qp])])
                for kc in range(4):
                    ncs = T(lambda e, kc=kc:
                            e.matmul(PT1[0:1, 0:512], ONES128[:],
                                     rows_acc[:, kc * 512:(kc + 1) * 512],
                                     start=True, stop=True),
                            waits=[(dv, nacc)] if kc == 0 else [], inc=True)
                    nve = V(lambda e, kc=kc: e.tensor_copy(
                        SBIAS[0:1, kc * 512:(kc + 1) * 512],
                        PT1[0:1, 0:512]), waits=[(pe, ncs)])
                    kb.wait("tensor", dv, nve)
                kb.wait("sync", dv, sems["dv"].n)
                STW(dr_cs[h:h + 1, :], SBIAS[0:1, :])
            for (sm_, tt_) in ST.totals():
                kb.wait("gpsimd", sm_, tt_)
            n_cc_cs = kb.op("gpsimd", lambda e: e.collective_compute(
                "AllReduce", ALU.add, replica_groups=rg,
                ins=[dr_cs], outs=[dr_cs_ar]), cc)
            csl = B[:, 4096:5120]
            kb.wait("gpsimd", cc, n_cc_cs)
            kb.wait("gpsimd", sems["gd0"], sems["gd0"].n)
            ngsel_sem = sems["gd0"]
            ngsel = kb.op("gpsimd", lambda e: e.indirect_dma_start(
                out=csl[0:16, :], out_offset=None,
                in_=dr_cs_ar.rearrange("h (s t) -> (h s) t", s=2),
                in_offset=bass.IndirectOffsetOnAxis(CSR[:], 0)),
                ngsel_sem, 16)
            for t in range(NT):
                v_sb = B[:, 0:2048]
                kb.wait("sync", pe, sems["pe"].n)
                kb.wait("sync", dv, sems["dv"].n)
                LD.load(v_sb, dr_v[t * 128:(t + 1) * 128, :])
                s_v, n_v = LD.close()
                nct = T(lambda e, s=csl[0:16, t * 128:(t + 1) * 128]:
                        e.transpose(PT1[:, 0:16], s, IDENT[0:16, 0:16]),
                        waits=[(ngsel_sem, ngsel)], inc=True)
                nve = V(lambda e: e.tensor_copy(CSML, PT1[:, 0:16]),
                        waits=[(pe, nct)])
                nsv = V(lambda e: e.tensor_tensor(
                    v_sb.rearrange("p (h d) -> p h d", d=128),
                    v_sb.rearrange("p (h d) -> p h d", d=128),
                    CSML.unsqueeze(2).to_broadcast([128, 16, 128]),
                    ALU.mult), waits=[(s_v, n_v)])
                bank_evac = {0: 0, 1: 0}
                for hc in range(16):
                    bk = hc % 2
                    pb = PT1 if bk == 0 else PT2
                    waits = [(dv, nsv)] if hc < 2 else []
                    if bank_evac[bk]:
                        waits = waits + [(dv, bank_evac[bk])]
                    n = T(lambda e, s=v_sb[:, hc * 128:(hc + 1) * 128], p=pb:
                          e.transpose(p[:, 0:128], s, IDENT[:]),
                          waits=waits, inc=True)
                    bank_evac[bk] = V(
                        lambda e, p=pb, d=A[:, hc * 128:(hc + 1) * 128]:
                        e.tensor_copy(d, p[:, 0:128]), waits=[(pe, n)])
                nvals = sems["dv"].n
                kb.wait("tensor", dv, nvals)
                for hc in range(16):
                    for blk in range(4):
                        wt, sl2, (sm2, ln2) = wr2.get(
                            l1_wo[hc * 128:(hc + 1) * 128,
                                  blk * 512:(blk + 1) * 512])
                        T(lambda e, a=A[:, hc * 128:(hc + 1) * 128], w=wt,
                          blk=blk, st=(hc == 0), sp=(hc == 15):
                          e.matmul(PB[:, blk * 512:(blk + 1) * 512], a, w,
                                   start=st, stop=sp),
                          waits=[(sm2, ln2)], inc=(blk == 3))
                        wr2.used(sl2)
                npj = sems["pe"].n
                nve = V(lambda e, t=t: e.tensor_tensor(
                    RES[:, t * D:(t + 1) * D], PB[:],
                    RES[:, t * D:(t + 1) * D], ALU.add), waits=[(pe, npj)])
                kb.wait("tensor", dv, nve)

        if phases >= 4:
            kb.barrier()
            nxt = ln_and_transpose()
            kb.wait("tensor", dv, nxt)

            def quantize(dst_ap, src_ap, tp, tn, tmp_ap, wl):
                V(lambda e: e.tensor_scalar(dst_ap, src_ap, tp, None,
                                            ALU.is_gt), waits=wl)
                V(lambda e: e.tensor_scalar(tmp_ap, src_ap, tn, None,
                                            ALU.is_lt))
                return V(lambda e: e.tensor_tensor(dst_ap, dst_ap, tmp_ap,
                                                   ALU.subtract))

            QWa = B[:, 4096:6144]
            QWb = B[:, 6144:8192]
            QTMP = B[:, 0:2048]
            qw_mark = {0: 0, 1: 0}
            for c in range(NCH):
                for wi, (wsrc, psm, tp, tn) in enumerate(
                        ((f1_w1t, PQ, T1P, T1N), (f1_w2t, PB, T2P, T2N))):
                    wt, sl, (sm, ln) = wr1.get(wsrc[c], ch=LDW)
                    par = wi
                    qw = QWa if par == 0 else QWb
                    kb.wait("vector", pe, qw_mark[par])
                    nq = quantize(qw, wt, tp, tn, QTMP,
                                  [(sm, ln), (dv, n_gfin)])
                    for di in range(16):
                        for half in range(2):
                            rhs = A[:, di * 1024 + half * 512:
                                    di * 1024 + half * 512 + 512]
                            T(lambda e, p=psm, di=di,
                              w=qw[:, di * 128:(di + 1) * 128], r=rhs,
                              st=(di == 0), sp=(di == 15), hh=half:
                              e.matmul(p[:, hh * 512:(hh + 1) * 512], w, r,
                                       start=st, stop=sp),
                              waits=[(dv, nq)] if di == 0 else [],
                              inc=(half == 1))
                    def _cb(v, par=par):
                        qw_mark[par] = v
                    kb.pe_pending.append(_cb)
                    wr1.used(sl)
                w3raw = QTMP
                w3tmp = B[:, 2048:3072 + 1024][:, 0:1024]
                kb.wait("sync", dv, sems["dv"].n)
                sw3, lw3 = LD.load1(w3raw, f1_w3t[c * 128:(c + 1) * 128, :])
                V(lambda e, w=w3raw: e.tensor_scalar(
                    B[:, 2048:4096], w, T3P, None, ALU.is_gt),
                  waits=[(sw3, lw3), (dv, n_gfin)])
                V(lambda e, w=w3raw: e.tensor_scalar(w, w, T3N, None,
                                                     ALU.is_lt))
                nw3q = V(lambda e, w=w3raw: e.tensor_tensor(
                    w, B[:, 2048:4096], w, ALU.subtract))
                kb.wait("sync", dv, nw3q)
                STW(dr_w3q[c * 128:(c + 1) * 128, :], w3raw)
                npb = sems["pe"].n
                hsb = B[:, 2048:3072]
                na = C(lambda e: e.activation(hsb, PQ[:], ACT.Sigmoid),
                       waits=[(pe, npb), (dv, nw3q)])
                nh = V(lambda e: e.tensor_tensor(hsb, hsb, PB[:, 0:1024],
                                                 ALU.mult), waits=[(ac, na)])
                kb.wait("sync", dv, nh)
                STW(dr_h[c * 128:(c + 1) * 128, :], hsb)
                kb.wait("tensor", dv, nh)
                kb.wait("tensor", ac, na)
            for (sm_, tt_) in ST.totals():
                kb.wait("sync", sm_, tt_)

            htr = WRing(B, 128, 16)
            w3mark = {0: 0, 1: 0}
            for tp_i in range(4):
                t0, t1 = 2 * tp_i, 2 * tp_i + 1
                for c in range(NCH):
                    p3 = c % 2
                    w3raw = B[:, 2048 + p3 * 2048:2048 + (p3 + 1) * 2048]
                    kb.wait("scalar", pe, w3mark[p3])
                    sm3, ln3 = LDW.load1(w3raw,
                                         dr_w3q[c * 128:(c + 1) * 128, :])
                    nq3 = 0
                    hA, slA, (smA, lnA) = htr.get(
                        dr_h[c * 128:(c + 1) * 128, t0 * 128:(t0 + 1) * 128])
                    hB, slB, (smB, lnB) = htr.get(
                        dr_h[c * 128:(c + 1) * 128, t1 * 128:(t1 + 1) * 128])
                    for blk in range(4):
                        T(lambda e, a=hA,
                          w=w3raw[:, blk * 512:(blk + 1) * 512],
                          blk=blk, st=(c == 0), sp=(c == NCH - 1):
                          e.matmul(PB[:, blk * 512:(blk + 1) * 512], a, w,
                                   start=st, stop=sp),
                          waits=[(smA, lnA), (sm3, ln3)] if blk == 0
                          else [], inc=True)
                        pqs = (PQ[:, 0:512], PQ[:, 512:1024],
                               PT1[:, 0:512], PT2[:, 0:512])[blk]
                        T(lambda e, a=hB,
                          w=w3raw[:, blk * 512:(blk + 1) * 512],
                          p=pqs, st=(c == 0), sp=(c == NCH - 1):
                          e.matmul(p, a, w, start=st, stop=sp),
                          waits=[(smB, lnB)] if blk == 0 else [],
                          inc=(blk == 3))
                    def _cb3(v, p3=p3):
                        w3mark[p3] = v
                    kb.pe_pending.append(_cb3)
                    htr.used(slA)
                    htr.used(slB)
                npj = sems["pe"].n
                nv0 = V(lambda e, t=t0: e.tensor_tensor(
                    RES[:, t * D:(t + 1) * D], PB[:],
                    RES[:, t * D:(t + 1) * D], ALU.add), waits=[(pe, npj)])
                V(lambda e, t=t1: e.tensor_tensor(
                    RES[:, t * D:t * D + 512], PQ[:, 0:512],
                    RES[:, t * D:t * D + 512], ALU.add))
                V(lambda e, t=t1: e.tensor_tensor(
                    RES[:, t * D + 512:t * D + 1024], PQ[:, 512:1024],
                    RES[:, t * D + 512:t * D + 1024], ALU.add))
                V(lambda e, t=t1: e.tensor_tensor(
                    RES[:, t * D + 1024:t * D + 1536], PT1[:, 0:512],
                    RES[:, t * D + 1024:t * D + 1536], ALU.add))
                nv1 = V(lambda e, t=t1: e.tensor_tensor(
                    RES[:, t * D + 1536:t * D + 2048], PT2[:, 0:512],
                    RES[:, t * D + 1536:t * D + 2048], ALU.add))
                kb.wait("tensor", dv, nv1)

        kb.barrier()
        for t in range(NT):
            ST.load1(out_d[t * 128:(t + 1) * 128, :], RES[:, t * D:(t + 1) * D])
        for (sm_, tt_) in ST.totals():
            kb.wait("sync", sm_, tt_)

        with nc.Block() as block:
            @block.sync
            def _(e):
                for fn in kb.q["sync"]:
                    fn(e)

            @block.tensor
            def _(e):
                for fn in kb.q["tensor"]:
                    fn(e)

            @block.vector
            def _(e):
                for fn in kb.q["vector"]:
                    fn(e)

            @block.scalar
            def _(e):
                for fn in kb.q["scalar"]:
                    fn(e)

            @block.gpsimd
            def _(e):
                for fn in kb.q["gpsimd"]:
                    fn(e)

    return nc


def _vtile(v):
    v = np.asarray(v, dtype=np.float32)
    ndi, ndo = v.shape[0] // 128, v.shape[1] // 128
    t = v.reshape(ndi, 128, ndo, 128).transpose(2, 1, 0, 3)
    return np.ascontiguousarray(t.reshape(ndo, 128, ndi * 128))


def prep_inputs(kw):
    X = np.asarray(kw["L0_attn_x"], dtype=np.float32)
    keys = np.asarray(kw["L0_ffn_keys"], dtype=np.float32)
    common = {
        "l0_wq": _vtile(np.asarray(kw["L0_attn_Wq"]).T),
        "l0_wk": _vtile(np.asarray(kw["L0_attn_Wk"]).T),
        "l0_wv": _vtile(np.asarray(kw["L0_attn_Wv"]).T),
        "l0_wo": np.ascontiguousarray(np.asarray(kw["L0_attn_Wo"]).T),
        "l0_bq": np.ascontiguousarray(np.asarray(kw["L0_attn_bq"],
                                                 dtype=np.float32)
                                      .reshape(16, 128).T),
        "l0_bk": np.ascontiguousarray(np.asarray(kw["L0_attn_bk"],
                                                 dtype=np.float32)
                                      .reshape(16, 128).T),
        "l0_bv": np.ascontiguousarray(np.asarray(kw["L0_attn_bv"],
                                                 dtype=np.float32)
                                      .reshape(16, 128).T),
        "l0_bo": np.ascontiguousarray(np.asarray(kw["L0_attn_bo"],
                                                 dtype=np.float32)
                                      .reshape(1, D)),
        "f0_wq": _vtile(np.asarray(kw["L0_ffn_Wq"])),
        "f0_k1t": np.ascontiguousarray(keys[:, :, 0, :].transpose(0, 2, 1)),
        "f0_k2t": np.ascontiguousarray(keys[:, :, 1, :].transpose(0, 2, 1)),
        "f0_down": np.ascontiguousarray(np.asarray(kw["L0_ffn_e_down"],
                                                   dtype=np.float32)),
        "f0_up": np.ascontiguousarray(np.asarray(kw["L0_ffn_e_up"],
                                                 dtype=np.float32)),
        "l1_wqkv": _vtile(np.asarray(kw["L1_attn_w_qkv"])),
        "l1_wo": np.ascontiguousarray(np.asarray(kw["L1_attn_w_o"],
                                                 dtype=np.float32)),
        "f1_w1t": _vtile(np.asarray(kw["L1_ffn_w1"]).T),
        "f1_w2t": _vtile(np.asarray(kw["L1_ffn_w2"]).T),
        "f1_w3t": np.ascontiguousarray(np.asarray(kw["L1_ffn_w3"],
                                                  dtype=np.float32).T),
        "c_ident": np.eye(128, dtype=np.float32),
        "c_iota": np.broadcast_to(np.arange(64, dtype=np.float32),
                                  (128, 64)).copy(),
    }
    in_maps = []
    for c in range(8):
        b, half = c // 2, c % 2
        m = dict(common)
        m["x_in"] = np.ascontiguousarray(X[b, half * TLOC:(half + 1) * TLOC, :])
        m["cs_rows"] = np.arange(NH, dtype=np.int32).reshape(NH, 1) * 2 + half
        in_maps.append(m)
    return in_maps


def kernel(**kw):
    in_maps = prep_inputs(kw)
    nc = build_nc(phases=4)
    res = run_bass_kernel_spmd(nc, in_maps, core_ids=list(range(8)))
    out = np.empty((4, 2048, D), dtype=np.float32)
    for c in range(8):
        b, half = c // 2, c % 2
        out[b, half * TLOC:(half + 1) * TLOC, :] = res.results[c]["out"]
    return out


# version 09
# speedup vs baseline: 1.0563x; 1.0226x over previous
import math
from contextlib import ExitStack

import numpy as np

import concourse.bass as bass
import concourse.mybir as mybir
from concourse.bass_utils import run_bass_kernel_spmd

F32 = mybir.dt.float32
I32 = mybir.dt.int32
U32 = mybir.dt.uint32
ALU = mybir.AluOpType
ACT = mybir.ActivationFunctionType
AX = mybir.AxisListType

D = 2048
TLOC = 1024
NT = TLOC // 128
NH = 16
DH = 128
EPS_LN = 1e-5
FH = 5632
NCH = FH // 128
RSC = 1.0 / math.sqrt(DH)

ENGS = ("sync", "tensor", "vector", "scalar", "gpsimd")


class Sem:
    def __init__(self, handle, name):
        self.h = handle
        self.name = name
        self.n = 0

    def bump(self, k=1):
        self.n += k
        return self.n


class KB:
    def __init__(self, nc, sems):
        self.nc = nc
        self.q = {e: [] for e in ENGS}
        self.s = sems
        self.pe_pending = []

    def wait(self, eng, sem, val):
        if val <= 0:
            return
        self.q[eng].append(lambda e, s=sem.h, v=val: e.wait_ge(s, v))

    def op(self, eng, fn, sem=None, k=1):
        if sem is None:
            self.q[eng].append(fn)
            return None
        n = sem.bump(k)
        self.q[eng].append(lambda e, f=fn, s=sem.h, kk=k: f(e).then_inc(s, kk))
        if sem is self.s["pe"]:
            for cb in self.pe_pending:
                cb(n)
            self.pe_pending = []
        return n

    def V(self, fn, waits=()):
        dv = self.s["dv"]
        for (sem, val) in waits:
            self.wait("vector", sem, val)
        self.wait("vector", dv, dv.n)
        return self.op("vector", fn, dv)

    def C(self, fn, waits=()):
        ac = self.s["ac"]
        for (sem, val) in waits:
            self.wait("scalar", sem, val)
        self.wait("scalar", ac, ac.n)
        return self.op("scalar", fn, ac)

    def G(self, fn, waits=()):
        gp = self.s["gp"]
        for (sem, val) in waits:
            self.wait("gpsimd", sem, val)
        self.wait("gpsimd", gp, gp.n)
        return self.op("gpsimd", fn, gp)

    def T(self, fn, waits=(), inc=False):
        for (sem, val) in waits:
            self.wait("tensor", sem, val)
        return self.op("tensor", fn, self.s["pe"] if inc else None)

    def barrier(self):
        for ch in getattr(self, "channels", []):
            if ch.open:
                ch.close()
        marks = {nm: self.s[nm].n for nm in self.s}
        for eng in ENGS:
            for nm, v in marks.items():
                self.wait(eng, self.s[nm], v)


class LoadChan:

    def __init__(self, kb, sems, eng="sync"):
        self.kb = kb
        self.sems = sems
        self.eng = eng
        self.cur = 0
        self.closed_total = [0] * len(sems)
        self.open = False

    def load(self, out_ap, in_ap):
        kb = self.kb
        s = self.sems[self.cur]
        if not self.open:
            kb.wait(self.eng, s, self.closed_total[self.cur])
            self.open = True
        kb.op(self.eng, lambda e, o=out_ap, i=in_ap: e.dma_start(out=o, in_=i),
              s, 16)
        return s, s.n

    def close(self):
        s = self.sems[self.cur]
        tot = s.n
        self.closed_total[self.cur] = tot
        self.cur = (self.cur + 1) % len(self.sems)
        self.open = False
        return s, tot

    def load1(self, out_ap, in_ap):
        assert not self.open
        sem, n = self.load(out_ap, in_ap)
        self.close()
        return sem, n

    def totals(self):
        assert not self.open
        return [(sm, sm.n) for sm in self.sems]


def build_nc(phases=4):
    nc = bass.Bass("TRN2", target_bir_lowering=False)

    x_in = nc.dram_tensor("x_in", [TLOC, D], F32, kind="ExternalInput").ap()
    out_d = nc.dram_tensor("out", [TLOC, D], F32, kind="ExternalOutput").ap()

    l0_wq = nc.dram_tensor("l0_wq", [NH, 128, D], F32, kind="ExternalInput").ap()
    l0_wk = nc.dram_tensor("l0_wk", [NH, 128, D], F32, kind="ExternalInput").ap()
    l0_wv = nc.dram_tensor("l0_wv", [NH, 128, D], F32, kind="ExternalInput").ap()
    l0_wo = nc.dram_tensor("l0_wo", [D, D], F32, kind="ExternalInput").ap()
    l0_bq = nc.dram_tensor("l0_bq", [128, 16], F32, kind="ExternalInput").ap()
    l0_bk = nc.dram_tensor("l0_bk", [128, 16], F32, kind="ExternalInput").ap()
    l0_bv = nc.dram_tensor("l0_bv", [128, 16], F32, kind="ExternalInput").ap()
    l0_bo = nc.dram_tensor("l0_bo", [1, D], F32, kind="ExternalInput").ap()

    f0_wq = nc.dram_tensor("f0_wq", [4, 128, D], F32, kind="ExternalInput").ap()
    f0_k1t = nc.dram_tensor("f0_k1t", [4, 64, 128], F32, kind="ExternalInput").ap()
    f0_k2t = nc.dram_tensor("f0_k2t", [4, 64, 128], F32, kind="ExternalInput").ap()
    f0_down = nc.dram_tensor("f0_down", [16384, D], F32, kind="ExternalInput").ap()
    f0_up = nc.dram_tensor("f0_up", [16384, D], F32, kind="ExternalInput").ap()

    l1_wqkv = nc.dram_tensor("l1_wqkv", [48, 128, D], F32, kind="ExternalInput").ap()
    l1_wo = nc.dram_tensor("l1_wo", [D, D], F32, kind="ExternalInput").ap()

    f1_w1t = nc.dram_tensor("f1_w1t", [NCH, 128, D], F32, kind="ExternalInput").ap()
    f1_w2t = nc.dram_tensor("f1_w2t", [NCH, 128, D], F32, kind="ExternalInput").ap()
    f1_w3t = nc.dram_tensor("f1_w3t", [FH, D], F32, kind="ExternalInput").ap()

    c_ident = nc.dram_tensor("c_ident", [128, 128], F32, kind="ExternalInput").ap()
    c_iota = nc.dram_tensor("c_iota", [128, 64], F32, kind="ExternalInput").ap()
    cs_rows = nc.dram_tensor("cs_rows", [NH, 1], I32, kind="ExternalInput").ap()

    dr_kt = nc.dram_tensor("dr_kt", [D, TLOC], F32, kind="Internal").ap()
    dr_kt_ag = nc.dram_tensor("dr_kt_ag", [2 * D, TLOC], F32, kind="Internal").ap()
    dr_v = nc.dram_tensor("dr_v", [TLOC, D], F32, kind="Internal").ap()
    dr_v_ag = nc.dram_tensor("dr_v_ag", [2 * TLOC, D], F32, kind="Internal").ap()
    dr_qt = nc.dram_tensor("dr_qt", [D, TLOC], F32, kind="Internal").ap()
    dr_attn = nc.dram_tensor("dr_attn", [D, TLOC], F32, kind="Internal").ap()
    dr_xn = nc.dram_tensor("dr_xn", [TLOC, D], F32, kind="Internal").ap()
    dr_kt1 = nc.dram_tensor("dr_kt1", [D, TLOC], F32, kind="Internal").ap()
    dr_kt1_ag = nc.dram_tensor("dr_kt1_ag", [2 * D, TLOC], F32, kind="Internal").ap()
    dr_cs = nc.dram_tensor("dr_cs", [NH, 2048], F32, kind="Internal").ap()
    dr_cs_ar = nc.dram_tensor("dr_cs_ar", [NH, 2048], F32, kind="Internal").ap()
    dr_h = nc.dram_tensor("dr_h", [FH, TLOC], F32, kind="Internal").ap()
    dr_w3q = nc.dram_tensor("dr_w3q", [FH, D], F32, kind="Internal").ap()

    ctx = ExitStack()
    with ctx:
        def sbuf(name, shape, dt=F32):
            return ctx.enter_context(nc.sbuf_tensor(name, shape, dt))

        def psum(name, shape, dt=F32):
            return ctx.enter_context(nc.psum_tensor(name, shape, dt))

        RES = sbuf("RES", [128, NT * D])
        A = sbuf("A", [128, 16384])
        B = sbuf("B", [128, 8192])
        SM = sbuf("SM", [128, 1536])
        WR1 = sbuf("WR1", [128, 4096])
        WR2 = sbuf("WR2", [128, 2048])
        SBIAS = sbuf("SBIAS", [1, D])
        IDENT = sbuf("IDENT", [128, 128])
        IOTA = sbuf("IOTA", [128, 64])
        KEYS = sbuf("KEYS", [128, 1024])
        ONES1 = sbuf("ONES1", [1, 128])
        ONES128 = sbuf("ONES128", [128, 1])
        U1 = sbuf("U1", [128, 8], U32)
        U2 = sbuf("U2", [128, 8], U32)
        UP = sbuf("UPOS", [128, 8], U32)
        IDXI = sbuf("IDXI", [128, 32], I32)
        CSR = sbuf("CSR", [NH, 1], I32)

        PB = psum("PB", [128, 2048])
        PQ = psum("PQ", [128, 1024])
        PT1 = psum("PT1", [128, 512])
        PT2 = psum("PT2", [128, 512])

        sems = {}
        for nm in ["pe", "dv", "ac", "gp", "ld0", "ld1", "ld2", "ld3",
                   "st", "st2", "gd0", "gd1", "cc", "lw0", "lw1"]:
            sems[nm] = Sem(ctx.enter_context(nc.semaphore(nm)), nm)

        kb = KB(nc, sems)
        LD = LoadChan(kb, [sems["ld0"], sems["ld1"], sems["ld2"], sems["ld3"]])
        ST = LoadChan(kb, [sems["st"], sems["st2"]])
        LDW = LoadChan(kb, [sems["lw0"], sems["lw1"]], eng="scalar")
        kb.channels = [LD, ST, LDW]

        def STW(dst, src):
            sm_, tt_ = ST.load1(dst, src)
            kb.wait("vector", sm_, tt_)
            kb.wait("scalar", sm_, tt_)
            return sm_, tt_

        BQ = SM[:, 0:16]
        BK = SM[:, 16:32]
        BV = SM[:, 32:48]
        STM = 48

        def smc(i, w=1):
            return SM[:, STM + i:STM + i + w]

        SCO = SM[:, 80:112]
        SCOR = SM[:, 112:144]
        IDXF = SM[:, 144:176]
        HID = SM[:, 176:208]
        GG = SM[:, 208:240]
        S1 = SM[:, 240:248]
        S2 = SM[:, 248:256]
        FS = SM[:, 256:264]
        I1F = SM[:, 264:272]
        I2F = SM[:, 272:280]
        PF = SM[:, 280:288]
        I64 = SM[:, 288:352]
        S64 = SM[:, 352:416]
        EQB = SM[:, 416:928]
        SIM1 = SM[:, 928:1056]
        SIM2 = SM[:, 1056:1184]
        GAC1 = SM[:, 1184:1185]
        GAC2 = SM[:, 1185:1186]
        GAC3 = SM[:, 1186:1187]
        GPART = SM[:, 1187:1188]
        GTOT = SM[0:1, 1190:1191]
        CSML = SM[:, 1200:1216]

        V, C, G, T, W = kb.V, kb.C, kb.G, kb.T, kb.wait
        pe, dv, ac, gp, cc = (sems["pe"], sems["dv"], sems["ac"], sems["gp"],
                              sems["cc"])
        rg = [[0, 1], [2, 3], [4, 5], [6, 7]]

        for t in range(NT):
            LD.load(RES[:, t * D:(t + 1) * D], x_in[t * 128:(t + 1) * 128, :])
        LD.load(IDENT[:], c_ident)
        LD.load(IOTA[:], c_iota)
        LD.load(BQ, l0_bq)
        LD.load(BK, l0_bk)
        LD.load(BV, l0_bv)
        LD.load(SBIAS[:], l0_bo)
        LD.load(CSR[:], cs_rows)
        for h in range(4):
            LD.load(KEYS[0:64, h * 128:(h + 1) * 128], f0_k1t[h])
            LD.load(KEYS[64:128, h * 128:(h + 1) * 128], f0_k2t[h])
        s_pro, n_pro = LD.close()
        for eng in ("vector", "tensor", "scalar", "gpsimd"):
            kb.wait(eng, s_pro, n_pro)
        V(lambda e: e.memset(ONES1[:], 1.0))
        V(lambda e: e.memset(ONES128[:], 1.0))

        def layernorm_tile(t, xn_ap):
            rt = RES[:, t * D:(t + 1) * D]
            m = smc(0)
            vv = smc(1)
            inv = smc(2)
            t1 = smc(3)
            V(lambda e: e.reduce_sum(m, rt, AX.X))
            V(lambda e: e.tensor_scalar_mul(m, m, 1.0 / D))
            nc_ = V(lambda e: e.tensor_scalar(xn_ap, rt, m, None, ALU.subtract))
            nsq = C(lambda e: e.activation(B[:, 6144:8192], xn_ap, ACT.Square,
                                           accum_out=vv), waits=[(dv, nc_)])
            V(lambda e: e.tensor_scalar(vv, vv, 1.0 / D, EPS_LN,
                                        ALU.mult, ALU.add), waits=[(ac, nsq)])
            nv = sems["dv"].n
            na = C(lambda e: e.activation(inv, vv, ACT.Sqrt), waits=[(dv, nv)])
            V(lambda e: e.reciprocal(inv, inv), waits=[(ac, na)])
            V(lambda e: e.tensor_tensor(t1, inv, inv, ALU.mult))
            V(lambda e: e.tensor_tensor(t1, t1, vv, ALU.mult))
            V(lambda e: e.tensor_scalar(t1, t1, -0.5, 1.5, ALU.mult, ALU.add))
            V(lambda e: e.tensor_tensor(inv, inv, t1, ALU.mult))
            return V(lambda e: e.tensor_scalar_mul(xn_ap, xn_ap, inv))

        def ln_and_transpose(store_xn=None):
            tile_pe = {}
            bank_evac = {0: 0, 1: 0}
            for t in range(NT):
                xn = B[:, (t % 2) * D:((t % 2) + 1) * D]
                if (t - 2) in tile_pe:
                    kb.wait("vector", pe, tile_pe[t - 2])
                nln = layernorm_tile(t, xn)
                if store_xn is not None:
                    kb.wait("sync", dv, nln)
                    STW(store_xn[t * 128:(t + 1) * 128, :], xn)
                for c in range(16):
                    bk = c % 2
                    pb = PT1 if bk == 0 else PT2
                    waits = [(dv, nln)] if c < 2 else []
                    if bank_evac[bk]:
                        waits = waits + [(dv, bank_evac[bk])]
                    n = T(lambda e, s=xn[:, c * 128:(c + 1) * 128], p=pb:
                          e.transpose(p[:, 0:128], s, IDENT[:]),
                          waits=waits, inc=True)
                    dst = A[:, c * 1024 + t * 128: c * 1024 + (t + 1) * 128]
                    bank_evac[bk] = V(lambda e, p=pb, d=dst:
                                      e.tensor_copy(d, p[:, 0:128]),
                                      waits=[(pe, n)])
                tile_pe[t] = sems["pe"].n
            return sems["dv"].n

        class WRing:
            def __init__(self, area, slot_cols, nslots):
                self.area = area
                self.w = slot_cols
                self.n = nslots
                self.i = 0
                self.last_use = [0] * nslots

            def __init2__(self):
                pass

            def get(self, src_ap, rows=128, ch=None):
                ch = ch or LD
                sl = self.i % self.n
                self.i += 1
                ap = self.area[0:rows, sl * self.w:sl * self.w + self.w]
                kb.wait(ch.eng, pe, self.last_use[sl])
                sem, n = ch.load1(ap, src_ap)
                return ap, sl, (sem, n)

            def used(self, sl):
                def cb(v, sl=sl):
                    self.last_use[sl] = v
                kb.pe_pending.append(cb)

        wr1 = WRing(WR1, 2048, 2)
        wr2 = WRing(WR2, 512, 4)

        def proj_fm(w_dram, h, psum_ap):
            wt, sl, (sm, ln) = wr1.get(w_dram[h], ch=LDW)
            for half in range(2):
                for di in range(16):
                    rhs = A[:, di * 1024 + half * 512:
                            di * 1024 + half * 512 + 512]
                    T(lambda e, p=psum_ap, w=wt[:, di * 128:(di + 1) * 128],
                      r=rhs, st=(di == 0), sp=(di == 15), hh=half:
                      e.matmul(p[:, hh * 512:(hh + 1) * 512], w, r,
                               start=st, stop=sp),
                      waits=[(sm, ln)] if (di == 0 and half == 0) else [],
                      inc=(di == 15))
            wr1.used(sl)
            return sems["pe"].n

        def attn_scores(qt_sb, kt_sb, qt, waits0=()):
            nsc = None
            for kc in range(4):
                nsc = T(lambda e, q=qt_sb[:, qt * 128:(qt + 1) * 128],
                        r=kt_sb[:, kc * 512:(kc + 1) * 512], kc=kc:
                        e.matmul(PB[:, kc * 512:(kc + 1) * 512], q, r,
                                 start=True, stop=True),
                        waits=waits0 if kc == 0 else (), inc=(kc == 3))
            return nsc

        def attn_softmax(probs, nsc, par):
            nm = smc(4 + 3 * par)
            rs = smc(5 + 3 * par)
            ri = smc(6 + 3 * par)
            V(lambda e, nm=nm: e.reduce_max(nm, PB[:], AX.X, negate=True),
              waits=[(pe, nsc)])
            nvm = V(lambda e, nm=nm: e.tensor_scalar_mul(nm, nm, RSC))
            ne = C(lambda e, nm=nm, rs=rs, p=probs:
                   e.activation(p, PB[:], ACT.Exp, bias=nm,
                                scale=RSC, accum_out=rs),
                   waits=[(dv, nvm)])
            nri = V(lambda e, ri=ri, rs=rs: e.reciprocal(ri, rs),
                    waits=[(ac, ne)])
            nsm = C(lambda e, ri=ri, p=probs:
                    e.activation(p, p, ACT.Copy, scale=ri),
                    waits=[(dv, nri)])
            return ne, nsm

        gsteps = []
        if phases >= 4:
            gpar = [0]
            gmarkp = {0: 0, 1: 0}
            gfirst = {}

            def g_step(src, gac):
                def run():
                    p = gpar[0] % 2
                    gpar[0] += 1
                    wdt = B[:, p * 2048:(p + 1) * 2048]
                    kb.wait("sync", dv, gmarkp[p])
                    s_w, n_w = LD.load1(wdt, src)
                    V(lambda e: e.reduce_sum(GPART, wdt, AX.X,
                                             apply_absolute_value=True),
                      waits=[(s_w, n_w)])
                    if gac.offset not in gfirst:
                        gfirst[gac.offset] = True
                        nr = V(lambda e: e.tensor_copy(gac, GPART))
                    else:
                        nr = V(lambda e: e.tensor_tensor(gac, gac, GPART,
                                                         ALU.add))
                    gmarkp[p] = nr
                return run

            for c_ in range(NCH):
                gsteps.append(g_step(f1_w1t[c_], GAC1))
            for c_ in range(NCH):
                gsteps.append(g_step(f1_w2t[c_], GAC2))
            for c_ in range(NCH):
                gsteps.append(g_step(f1_w3t[c_ * 128:(c_ + 1) * 128, :], GAC3))

        nxt = ln_and_transpose()
        kb.wait("tensor", dv, nxt)
        if phases >= 4:
            gmarkp[0] = nxt
            gmarkp[1] = nxt

        for h in range(NH):
            for _ in range(9):
                if gsteps:
                    gsteps.pop(0)()
            nq = proj_fm(l0_wq, h, PQ)
            qsb = B[:, 4096:5120]
            nv = V(lambda e, b=BQ[:, h:h + 1], d=qsb:
                   e.tensor_scalar_add(d, PQ[:], b), waits=[(pe, nq)])
            kb.wait("sync", dv, nv)
            STW(dr_qt[h * 128:(h + 1) * 128, :], qsb)
            kb.wait("tensor", dv, nv)
            nk = proj_fm(l0_wk, h, PQ)
            ksb = B[:, 5120:6144]
            nv = V(lambda e, b=BK[:, h:h + 1], d=ksb:
                   e.tensor_scalar_add(d, PQ[:], b), waits=[(pe, nk)])
            kb.wait("sync", dv, nv)
            STW(dr_kt[h * 128:(h + 1) * 128, :], ksb)
            kb.wait("tensor", dv, nv)
            nvm = proj_fm(l0_wv, h, PQ)
            vsb = B[:, 4096:5120]
            nv = V(lambda e, b=BV[:, h:h + 1], d=vsb:
                   e.tensor_scalar_add(d, PQ[:], b), waits=[(pe, nvm)])
            stage = B[:, 6144:6272]
            prev_ev = 0
            for t in range(NT):
                pb = PT1 if t % 2 == 0 else PT2
                waits = [(dv, nv)]
                if prev_ev:
                    waits.append((dv, prev_ev))
                n = T(lambda e, s=vsb[:, t * 128:(t + 1) * 128], p=pb:
                      e.transpose(p[:, 0:128], s, IDENT[:]),
                      waits=waits, inc=True)
                nve = V(lambda e, p=pb, d=stage: e.tensor_copy(d, p[:, 0:128]),
                        waits=[(pe, n)])
                kb.wait("sync", dv, nve)
                s_s, n_s = ST.load1(
                    dr_v[t * 128:(t + 1) * 128, h * 128:(h + 1) * 128], stage)
                kb.wait("vector", s_s, n_s)
                prev_ev = nve
            kb.wait("tensor", dv, nv)
        while gsteps:
            gsteps.pop(0)()
        n_gfin = 0
        if phases >= 4:
            T1P = smc(10)
            T1N = smc(12)
            T2P = smc(14)
            T2N = smc(16)
            T3P = smc(18)
            T3N = smc(20)

            def g_fin(gac, tp_ap, tn_ap):
                ng = sems["dv"].n
                ntt = T(lambda e, g=gac: e.matmul(PT2[0:1, 0:1], g, ONES128[:],
                                                  start=True, stop=True),
                        waits=[(dv, ng)], inc=True)
                V(lambda e: e.tensor_copy(GTOT, PT2[0:1, 0:1]),
                  waits=[(pe, ntt)])
                V(lambda e: e.tensor_scalar(GTOT, GTOT, 0.5 / (FH * D),
                                            0.5e-8, ALU.mult, ALU.add))
                ngt = sems["dv"].n
                nrep = T(lambda e: e.matmul(PT2[:, 0:1], ONES1[:], GTOT,
                                            start=True, stop=True),
                         waits=[(dv, ngt)], inc=True)
                V(lambda e, d=tp_ap: e.tensor_copy(d, PT2[:, 0:1]),
                  waits=[(pe, nrep)])
                return V(lambda e, d=tn_ap, sr=tp_ap:
                         e.tensor_scalar_mul(d, sr, -1.0))

            g_fin(GAC1, T1P, T1N)
            g_fin(GAC2, T2P, T2N)
            n_gfin = g_fin(GAC3, T3P, T3N)

        for (sm_, tt_) in ST.totals():
            kb.wait("gpsimd", sm_, tt_)
        n_cc_k = kb.op("gpsimd", lambda e: e.collective_compute(
            "AllGather", ALU.bypass, replica_groups=rg,
            ins=[dr_kt], outs=[dr_kt_ag]), cc)
        n_cc_v = kb.op("gpsimd", lambda e: e.collective_compute(
            "AllGather", ALU.bypass, replica_groups=rg,
            ins=[dr_v], outs=[dr_v_ag]), cc)

        kb.wait("sync", cc, n_cc_v)
        for h in range(NH):
            kt_sb = A[:, 0:2048]
            v_sb = A[:, 2048:4096]
            qt_sb = A[:, 4096:5120]
            probs = A[:, 5120:7168]
            probs2 = A[:, 9216:11264]
            pt_sb = A[:, 7168:9216]
            att_h = B[:, 0:1024]
            kb.wait("sync", pe, sems["pe"].n)
            kb.wait("sync", dv, sems["dv"].n)
            kb.wait("sync", ac, sems["ac"].n)
            LD.load(kt_sb[:, 0:1024], dr_kt_ag[h * 128:(h + 1) * 128, :])
            LD.load(kt_sb[:, 1024:2048],
                    dr_kt_ag[D + h * 128:D + (h + 1) * 128, :])
            LD.load(v_sb.rearrange("p (kt d) -> p kt d", d=128),
                    dr_v_ag[:, h * 128:(h + 1) * 128]
                    .rearrange("(kt p) d -> p kt d", p=128))
            LD.load(qt_sb[:], dr_qt[h * 128:(h + 1) * 128, :])
            s_h, n_h = LD.close()
            for eng in ("tensor", "vector", "scalar"):
                kb.wait(eng, s_h, n_h)
            nexp = {}
            nsm = {}
            for qt in range(NT + 1):
                if qt < NT:
                    par = qt % 2
                    pr = probs if par == 0 else probs2
                    w0 = [(ac, nexp[qt - 1])] if qt >= 1 else []
                    nsc = attn_scores(qt_sb, kt_sb, qt, waits0=w0)
                    nexp[qt], nsm[qt] = attn_softmax(pr, nsc, par)
                if qt == 0:
                    continue
                qp = qt - 1
                prp = probs if qp % 2 == 0 else probs2
                lastpt = []
                bank_evac = {0: 0, 1: 0}
                for grp in range(4):
                    bk = grp % 2
                    pb = PT1 if bk == 0 else PT2
                    waits = [(ac, nsm[qp])] if grp < 2 else []
                    if bank_evac[bk]:
                        waits = waits + [(dv, bank_evac[bk])]
                    n = None
                    for k4 in range(4):
                        kc16 = grp * 4 + k4
                        n = T(lambda e, k4=k4,
                              sp=prp[:, kc16 * 128:(kc16 + 1) * 128],
                              p=pb: e.transpose(
                                  p[:, k4 * 128:(k4 + 1) * 128], sp,
                                  IDENT[:]),
                              waits=waits if k4 == 0 else [], inc=(k4 == 3))
                    nve = V(lambda e, p=pb, grp=grp:
                            e.tensor_copy(pt_sb[:, grp * 512:(grp + 1) * 512],
                                          p[:, 0:512]), waits=[(pe, n)])
                    bank_evac[bk] = nve
                    lastpt.append(nve)
                nav = None
                for kc16 in range(16):
                    nav = T(lambda e, v=v_sb[:, kc16 * 128:(kc16 + 1) * 128],
                            r=pt_sb[:, kc16 * 128:(kc16 + 1) * 128],
                            st=(kc16 == 0), sp=(kc16 == 15):
                            e.matmul(PQ[:, 0:128], v, r, start=st, stop=sp),
                            waits=[(dv, lastpt[3])] if kc16 == 0 else [],
                            inc=(kc16 == 15))
                nve = V(lambda e, d=att_h[:, qp * 128:(qp + 1) * 128]:
                        e.tensor_copy(d, PQ[:, 0:128]), waits=[(pe, nav)])
                kb.wait("tensor", dv, nve)
            kb.wait("sync", dv, sems["dv"].n)
            STW(dr_attn[h * 128:(h + 1) * 128, :], att_h)
        for (sm_, tt_) in ST.totals():
            kb.wait("sync", sm_, tt_)
        atr = WRing(B, 128, 16)
        for t in range(NT):
            for blk in range(4):
                T(lambda e, blk=blk:
                  e.matmul(PB[:, blk * 512:(blk + 1) * 512], ONES1[:],
                           SBIAS[:, blk * 512:(blk + 1) * 512],
                           start=True, stop=False))
            for hc in range(16):
                at, sl, (sm, ln) = atr.get(
                    dr_attn[hc * 128:(hc + 1) * 128, t * 128:(t + 1) * 128])
                for blk in range(4):
                    wt, sl2, (sm2, ln2) = wr2.get(
                        l0_wo[hc * 128:(hc + 1) * 128,
                              blk * 512:(blk + 1) * 512])
                    T(lambda e, a=at, w=wt, blk=blk, sp=(hc == 15):
                      e.matmul(PB[:, blk * 512:(blk + 1) * 512], a, w,
                               start=False, stop=sp),
                      waits=[(sm, ln), (sm2, ln2)],
                      inc=(blk == 3))
                    wr2.used(sl2)
                atr.used(sl)
            npj = sems["pe"].n
            nve = V(lambda e, t=t: e.tensor_tensor(
                RES[:, t * D:(t + 1) * D], PB[:],
                RES[:, t * D:(t + 1) * D], ALU.add), waits=[(pe, npj)])
            kb.wait("tensor", dv, nve)

        if phases >= 2:
            kb.barrier()
            nxt = ln_and_transpose(store_xn=dr_xn)
            kb.wait("tensor", dv, nxt)
            for hq in range(4):
                nq = proj_fm(f0_wq, hq, PQ)
                nv = V(lambda e, d=B[:, hq * 1024:(hq + 1) * 1024]:
                       e.tensor_copy(d, PQ[:]), waits=[(pe, nq)])
                kb.wait("tensor", dv, nv)
            sim_ev = {0: 0, 1: 0}
            for t in range(NT):
                xn2 = B[:, 4096:6144]
                outacc = B[:, 6144:8192]
                kb.wait("sync", dv, sems["dv"].n)
                kb.wait("sync", gp, sems["gp"].n)
                LD.load(xn2, dr_xn[t * 128:(t + 1) * 128, :])
                s_t, n_t = LD.close()
                kb.wait("vector", s_t, n_t)
                for h in range(4):
                    q1 = B[0:64, h * 1024 + t * 128:h * 1024 + (t + 1) * 128]
                    q2 = B[64:128, h * 1024 + t * 128:h * 1024 + (t + 1) * 128]
                    w1 = [(dv, sim_ev[0])] if sim_ev[0] else []
                    n1 = T(lambda e, q=q1,
                           k=KEYS[0:64, h * 128:(h + 1) * 128]:
                           e.matmul(PT1[:, 0:128], q, k, start=True,
                                    stop=True), waits=w1, inc=True)
                    nv1 = V(lambda e: e.tensor_copy(SIM1, PT1[:, 0:128]),
                            waits=[(pe, n1)])
                    sim_ev[0] = nv1
                    w2 = [(dv, sim_ev[1])] if sim_ev[1] else []
                    n2 = T(lambda e, q=q2,
                           k=KEYS[64:128, h * 128:(h + 1) * 128]:
                           e.matmul(PT2[:, 0:128], q, k, start=True,
                                    stop=True), waits=w2, inc=True)
                    nv2 = V(lambda e: e.tensor_copy(SIM2, PT2[:, 0:128]),
                            waits=[(pe, n2)])
                    sim_ev[1] = nv2
                    V(lambda e: e.max(out=S1, in_=SIM1))
                    V(lambda e: e.max_index(out=U1[:], in_max=S1, in_values=SIM1))
                    V(lambda e: e.max(out=S2, in_=SIM2))
                    V(lambda e: e.max_index(out=U2[:], in_max=S2, in_values=SIM2))
                    V(lambda e: e.tensor_copy(I1F, U1[:]))
                    V(lambda e: e.tensor_copy(I2F, U2[:]))
                    V(lambda e: e.tensor_tensor(
                        S64.rearrange("p (i j) -> p i j", j=8),
                        S1.unsqueeze(2).to_broadcast([128, 8, 8]),
                        S2.unsqueeze(1).to_broadcast([128, 8, 8]), ALU.add))
                    V(lambda e: e.tensor_scalar_mul(I1F, I1F, 128.0))
                    V(lambda e: e.tensor_tensor(
                        I64.rearrange("p (i j) -> p i j", j=8),
                        I1F.unsqueeze(2).to_broadcast([128, 8, 8]),
                        I2F.unsqueeze(1).to_broadcast([128, 8, 8]), ALU.add))
                    V(lambda e: e.max(out=FS, in_=S64))
                    V(lambda e: e.max_index(out=UP[:], in_max=FS, in_values=S64))
                    V(lambda e: e.tensor_copy(PF, UP[:]))
                    V(lambda e, h=h: e.tensor_copy(SCO[:, h * 8:(h + 1) * 8],
                                                   FS))
                    V(lambda e: e.tensor_tensor(
                        EQB.rearrange("p (s j) -> p s j", j=64),
                        PF.unsqueeze(2).to_broadcast([128, 8, 64]),
                        IOTA[:].unsqueeze(1).to_broadcast([128, 8, 64]),
                        ALU.is_equal))
                    V(lambda e: e.tensor_tensor(
                        EQB.rearrange("p (s j) -> p s j", j=64),
                        EQB.rearrange("p (s j) -> p s j", j=64),
                        I64.unsqueeze(1).to_broadcast([128, 8, 64]), ALU.mult))
                    V(lambda e, h=h: e.tensor_reduce(
                        IDXF[:, h * 8:(h + 1) * 8],
                        EQB.rearrange("p (s j) -> p s j", j=64),
                        AX.X, ALU.add))
                kb.wait("vector", sems["gd0"], sems["gd0"].n)
                kb.wait("vector", sems["gd1"], sems["gd1"].n)
                nidx = V(lambda e: e.tensor_copy(IDXI[:], IDXF))
                nrel = V(lambda e: e.tensor_scalar_max(SCOR, SCO, 0.0))
                gmark = {}
                for g in range(16):
                    par = g % 2
                    gdsem = sems["gd0"] if par == 0 else sems["gd1"]
                    kb.wait("gpsimd", dv, nidx)
                    if g >= 2:
                        kb.wait("gpsimd", dv, gmark[g - 2])
                    dslot = [A[:, par * 4096 + j2 * 2048:
                               par * 4096 + (j2 + 1) * 2048] for j2 in range(2)]
                    uslot = [A[:, 8192 + par * 4096 + j2 * 2048:
                               8192 + par * 4096 + (j2 + 1) * 2048]
                             for j2 in range(2)]
                    for j2 in range(2):
                        j = g * 2 + j2
                        kb.op("gpsimd",
                              lambda e, j=j, d=dslot[j2]: e.indirect_dma_start(
                                  out=d, out_offset=None, in_=f0_down,
                                  in_offset=bass.IndirectOffsetOnAxis(
                                      IDXI[:, j:j + 1], 0)), gdsem, 16)
                        kb.op("gpsimd",
                              lambda e, j=j, u=uslot[j2]: e.indirect_dma_start(
                                  out=u, out_offset=None, in_=f0_up,
                                  in_offset=bass.IndirectOffsetOnAxis(
                                      IDXI[:, j:j + 1], 0)), gdsem, 16)
                    ngd = gdsem.n
                    for j2 in range(2):
                        j = g * 2 + j2
                        nmul = kb.G(lambda e, d=dslot[j2]: e.tensor_tensor(
                            d, d, xn2, ALU.mult),
                            waits=[(gdsem, ngd), (s_t, n_t)])
                        C(lambda e, d=dslot[j2], j=j: e.activation(
                            d, d, ACT.Copy, accum_out=HID[:, j:j + 1]),
                          waits=[(gp, nmul)])
                    nsl = C(lambda e, g=g: e.activation(
                        GG[:, g * 2:(g + 1) * 2],
                        HID[:, g * 2:(g + 1) * 2], ACT.Silu))
                    ngg = V(lambda e, g=g: e.tensor_tensor(
                        GG[:, g * 2:(g + 1) * 2], GG[:, g * 2:(g + 1) * 2],
                        SCOR[:, g * 2:(g + 1) * 2], ALU.mult),
                        waits=[(ac, nsl), (dv, nrel)])
                    for j2 in range(2):
                        j = g * 2 + j2
                        up = uslot[j2]
                        gcol = GG[:, j:j + 1]
                        if j == 0:
                            V(lambda e, u=up, gc=gcol: e.tensor_scalar(
                                outacc, u, gc, None, ALU.mult))
                        else:
                            V(lambda e, u=up, gc=gcol: e.scalar_tensor_tensor(
                                outacc, u, gc, outacc, ALU.mult, ALU.add))
                    gmark[g] = sems["dv"].n
                V(lambda e, t=t: e.tensor_tensor(
                    RES[:, t * D:(t + 1) * D], outacc,
                    RES[:, t * D:(t + 1) * D], ALU.add))

        if phases >= 3:
            kb.barrier()
            nxt = ln_and_transpose()
            kb.wait("tensor", dv, nxt)
            for h in range(NH):
                nq = proj_fm(l1_wqkv, h, PQ)
                qsb = B[:, 0:1024]
                nv = V(lambda e, d=qsb: e.tensor_copy(d, PQ[:]),
                       waits=[(pe, nq)])
                kb.wait("sync", dv, nv)
                STW(dr_qt[h * 128:(h + 1) * 128, :], qsb)
                kb.wait("tensor", dv, nv)
                nk = proj_fm(l1_wqkv, NH + h, PQ)
                ksb = B[:, 1024:2048]
                nv = V(lambda e, d=ksb: e.tensor_copy(d, PQ[:]),
                       waits=[(pe, nk)])
                kb.wait("sync", dv, nv)
                STW(dr_kt1[h * 128:(h + 1) * 128, :], ksb)
                kb.wait("tensor", dv, nv)
            for t in range(NT):
                for dc in range(16):
                    wt, sl, (sm, ln) = wr1.get(l1_wqkv[32 + dc], ch=LDW)
                    for di in range(16):
                        T(lambda e, w=wt[:, di * 128:(di + 1) * 128], dc=dc,
                          x=A[:, di * 1024 + t * 128:
                              di * 1024 + (t + 1) * 128],
                          st=(di == 0), sp=(di == 15):
                          e.matmul(PB[:, dc * 128:(dc + 1) * 128], x, w,
                                   start=st, stop=sp),
                          waits=[(sm, ln)] if di == 0 else [],
                          inc=(di == 15))
                    wr1.used(sl)
                nvt = sems["pe"].n
                vsb = B[:, 2048:4096]
                nv = V(lambda e, d=vsb: e.tensor_copy(d, PB[:]),
                       waits=[(pe, nvt)])
                kb.wait("sync", dv, nv)
                STW(dr_v[t * 128:(t + 1) * 128, :], vsb)
                kb.wait("tensor", dv, nv)
            for (sm_, tt_) in ST.totals():
                kb.wait("gpsimd", sm_, tt_)
            n_cc_k1 = kb.op("gpsimd", lambda e: e.collective_compute(
                "AllGather", ALU.bypass, replica_groups=rg,
                ins=[dr_kt1], outs=[dr_kt1_ag]), cc)
            kb.wait("sync", cc, n_cc_k1)
            rows_acc = B[:, 0:2048]
            for h in range(NH):
                kt_sb = A[:, 0:2048]
                qt_sb = A[:, 4096:5120]
                probs = A[:, 5120:7168]
                probs2 = A[:, 9216:11264]
                kb.wait("sync", pe, sems["pe"].n)
                kb.wait("sync", dv, sems["dv"].n)
                kb.wait("sync", ac, sems["ac"].n)
                LD.load(kt_sb[:, 0:1024], dr_kt1_ag[h * 128:(h + 1) * 128, :])
                LD.load(kt_sb[:, 1024:2048],
                        dr_kt1_ag[D + h * 128:D + (h + 1) * 128, :])
                LD.load(qt_sb[:], dr_qt[h * 128:(h + 1) * 128, :])
                s_h, n_h = LD.close()
                for eng in ("tensor", "vector", "scalar"):
                    kb.wait(eng, s_h, n_h)
                nacc = 0
                nexp = {}
                nsm = {}
                for qt in range(NT + 1):
                    if qt < NT:
                        par = qt % 2
                        pr = probs if par == 0 else probs2
                        w0 = [(ac, nexp[qt - 1])] if qt >= 1 else []
                        nsc = attn_scores(qt_sb, kt_sb, qt, waits0=w0)
                        nexp[qt], nsm[qt] = attn_softmax(pr, nsc, par)
                    if qt == 0:
                        continue
                    qp = qt - 1
                    prp = probs if qp % 2 == 0 else probs2
                    if qp == 0:
                        nacc = V(lambda e, p=prp: e.tensor_copy(rows_acc, p),
                                 waits=[(ac, nsm[qp])])
                    else:
                        nacc = V(lambda e, p=prp: e.tensor_tensor(
                            rows_acc, rows_acc, p, ALU.add),
                            waits=[(ac, nsm[qp])])
                for kc in range(4):
                    ncs = T(lambda e, kc=kc:
                            e.matmul(PT1[0:1, 0:512], ONES128[:],
                                     rows_acc[:, kc * 512:(kc + 1) * 512],
                                     start=True, stop=True),
                            waits=[(dv, nacc)] if kc == 0 else [], inc=True)
                    nve = V(lambda e, kc=kc: e.tensor_copy(
                        SBIAS[0:1, kc * 512:(kc + 1) * 512],
                        PT1[0:1, 0:512]), waits=[(pe, ncs)])
                    kb.wait("tensor", dv, nve)
                kb.wait("sync", dv, sems["dv"].n)
                STW(dr_cs[h:h + 1, :], SBIAS[0:1, :])
            for (sm_, tt_) in ST.totals():
                kb.wait("gpsimd", sm_, tt_)
            n_cc_cs = kb.op("gpsimd", lambda e: e.collective_compute(
                "AllReduce", ALU.add, replica_groups=rg,
                ins=[dr_cs], outs=[dr_cs_ar]), cc)
            csl = B[:, 4096:5120]
            kb.wait("gpsimd", cc, n_cc_cs)
            kb.wait("gpsimd", sems["gd0"], sems["gd0"].n)
            ngsel_sem = sems["gd0"]
            ngsel = kb.op("gpsimd", lambda e: e.indirect_dma_start(
                out=csl[0:16, :], out_offset=None,
                in_=dr_cs_ar.rearrange("h (s t) -> (h s) t", s=2),
                in_offset=bass.IndirectOffsetOnAxis(CSR[:], 0)),
                ngsel_sem, 16)
            for t in range(NT):
                v_sb = B[:, 0:2048]
                kb.wait("sync", pe, sems["pe"].n)
                kb.wait("sync", dv, sems["dv"].n)
                LD.load(v_sb, dr_v[t * 128:(t + 1) * 128, :])
                s_v, n_v = LD.close()
                nct = T(lambda e, s=csl[0:16, t * 128:(t + 1) * 128]:
                        e.transpose(PT1[:, 0:16], s, IDENT[0:16, 0:16]),
                        waits=[(ngsel_sem, ngsel)], inc=True)
                nve = V(lambda e: e.tensor_copy(CSML, PT1[:, 0:16]),
                        waits=[(pe, nct)])
                nsv = V(lambda e: e.tensor_tensor(
                    v_sb.rearrange("p (h d) -> p h d", d=128),
                    v_sb.rearrange("p (h d) -> p h d", d=128),
                    CSML.unsqueeze(2).to_broadcast([128, 16, 128]),
                    ALU.mult), waits=[(s_v, n_v)])
                bank_evac = {0: 0, 1: 0}
                for hc in range(16):
                    bk = hc % 2
                    pb = PT1 if bk == 0 else PT2
                    waits = [(dv, nsv)] if hc < 2 else []
                    if bank_evac[bk]:
                        waits = waits + [(dv, bank_evac[bk])]
                    n = T(lambda e, s=v_sb[:, hc * 128:(hc + 1) * 128], p=pb:
                          e.transpose(p[:, 0:128], s, IDENT[:]),
                          waits=waits, inc=True)
                    bank_evac[bk] = V(
                        lambda e, p=pb, d=A[:, hc * 128:(hc + 1) * 128]:
                        e.tensor_copy(d, p[:, 0:128]), waits=[(pe, n)])
                nvals = sems["dv"].n
                kb.wait("tensor", dv, nvals)
                for hc in range(16):
                    for blk in range(4):
                        wt, sl2, (sm2, ln2) = wr2.get(
                            l1_wo[hc * 128:(hc + 1) * 128,
                                  blk * 512:(blk + 1) * 512])
                        T(lambda e, a=A[:, hc * 128:(hc + 1) * 128], w=wt,
                          blk=blk, st=(hc == 0), sp=(hc == 15):
                          e.matmul(PB[:, blk * 512:(blk + 1) * 512], a, w,
                                   start=st, stop=sp),
                          waits=[(sm2, ln2)], inc=(blk == 3))
                        wr2.used(sl2)
                npj = sems["pe"].n
                nve = V(lambda e, t=t: e.tensor_tensor(
                    RES[:, t * D:(t + 1) * D], PB[:],
                    RES[:, t * D:(t + 1) * D], ALU.add), waits=[(pe, npj)])
                kb.wait("tensor", dv, nve)

        if phases >= 4:
            kb.barrier()
            nxt = ln_and_transpose()
            kb.wait("tensor", dv, nxt)

            def quantize(dst_ap, src_ap, tp, tn, tmp_ap, wl):
                V(lambda e: e.tensor_scalar(dst_ap, src_ap, tp, None,
                                            ALU.is_gt), waits=wl)
                V(lambda e: e.tensor_scalar(tmp_ap, src_ap, tn, None,
                                            ALU.is_lt))
                return V(lambda e: e.tensor_tensor(dst_ap, dst_ap, tmp_ap,
                                                   ALU.subtract))

            QWa = B[:, 4096:6144]
            QWb = B[:, 6144:8192]
            QTMP = B[:, 0:2048]
            qw_mark = {0: 0, 1: 0}
            for c in range(NCH):
                for wi, (wsrc, psm, tp, tn) in enumerate(
                        ((f1_w1t, PQ, T1P, T1N), (f1_w2t, PB, T2P, T2N))):
                    wt, sl, (sm, ln) = wr1.get(wsrc[c], ch=LDW)
                    par = wi
                    qw = QWa if par == 0 else QWb
                    kb.wait("vector", pe, qw_mark[par])
                    nq = quantize(qw, wt, tp, tn, QTMP,
                                  [(sm, ln), (dv, n_gfin)])
                    for di in range(16):
                        for half in range(2):
                            rhs = A[:, di * 1024 + half * 512:
                                    di * 1024 + half * 512 + 512]
                            T(lambda e, p=psm, di=di,
                              w=qw[:, di * 128:(di + 1) * 128], r=rhs,
                              st=(di == 0), sp=(di == 15), hh=half:
                              e.matmul(p[:, hh * 512:(hh + 1) * 512], w, r,
                                       start=st, stop=sp),
                              waits=[(dv, nq)] if di == 0 else [],
                              inc=(half == 1))
                    def _cb(v, par=par):
                        qw_mark[par] = v
                    kb.pe_pending.append(_cb)
                    wr1.used(sl)
                w3raw = QTMP
                w3tmp = B[:, 2048:3072 + 1024][:, 0:1024]
                kb.wait("sync", dv, sems["dv"].n)
                sw3, lw3 = LD.load1(w3raw, f1_w3t[c * 128:(c + 1) * 128, :])
                V(lambda e, w=w3raw: e.tensor_scalar(
                    B[:, 2048:4096], w, T3P, None, ALU.is_gt),
                  waits=[(sw3, lw3), (dv, n_gfin)])
                V(lambda e, w=w3raw: e.tensor_scalar(w, w, T3N, None,
                                                     ALU.is_lt))
                nw3q = V(lambda e, w=w3raw: e.tensor_tensor(
                    w, B[:, 2048:4096], w, ALU.subtract))
                kb.wait("sync", dv, nw3q)
                STW(dr_w3q[c * 128:(c + 1) * 128, :], w3raw)
                npb = sems["pe"].n
                hsb = B[:, 2048:3072]
                na = C(lambda e: e.activation(hsb, PQ[:], ACT.Sigmoid),
                       waits=[(pe, npb), (dv, nw3q)])
                nh = V(lambda e: e.tensor_tensor(hsb, hsb, PB[:, 0:1024],
                                                 ALU.mult), waits=[(ac, na)])
                kb.wait("sync", dv, nh)
                STW(dr_h[c * 128:(c + 1) * 128, :], hsb)
                kb.wait("tensor", dv, nh)
                kb.wait("tensor", ac, na)
            for (sm_, tt_) in ST.totals():
                kb.wait("sync", sm_, tt_)

            htr = WRing(B, 128, 16)
            w3mark = {0: 0, 1: 0}
            for tp_i in range(4):
                t0, t1 = 2 * tp_i, 2 * tp_i + 1
                for c in range(NCH):
                    p3 = c % 2
                    w3raw = B[:, 2048 + p3 * 2048:2048 + (p3 + 1) * 2048]
                    kb.wait("scalar", pe, w3mark[p3])
                    sm3, ln3 = LDW.load1(w3raw,
                                         dr_w3q[c * 128:(c + 1) * 128, :])
                    nq3 = 0
                    hA, slA, (smA, lnA) = htr.get(
                        dr_h[c * 128:(c + 1) * 128, t0 * 128:(t0 + 1) * 128])
                    hB, slB, (smB, lnB) = htr.get(
                        dr_h[c * 128:(c + 1) * 128, t1 * 128:(t1 + 1) * 128])
                    for blk in range(4):
                        T(lambda e, a=hA,
                          w=w3raw[:, blk * 512:(blk + 1) * 512],
                          blk=blk, st=(c == 0), sp=(c == NCH - 1):
                          e.matmul(PB[:, blk * 512:(blk + 1) * 512], a, w,
                                   start=st, stop=sp),
                          waits=[(smA, lnA), (sm3, ln3)] if blk == 0
                          else [], inc=True)
                        pqs = (PQ[:, 0:512], PQ[:, 512:1024],
                               PT1[:, 0:512], PT2[:, 0:512])[blk]
                        T(lambda e, a=hB,
                          w=w3raw[:, blk * 512:(blk + 1) * 512],
                          p=pqs, st=(c == 0), sp=(c == NCH - 1):
                          e.matmul(p, a, w, start=st, stop=sp),
                          waits=[(smB, lnB)] if blk == 0 else [],
                          inc=(blk == 3))
                    def _cb3(v, p3=p3):
                        w3mark[p3] = v
                    kb.pe_pending.append(_cb3)
                    htr.used(slA)
                    htr.used(slB)
                npj = sems["pe"].n
                nv0 = V(lambda e, t=t0: e.tensor_tensor(
                    RES[:, t * D:(t + 1) * D], PB[:],
                    RES[:, t * D:(t + 1) * D], ALU.add), waits=[(pe, npj)])
                V(lambda e, t=t1: e.tensor_tensor(
                    RES[:, t * D:t * D + 512], PQ[:, 0:512],
                    RES[:, t * D:t * D + 512], ALU.add))
                V(lambda e, t=t1: e.tensor_tensor(
                    RES[:, t * D + 512:t * D + 1024], PQ[:, 512:1024],
                    RES[:, t * D + 512:t * D + 1024], ALU.add))
                V(lambda e, t=t1: e.tensor_tensor(
                    RES[:, t * D + 1024:t * D + 1536], PT1[:, 0:512],
                    RES[:, t * D + 1024:t * D + 1536], ALU.add))
                nv1 = V(lambda e, t=t1: e.tensor_tensor(
                    RES[:, t * D + 1536:t * D + 2048], PT2[:, 0:512],
                    RES[:, t * D + 1536:t * D + 2048], ALU.add))
                kb.wait("tensor", dv, nv1)

        kb.barrier()
        for t in range(NT):
            ST.load1(out_d[t * 128:(t + 1) * 128, :], RES[:, t * D:(t + 1) * D])
        for (sm_, tt_) in ST.totals():
            kb.wait("sync", sm_, tt_)

        with nc.Block() as block:
            @block.sync
            def _(e):
                for fn in kb.q["sync"]:
                    fn(e)

            @block.tensor
            def _(e):
                for fn in kb.q["tensor"]:
                    fn(e)

            @block.vector
            def _(e):
                for fn in kb.q["vector"]:
                    fn(e)

            @block.scalar
            def _(e):
                for fn in kb.q["scalar"]:
                    fn(e)

            @block.gpsimd
            def _(e):
                for fn in kb.q["gpsimd"]:
                    fn(e)

    return nc


def _vtile(v):
    v = np.asarray(v, dtype=np.float32)
    ndi, ndo = v.shape[0] // 128, v.shape[1] // 128
    t = v.reshape(ndi, 128, ndo, 128).transpose(2, 1, 0, 3)
    return np.ascontiguousarray(t.reshape(ndo, 128, ndi * 128))


def prep_inputs(kw):
    X = np.asarray(kw["L0_attn_x"], dtype=np.float32)
    keys = np.asarray(kw["L0_ffn_keys"], dtype=np.float32)
    common = {
        "l0_wq": _vtile(np.asarray(kw["L0_attn_Wq"]).T),
        "l0_wk": _vtile(np.asarray(kw["L0_attn_Wk"]).T),
        "l0_wv": _vtile(np.asarray(kw["L0_attn_Wv"]).T),
        "l0_wo": np.ascontiguousarray(np.asarray(kw["L0_attn_Wo"]).T),
        "l0_bq": np.ascontiguousarray(np.asarray(kw["L0_attn_bq"],
                                                 dtype=np.float32)
                                      .reshape(16, 128).T),
        "l0_bk": np.ascontiguousarray(np.asarray(kw["L0_attn_bk"],
                                                 dtype=np.float32)
                                      .reshape(16, 128).T),
        "l0_bv": np.ascontiguousarray(np.asarray(kw["L0_attn_bv"],
                                                 dtype=np.float32)
                                      .reshape(16, 128).T),
        "l0_bo": np.ascontiguousarray(np.asarray(kw["L0_attn_bo"],
                                                 dtype=np.float32)
                                      .reshape(1, D)),
        "f0_wq": _vtile(np.asarray(kw["L0_ffn_Wq"])),
        "f0_k1t": np.ascontiguousarray(keys[:, :, 0, :].transpose(0, 2, 1)),
        "f0_k2t": np.ascontiguousarray(keys[:, :, 1, :].transpose(0, 2, 1)),
        "f0_down": np.ascontiguousarray(np.asarray(kw["L0_ffn_e_down"],
                                                   dtype=np.float32)),
        "f0_up": np.ascontiguousarray(np.asarray(kw["L0_ffn_e_up"],
                                                 dtype=np.float32)),
        "l1_wqkv": _vtile(np.asarray(kw["L1_attn_w_qkv"])),
        "l1_wo": np.ascontiguousarray(np.asarray(kw["L1_attn_w_o"],
                                                 dtype=np.float32)),
        "f1_w1t": _vtile(np.asarray(kw["L1_ffn_w1"]).T),
        "f1_w2t": _vtile(np.asarray(kw["L1_ffn_w2"]).T),
        "f1_w3t": np.ascontiguousarray(np.asarray(kw["L1_ffn_w3"],
                                                  dtype=np.float32).T),
        "c_ident": np.eye(128, dtype=np.float32),
        "c_iota": np.broadcast_to(np.arange(64, dtype=np.float32),
                                  (128, 64)).copy(),
    }
    in_maps = []
    for c in range(8):
        b, half = c // 2, c % 2
        m = dict(common)
        m["x_in"] = np.ascontiguousarray(X[b, half * TLOC:(half + 1) * TLOC, :])
        m["cs_rows"] = np.arange(NH, dtype=np.int32).reshape(NH, 1) * 2 + half
        in_maps.append(m)
    return in_maps


def kernel(**kw):
    in_maps = prep_inputs(kw)
    nc = build_nc(phases=4)
    res = run_bass_kernel_spmd(nc, in_maps, core_ids=list(range(8)))
    out = np.empty((4, 2048, D), dtype=np.float32)
    for c in range(8):
        b, half = c // 2, c % 2
        out[b, half * TLOC:(half + 1) * TLOC, :] = res.results[c]["out"]
    return out


# version 10 (final)
# speedup vs baseline: 1.0582x; 1.0018x over previous
import math
from contextlib import ExitStack

import numpy as np

import concourse.bass as bass
import concourse.mybir as mybir
from concourse.bass_utils import run_bass_kernel_spmd

F32 = mybir.dt.float32
I32 = mybir.dt.int32
U32 = mybir.dt.uint32
ALU = mybir.AluOpType
ACT = mybir.ActivationFunctionType
AX = mybir.AxisListType

D = 2048
TLOC = 1024
NT = TLOC // 128
NH = 16
DH = 128
EPS_LN = 1e-5
FH = 5632
NCH = FH // 128
RSC = 1.0 / math.sqrt(DH)

ENGS = ("sync", "tensor", "vector", "scalar", "gpsimd")


class Sem:
    def __init__(self, handle, name):
        self.h = handle
        self.name = name
        self.n = 0

    def bump(self, k=1):
        self.n += k
        return self.n


class KB:
    def __init__(self, nc, sems):
        self.nc = nc
        self.q = {e: [] for e in ENGS}
        self.s = sems
        self.pe_pending = []

    def wait(self, eng, sem, val):
        if val <= 0:
            return
        self.q[eng].append(lambda e, s=sem.h, v=val: e.wait_ge(s, v))

    def op(self, eng, fn, sem=None, k=1):
        if sem is None:
            self.q[eng].append(fn)
            return None
        n = sem.bump(k)
        self.q[eng].append(lambda e, f=fn, s=sem.h, kk=k: f(e).then_inc(s, kk))
        if sem is self.s["pe"]:
            for cb in self.pe_pending:
                cb(n)
            self.pe_pending = []
        return n

    def V(self, fn, waits=()):
        dv = self.s["dv"]
        for (sem, val) in waits:
            self.wait("vector", sem, val)
        self.wait("vector", dv, dv.n)
        return self.op("vector", fn, dv)

    def C(self, fn, waits=()):
        ac = self.s["ac"]
        for (sem, val) in waits:
            self.wait("scalar", sem, val)
        self.wait("scalar", ac, ac.n)
        return self.op("scalar", fn, ac)

    def G(self, fn, waits=()):
        gp = self.s["gp"]
        for (sem, val) in waits:
            self.wait("gpsimd", sem, val)
        self.wait("gpsimd", gp, gp.n)
        return self.op("gpsimd", fn, gp)

    def T(self, fn, waits=(), inc=False):
        for (sem, val) in waits:
            self.wait("tensor", sem, val)
        return self.op("tensor", fn, self.s["pe"] if inc else None)

    def barrier(self):
        for ch in getattr(self, "channels", []):
            if ch.open:
                ch.close()
        marks = {nm: self.s[nm].n for nm in self.s}
        for eng in ENGS:
            for nm, v in marks.items():
                self.wait(eng, self.s[nm], v)


class LoadChan:

    def __init__(self, kb, sems, eng="sync"):
        self.kb = kb
        self.sems = sems
        self.eng = eng
        self.cur = 0
        self.closed_total = [0] * len(sems)
        self.open = False

    def load(self, out_ap, in_ap):
        kb = self.kb
        s = self.sems[self.cur]
        if not self.open:
            kb.wait(self.eng, s, self.closed_total[self.cur])
            self.open = True
        kb.op(self.eng, lambda e, o=out_ap, i=in_ap: e.dma_start(out=o, in_=i),
              s, 16)
        return s, s.n

    def close(self):
        s = self.sems[self.cur]
        tot = s.n
        self.closed_total[self.cur] = tot
        self.cur = (self.cur + 1) % len(self.sems)
        self.open = False
        return s, tot

    def load1(self, out_ap, in_ap):
        assert not self.open
        sem, n = self.load(out_ap, in_ap)
        self.close()
        return sem, n

    def totals(self):
        assert not self.open
        return [(sm, sm.n) for sm in self.sems]


def build_nc(phases=4):
    nc = bass.Bass("TRN2", target_bir_lowering=False)

    x_in = nc.dram_tensor("x_in", [TLOC, D], F32, kind="ExternalInput").ap()
    out_d = nc.dram_tensor("out", [TLOC, D], F32, kind="ExternalOutput").ap()

    l0_wq = nc.dram_tensor("l0_wq", [NH, 128, D], F32, kind="ExternalInput").ap()
    l0_wk = nc.dram_tensor("l0_wk", [NH, 128, D], F32, kind="ExternalInput").ap()
    l0_wv = nc.dram_tensor("l0_wv", [NH, 128, D], F32, kind="ExternalInput").ap()
    l0_wo = nc.dram_tensor("l0_wo", [D, D], F32, kind="ExternalInput").ap()
    l0_bq = nc.dram_tensor("l0_bq", [128, 16], F32, kind="ExternalInput").ap()
    l0_bk = nc.dram_tensor("l0_bk", [128, 16], F32, kind="ExternalInput").ap()
    l0_bv = nc.dram_tensor("l0_bv", [128, 16], F32, kind="ExternalInput").ap()
    l0_bo = nc.dram_tensor("l0_bo", [1, D], F32, kind="ExternalInput").ap()

    f0_wq = nc.dram_tensor("f0_wq", [4, 128, D], F32, kind="ExternalInput").ap()
    f0_k1t = nc.dram_tensor("f0_k1t", [4, 64, 128], F32, kind="ExternalInput").ap()
    f0_k2t = nc.dram_tensor("f0_k2t", [4, 64, 128], F32, kind="ExternalInput").ap()
    f0_down = nc.dram_tensor("f0_down", [16384, D], F32, kind="ExternalInput").ap()
    f0_up = nc.dram_tensor("f0_up", [16384, D], F32, kind="ExternalInput").ap()

    l1_wqkv = nc.dram_tensor("l1_wqkv", [48, 128, D], F32, kind="ExternalInput").ap()
    l1_wo = nc.dram_tensor("l1_wo", [D, D], F32, kind="ExternalInput").ap()

    f1_w1t = nc.dram_tensor("f1_w1t", [NCH, 128, D], F32, kind="ExternalInput").ap()
    f1_w2t = nc.dram_tensor("f1_w2t", [NCH, 128, D], F32, kind="ExternalInput").ap()
    f1_w3t = nc.dram_tensor("f1_w3t", [FH, D], F32, kind="ExternalInput").ap()

    c_ident = nc.dram_tensor("c_ident", [128, 128], F32, kind="ExternalInput").ap()
    c_iota = nc.dram_tensor("c_iota", [128, 64], F32, kind="ExternalInput").ap()
    cs_rows = nc.dram_tensor("cs_rows", [NH, 1], I32, kind="ExternalInput").ap()

    dr_kt = nc.dram_tensor("dr_kt", [D, TLOC], F32, kind="Internal").ap()
    dr_kt_ag = nc.dram_tensor("dr_kt_ag", [2 * D, TLOC], F32, kind="Internal").ap()
    dr_v = nc.dram_tensor("dr_v", [TLOC, D], F32, kind="Internal").ap()
    dr_v_ag = nc.dram_tensor("dr_v_ag", [2 * TLOC, D], F32, kind="Internal").ap()
    dr_qt = nc.dram_tensor("dr_qt", [D, TLOC], F32, kind="Internal").ap()
    dr_attn = nc.dram_tensor("dr_attn", [D, TLOC], F32, kind="Internal").ap()
    dr_xn = nc.dram_tensor("dr_xn", [TLOC, D], F32, kind="Internal").ap()
    dr_kt1 = nc.dram_tensor("dr_kt1", [D, TLOC], F32, kind="Internal").ap()
    dr_kt1_ag = nc.dram_tensor("dr_kt1_ag", [2 * D, TLOC], F32, kind="Internal").ap()
    dr_cs = nc.dram_tensor("dr_cs", [NH, 2048], F32, kind="Internal").ap()
    dr_cs_ar = nc.dram_tensor("dr_cs_ar", [NH, 2048], F32, kind="Internal").ap()
    dr_h = nc.dram_tensor("dr_h", [FH, TLOC], F32, kind="Internal").ap()
    dr_w3q = nc.dram_tensor("dr_w3q", [FH, D], F32, kind="Internal").ap()

    ctx = ExitStack()
    with ctx:
        def sbuf(name, shape, dt=F32):
            return ctx.enter_context(nc.sbuf_tensor(name, shape, dt))

        def psum(name, shape, dt=F32):
            return ctx.enter_context(nc.psum_tensor(name, shape, dt))

        RES = sbuf("RES", [128, NT * D])
        A = sbuf("A", [128, 16384])
        B = sbuf("B", [128, 8192])
        SM = sbuf("SM", [128, 1536])
        WR1 = sbuf("WR1", [128, 4096])
        WR2 = sbuf("WR2", [128, 2048])
        SBIAS = sbuf("SBIAS", [1, D])
        IDENT = sbuf("IDENT", [128, 128])
        IOTA = sbuf("IOTA", [128, 64])
        KEYS = sbuf("KEYS", [128, 1024])
        ONES1 = sbuf("ONES1", [1, 128])
        ONES128 = sbuf("ONES128", [128, 1])
        U1 = sbuf("U1", [128, 8], U32)
        U2 = sbuf("U2", [128, 8], U32)
        UP = sbuf("UPOS", [128, 8], U32)
        IDXI = sbuf("IDXI", [128, 32], I32)
        CSR = sbuf("CSR", [NH, 1], I32)

        PB = psum("PB", [128, 2048])
        PQ = psum("PQ", [128, 1024])
        PT1 = psum("PT1", [128, 512])
        PT2 = psum("PT2", [128, 512])

        sems = {}
        for nm in ["pe", "dv", "ac", "gp", "ld0", "ld1", "ld2", "ld3",
                   "st", "st2", "gd0", "gd1", "cc", "lw0", "lw1"]:
            sems[nm] = Sem(ctx.enter_context(nc.semaphore(nm)), nm)

        kb = KB(nc, sems)
        LD = LoadChan(kb, [sems["ld0"], sems["ld1"], sems["ld2"], sems["ld3"]])
        ST = LoadChan(kb, [sems["st"], sems["st2"]])
        LDW = LoadChan(kb, [sems["lw0"], sems["lw1"]], eng="scalar")
        kb.channels = [LD, ST, LDW]

        def STW(dst, src):
            sm_, tt_ = ST.load1(dst, src)
            kb.wait("vector", sm_, tt_)
            kb.wait("scalar", sm_, tt_)
            return sm_, tt_

        BQ = SM[:, 0:16]
        BK = SM[:, 16:32]
        BV = SM[:, 32:48]
        STM = 48

        def smc(i, w=1):
            return SM[:, STM + i:STM + i + w]

        SCO = SM[:, 80:112]
        SCOR = SM[:, 112:144]
        IDXF = SM[:, 144:176]
        HID = SM[:, 176:208]
        GG = SM[:, 208:240]
        S1 = SM[:, 240:248]
        S2 = SM[:, 248:256]
        FS = SM[:, 256:264]
        I1F = SM[:, 264:272]
        I2F = SM[:, 272:280]
        PF = SM[:, 280:288]
        I64 = SM[:, 288:352]
        S64 = SM[:, 352:416]
        EQB = SM[:, 416:928]
        SIM1 = SM[:, 928:1056]
        SIM2 = SM[:, 1056:1184]
        GAC1 = SM[:, 1184:1185]
        GAC2 = SM[:, 1185:1186]
        GAC3 = SM[:, 1186:1187]
        GPART = SM[:, 1187:1188]
        GTOT = SM[0:1, 1190:1191]
        CSML = SM[:, 1200:1216]

        V, C, G, T, W = kb.V, kb.C, kb.G, kb.T, kb.wait
        pe, dv, ac, gp, cc = (sems["pe"], sems["dv"], sems["ac"], sems["gp"],
                              sems["cc"])
        rg = [[0, 1], [2, 3], [4, 5], [6, 7]]

        for t in range(NT):
            LD.load(RES[:, t * D:(t + 1) * D], x_in[t * 128:(t + 1) * 128, :])
        LD.load(IDENT[:], c_ident)
        LD.load(IOTA[:], c_iota)
        LD.load(BQ, l0_bq)
        LD.load(BK, l0_bk)
        LD.load(BV, l0_bv)
        LD.load(SBIAS[:], l0_bo)
        LD.load(CSR[:], cs_rows)
        for h in range(4):
            LD.load(KEYS[0:64, h * 128:(h + 1) * 128], f0_k1t[h])
            LD.load(KEYS[64:128, h * 128:(h + 1) * 128], f0_k2t[h])
        s_pro, n_pro = LD.close()
        for eng in ("vector", "tensor", "scalar", "gpsimd"):
            kb.wait(eng, s_pro, n_pro)
        V(lambda e: e.memset(ONES1[:], 1.0))
        V(lambda e: e.memset(ONES128[:], 1.0))

        def layernorm_tile(t, xn_ap):
            rt = RES[:, t * D:(t + 1) * D]
            m = smc(0)
            vv = smc(1)
            inv = smc(2)
            t1 = smc(3)
            V(lambda e: e.reduce_sum(m, rt, AX.X))
            V(lambda e: e.tensor_scalar_mul(m, m, 1.0 / D))
            nc_ = V(lambda e: e.tensor_scalar(xn_ap, rt, m, None, ALU.subtract))
            nsq = C(lambda e: e.activation(B[:, 6144:8192], xn_ap, ACT.Square,
                                           accum_out=vv), waits=[(dv, nc_)])
            V(lambda e: e.tensor_scalar(vv, vv, 1.0 / D, EPS_LN,
                                        ALU.mult, ALU.add), waits=[(ac, nsq)])
            nv = sems["dv"].n
            na = C(lambda e: e.activation(inv, vv, ACT.Sqrt), waits=[(dv, nv)])
            V(lambda e: e.reciprocal(inv, inv), waits=[(ac, na)])
            V(lambda e: e.tensor_tensor(t1, inv, inv, ALU.mult))
            V(lambda e: e.tensor_tensor(t1, t1, vv, ALU.mult))
            V(lambda e: e.tensor_scalar(t1, t1, -0.5, 1.5, ALU.mult, ALU.add))
            V(lambda e: e.tensor_tensor(inv, inv, t1, ALU.mult))
            return V(lambda e: e.tensor_scalar_mul(xn_ap, xn_ap, inv))

        def ln_and_transpose(store_xn=None):
            tile_pe = {}
            bank_evac = {0: 0, 1: 0}
            for t in range(NT):
                xn = B[:, (t % 2) * D:((t % 2) + 1) * D]
                if (t - 2) in tile_pe:
                    kb.wait("vector", pe, tile_pe[t - 2])
                nln = layernorm_tile(t, xn)
                if store_xn is not None:
                    kb.wait("sync", dv, nln)
                    STW(store_xn[t * 128:(t + 1) * 128, :], xn)
                for c in range(16):
                    bk = c % 2
                    pb = PT1 if bk == 0 else PT2
                    waits = [(dv, nln)] if c < 2 else []
                    if bank_evac[bk]:
                        waits = waits + [(dv, bank_evac[bk])]
                    n = T(lambda e, s=xn[:, c * 128:(c + 1) * 128], p=pb:
                          e.transpose(p[:, 0:128], s, IDENT[:]),
                          waits=waits, inc=True)
                    dst = A[:, c * 1024 + t * 128: c * 1024 + (t + 1) * 128]
                    bank_evac[bk] = V(lambda e, p=pb, d=dst:
                                      e.tensor_copy(d, p[:, 0:128]),
                                      waits=[(pe, n)])
                tile_pe[t] = sems["pe"].n
            return sems["dv"].n

        class WRing:
            def __init__(self, area, slot_cols, nslots):
                self.area = area
                self.w = slot_cols
                self.n = nslots
                self.i = 0
                self.last_use = [0] * nslots

            def __init2__(self):
                pass

            def get(self, src_ap, rows=128, ch=None):
                ch = ch or LD
                sl = self.i % self.n
                self.i += 1
                ap = self.area[0:rows, sl * self.w:sl * self.w + self.w]
                kb.wait(ch.eng, pe, self.last_use[sl])
                sem, n = ch.load1(ap, src_ap)
                return ap, sl, (sem, n)

            def used(self, sl):
                def cb(v, sl=sl):
                    self.last_use[sl] = v
                kb.pe_pending.append(cb)

        wr1 = WRing(WR1, 2048, 2)
        wr2 = WRing(WR2, 512, 4)

        def proj_fm(w_dram, h, psum_ap):
            wt, sl, (sm, ln) = wr1.get(w_dram[h], ch=LDW)
            for half in range(2):
                for di in range(16):
                    rhs = A[:, di * 1024 + half * 512:
                            di * 1024 + half * 512 + 512]
                    T(lambda e, p=psum_ap, w=wt[:, di * 128:(di + 1) * 128],
                      r=rhs, st=(di == 0), sp=(di == 15), hh=half:
                      e.matmul(p[:, hh * 512:(hh + 1) * 512], w, r,
                               start=st, stop=sp),
                      waits=[(sm, ln)] if (di == 0 and half == 0) else [],
                      inc=(di == 15))
            wr1.used(sl)
            return sems["pe"].n

        def attn_scores(qt_sb, kt_sb, qt, waits0=()):
            nsc = None
            for kc in range(4):
                nsc = T(lambda e, q=qt_sb[:, qt * 128:(qt + 1) * 128],
                        r=kt_sb[:, kc * 512:(kc + 1) * 512], kc=kc:
                        e.matmul(PB[:, kc * 512:(kc + 1) * 512], q, r,
                                 start=True, stop=True),
                        waits=waits0 if kc == 0 else (), inc=(kc == 3))
            return nsc

        def attn_softmax(probs, nsc, par):
            nm = smc(4 + 3 * par)
            rs = smc(5 + 3 * par)
            ri = smc(6 + 3 * par)
            V(lambda e, nm=nm: e.reduce_max(nm, PB[:], AX.X, negate=True),
              waits=[(pe, nsc)])
            nvm = V(lambda e, nm=nm: e.tensor_scalar_mul(nm, nm, RSC))
            ne = C(lambda e, nm=nm, rs=rs, p=probs:
                   e.activation(p, PB[:], ACT.Exp, bias=nm,
                                scale=RSC, accum_out=rs),
                   waits=[(dv, nvm)])
            nri = V(lambda e, ri=ri, rs=rs: e.reciprocal(ri, rs),
                    waits=[(ac, ne)])
            nsm = C(lambda e, ri=ri, p=probs:
                    e.activation(p, p, ACT.Copy, scale=ri),
                    waits=[(dv, nri)])
            return ne, nsm

        gsteps = []
        if phases >= 4:
            gpar = [0]
            gmarkp = {0: 0, 1: 0}
            gfirst = {}

            def g_step(src, gac):
                def run():
                    p = gpar[0] % 2
                    gpar[0] += 1
                    wdt = B[:, p * 2048:(p + 1) * 2048]
                    kb.wait("sync", dv, gmarkp[p])
                    s_w, n_w = LD.load1(wdt, src)
                    V(lambda e: e.reduce_sum(GPART, wdt, AX.X,
                                             apply_absolute_value=True),
                      waits=[(s_w, n_w)])
                    if gac.offset not in gfirst:
                        gfirst[gac.offset] = True
                        nr = V(lambda e: e.tensor_copy(gac, GPART))
                    else:
                        nr = V(lambda e: e.tensor_tensor(gac, gac, GPART,
                                                         ALU.add))
                    gmarkp[p] = nr
                return run

            for c_ in range(NCH):
                gsteps.append(g_step(f1_w1t[c_], GAC1))
            for c_ in range(NCH):
                gsteps.append(g_step(f1_w2t[c_], GAC2))
            for c_ in range(NCH):
                gsteps.append(g_step(f1_w3t[c_ * 128:(c_ + 1) * 128, :], GAC3))

        nxt = ln_and_transpose()
        kb.wait("tensor", dv, nxt)
        if phases >= 4:
            gmarkp[0] = nxt
            gmarkp[1] = nxt

        psum_marks = {0: 0, 1: 0, 2: 0}
        for h in range(NH):
            for _ in range(9):
                if gsteps:
                    gsteps.pop(0)()
            kb.wait("tensor", dv, psum_marks[0])
            nq = proj_fm(l0_wq, h, PQ)
            qsb = B[:, 4096:5120]
            nv = V(lambda e, b=BQ[:, h:h + 1], d=qsb:
                   e.tensor_scalar_add(d, PQ[:], b), waits=[(pe, nq)])
            psum_marks[0] = nv
            kb.wait("sync", dv, nv)
            STW(dr_qt[h * 128:(h + 1) * 128, :], qsb)
            kb.wait("tensor", dv, psum_marks[1])
            nk = proj_fm(l0_wk, h, PB[:, 0:1024])
            ksb = B[:, 5120:6144]
            nv = V(lambda e, b=BK[:, h:h + 1], d=ksb:
                   e.tensor_scalar_add(d, PB[:, 0:1024], b), waits=[(pe, nk)])
            psum_marks[1] = nv
            kb.wait("sync", dv, nv)
            STW(dr_kt[h * 128:(h + 1) * 128, :], ksb)
            kb.wait("tensor", dv, psum_marks[2])
            nvm = proj_fm(l0_wv, h, PB[:, 1024:2048])
            vsb = B[:, 6272:7296]
            nv = V(lambda e, b=BV[:, h:h + 1], d=vsb:
                   e.tensor_scalar_add(d, PB[:, 1024:2048], b),
                   waits=[(pe, nvm)])
            psum_marks[2] = nv
            stage = B[:, 6144:6272]
            prev_ev = 0
            for t in range(NT):
                pb = PT1 if t % 2 == 0 else PT2
                waits = [(dv, nv)]
                if prev_ev:
                    waits.append((dv, prev_ev))
                n = T(lambda e, s=vsb[:, t * 128:(t + 1) * 128], p=pb:
                      e.transpose(p[:, 0:128], s, IDENT[:]),
                      waits=waits, inc=True)
                nve = V(lambda e, p=pb, d=stage: e.tensor_copy(d, p[:, 0:128]),
                        waits=[(pe, n)])
                kb.wait("sync", dv, nve)
                s_s, n_s = ST.load1(
                    dr_v[t * 128:(t + 1) * 128, h * 128:(h + 1) * 128], stage)
                kb.wait("vector", s_s, n_s)
                prev_ev = nve
            kb.wait("tensor", dv, nv)
        while gsteps:
            gsteps.pop(0)()
        n_gfin = 0
        if phases >= 4:
            T1P = smc(10)
            T1N = smc(12)
            T2P = smc(14)
            T2N = smc(16)
            T3P = smc(18)
            T3N = smc(20)

            def g_fin(gac, tp_ap, tn_ap):
                ng = sems["dv"].n
                ntt = T(lambda e, g=gac: e.matmul(PT2[0:1, 0:1], g, ONES128[:],
                                                  start=True, stop=True),
                        waits=[(dv, ng)], inc=True)
                V(lambda e: e.tensor_copy(GTOT, PT2[0:1, 0:1]),
                  waits=[(pe, ntt)])
                V(lambda e: e.tensor_scalar(GTOT, GTOT, 0.5 / (FH * D),
                                            0.5e-8, ALU.mult, ALU.add))
                ngt = sems["dv"].n
                nrep = T(lambda e: e.matmul(PT2[:, 0:1], ONES1[:], GTOT,
                                            start=True, stop=True),
                         waits=[(dv, ngt)], inc=True)
                V(lambda e, d=tp_ap: e.tensor_copy(d, PT2[:, 0:1]),
                  waits=[(pe, nrep)])
                return V(lambda e, d=tn_ap, sr=tp_ap:
                         e.tensor_scalar_mul(d, sr, -1.0))

            g_fin(GAC1, T1P, T1N)
            g_fin(GAC2, T2P, T2N)
            n_gfin = g_fin(GAC3, T3P, T3N)

        for (sm_, tt_) in ST.totals():
            kb.wait("gpsimd", sm_, tt_)
        n_cc_k = kb.op("gpsimd", lambda e: e.collective_compute(
            "AllGather", ALU.bypass, replica_groups=rg,
            ins=[dr_kt], outs=[dr_kt_ag]), cc)
        n_cc_v = kb.op("gpsimd", lambda e: e.collective_compute(
            "AllGather", ALU.bypass, replica_groups=rg,
            ins=[dr_v], outs=[dr_v_ag]), cc)

        kb.wait("sync", cc, n_cc_v)
        for h in range(NH):
            kt_sb = A[:, 0:2048]
            v_sb = A[:, 2048:4096]
            qt_sb = A[:, 4096:5120]
            probs = A[:, 5120:7168]
            probs2 = A[:, 9216:11264]
            pt_sb = A[:, 7168:9216]
            att_h = B[:, 0:1024]
            kb.wait("sync", pe, sems["pe"].n)
            kb.wait("sync", dv, sems["dv"].n)
            kb.wait("sync", ac, sems["ac"].n)
            LD.load(kt_sb[:, 0:1024], dr_kt_ag[h * 128:(h + 1) * 128, :])
            LD.load(kt_sb[:, 1024:2048],
                    dr_kt_ag[D + h * 128:D + (h + 1) * 128, :])
            LD.load(v_sb.rearrange("p (kt d) -> p kt d", d=128),
                    dr_v_ag[:, h * 128:(h + 1) * 128]
                    .rearrange("(kt p) d -> p kt d", p=128))
            LD.load(qt_sb[:], dr_qt[h * 128:(h + 1) * 128, :])
            s_h, n_h = LD.close()
            for eng in ("tensor", "vector", "scalar"):
                kb.wait(eng, s_h, n_h)
            nexp = {}
            nsm = {}
            for qt in range(NT + 1):
                if qt < NT:
                    par = qt % 2
                    pr = probs if par == 0 else probs2
                    w0 = [(ac, nexp[qt - 1])] if qt >= 1 else []
                    nsc = attn_scores(qt_sb, kt_sb, qt, waits0=w0)
                    nexp[qt], nsm[qt] = attn_softmax(pr, nsc, par)
                if qt == 0:
                    continue
                qp = qt - 1
                prp = probs if qp % 2 == 0 else probs2
                lastpt = []
                bank_evac = {0: 0, 1: 0}
                for grp in range(4):
                    bk = grp % 2
                    pb = PT1 if bk == 0 else PT2
                    waits = [(ac, nsm[qp])] if grp < 2 else []
                    if bank_evac[bk]:
                        waits = waits + [(dv, bank_evac[bk])]
                    n = None
                    for k4 in range(4):
                        kc16 = grp * 4 + k4
                        n = T(lambda e, k4=k4,
                              sp=prp[:, kc16 * 128:(kc16 + 1) * 128],
                              p=pb: e.transpose(
                                  p[:, k4 * 128:(k4 + 1) * 128], sp,
                                  IDENT[:]),
                              waits=waits if k4 == 0 else [], inc=(k4 == 3))
                    nve = V(lambda e, p=pb, grp=grp:
                            e.tensor_copy(pt_sb[:, grp * 512:(grp + 1) * 512],
                                          p[:, 0:512]), waits=[(pe, n)])
                    bank_evac[bk] = nve
                    lastpt.append(nve)
                nav = None
                for kc16 in range(16):
                    nav = T(lambda e, v=v_sb[:, kc16 * 128:(kc16 + 1) * 128],
                            r=pt_sb[:, kc16 * 128:(kc16 + 1) * 128],
                            st=(kc16 == 0), sp=(kc16 == 15):
                            e.matmul(PQ[:, 0:128], v, r, start=st, stop=sp),
                            waits=[(dv, lastpt[3])] if kc16 == 0 else [],
                            inc=(kc16 == 15))
                nve = V(lambda e, d=att_h[:, qp * 128:(qp + 1) * 128]:
                        e.tensor_copy(d, PQ[:, 0:128]), waits=[(pe, nav)])
                kb.wait("tensor", dv, nve)
            kb.wait("sync", dv, sems["dv"].n)
            STW(dr_attn[h * 128:(h + 1) * 128, :], att_h)
        for (sm_, tt_) in ST.totals():
            kb.wait("sync", sm_, tt_)
        atr = WRing(B, 128, 16)
        for t in range(NT):
            for blk in range(4):
                T(lambda e, blk=blk:
                  e.matmul(PB[:, blk * 512:(blk + 1) * 512], ONES1[:],
                           SBIAS[:, blk * 512:(blk + 1) * 512],
                           start=True, stop=False))
            for hc in range(16):
                at, sl, (sm, ln) = atr.get(
                    dr_attn[hc * 128:(hc + 1) * 128, t * 128:(t + 1) * 128])
                for blk in range(4):
                    wt, sl2, (sm2, ln2) = wr2.get(
                        l0_wo[hc * 128:(hc + 1) * 128,
                              blk * 512:(blk + 1) * 512])
                    T(lambda e, a=at, w=wt, blk=blk, sp=(hc == 15):
                      e.matmul(PB[:, blk * 512:(blk + 1) * 512], a, w,
                               start=False, stop=sp),
                      waits=[(sm, ln), (sm2, ln2)],
                      inc=(blk == 3))
                    wr2.used(sl2)
                atr.used(sl)
            npj = sems["pe"].n
            nve = V(lambda e, t=t: e.tensor_tensor(
                RES[:, t * D:(t + 1) * D], PB[:],
                RES[:, t * D:(t + 1) * D], ALU.add), waits=[(pe, npj)])
            kb.wait("tensor", dv, nve)

        if phases >= 2:
            kb.barrier()
            nxt = ln_and_transpose(store_xn=dr_xn)
            kb.wait("tensor", dv, nxt)
            for hq in range(4):
                nq = proj_fm(f0_wq, hq, PQ)
                nv = V(lambda e, d=B[:, hq * 1024:(hq + 1) * 1024]:
                       e.tensor_copy(d, PQ[:]), waits=[(pe, nq)])
                kb.wait("tensor", dv, nv)
            sim_ev = {0: 0, 1: 0}
            for t in range(NT):
                xn2 = B[:, 4096:6144]
                outacc = B[:, 6144:8192]
                kb.wait("sync", dv, sems["dv"].n)
                kb.wait("sync", gp, sems["gp"].n)
                LD.load(xn2, dr_xn[t * 128:(t + 1) * 128, :])
                s_t, n_t = LD.close()
                kb.wait("vector", s_t, n_t)
                for h in range(4):
                    q1 = B[0:64, h * 1024 + t * 128:h * 1024 + (t + 1) * 128]
                    q2 = B[64:128, h * 1024 + t * 128:h * 1024 + (t + 1) * 128]
                    w1 = [(dv, sim_ev[0])] if sim_ev[0] else []
                    n1 = T(lambda e, q=q1,
                           k=KEYS[0:64, h * 128:(h + 1) * 128]:
                           e.matmul(PT1[:, 0:128], q, k, start=True,
                                    stop=True), waits=w1, inc=True)
                    nv1 = V(lambda e: e.tensor_copy(SIM1, PT1[:, 0:128]),
                            waits=[(pe, n1)])
                    sim_ev[0] = nv1
                    w2 = [(dv, sim_ev[1])] if sim_ev[1] else []
                    n2 = T(lambda e, q=q2,
                           k=KEYS[64:128, h * 128:(h + 1) * 128]:
                           e.matmul(PT2[:, 0:128], q, k, start=True,
                                    stop=True), waits=w2, inc=True)
                    nv2 = V(lambda e: e.tensor_copy(SIM2, PT2[:, 0:128]),
                            waits=[(pe, n2)])
                    sim_ev[1] = nv2
                    V(lambda e: e.max(out=S1, in_=SIM1))
                    V(lambda e: e.max_index(out=U1[:], in_max=S1, in_values=SIM1))
                    V(lambda e: e.max(out=S2, in_=SIM2))
                    V(lambda e: e.max_index(out=U2[:], in_max=S2, in_values=SIM2))
                    V(lambda e: e.tensor_copy(I1F, U1[:]))
                    V(lambda e: e.tensor_copy(I2F, U2[:]))
                    V(lambda e: e.tensor_tensor(
                        S64.rearrange("p (i j) -> p i j", j=8),
                        S1.unsqueeze(2).to_broadcast([128, 8, 8]),
                        S2.unsqueeze(1).to_broadcast([128, 8, 8]), ALU.add))
                    V(lambda e: e.tensor_scalar_mul(I1F, I1F, 128.0))
                    V(lambda e: e.tensor_tensor(
                        I64.rearrange("p (i j) -> p i j", j=8),
                        I1F.unsqueeze(2).to_broadcast([128, 8, 8]),
                        I2F.unsqueeze(1).to_broadcast([128, 8, 8]), ALU.add))
                    V(lambda e: e.max(out=FS, in_=S64))
                    V(lambda e: e.max_index(out=UP[:], in_max=FS, in_values=S64))
                    V(lambda e: e.tensor_copy(PF, UP[:]))
                    V(lambda e, h=h: e.tensor_copy(SCO[:, h * 8:(h + 1) * 8],
                                                   FS))
                    V(lambda e: e.tensor_tensor(
                        EQB.rearrange("p (s j) -> p s j", j=64),
                        PF.unsqueeze(2).to_broadcast([128, 8, 64]),
                        IOTA[:].unsqueeze(1).to_broadcast([128, 8, 64]),
                        ALU.is_equal))
                    V(lambda e: e.tensor_tensor(
                        EQB.rearrange("p (s j) -> p s j", j=64),
                        EQB.rearrange("p (s j) -> p s j", j=64),
                        I64.unsqueeze(1).to_broadcast([128, 8, 64]), ALU.mult))
                    V(lambda e, h=h: e.tensor_reduce(
                        IDXF[:, h * 8:(h + 1) * 8],
                        EQB.rearrange("p (s j) -> p s j", j=64),
                        AX.X, ALU.add))
                kb.wait("vector", sems["gd0"], sems["gd0"].n)
                kb.wait("vector", sems["gd1"], sems["gd1"].n)
                nidx = V(lambda e: e.tensor_copy(IDXI[:], IDXF))
                nrel = V(lambda e: e.tensor_scalar_max(SCOR, SCO, 0.0))
                gmark = {}
                for g in range(16):
                    par = g % 2
                    gdsem = sems["gd0"] if par == 0 else sems["gd1"]
                    kb.wait("gpsimd", dv, nidx)
                    if g >= 2:
                        kb.wait("gpsimd", dv, gmark[g - 2])
                    dslot = [A[:, par * 4096 + j2 * 2048:
                               par * 4096 + (j2 + 1) * 2048] for j2 in range(2)]
                    uslot = [A[:, 8192 + par * 4096 + j2 * 2048:
                               8192 + par * 4096 + (j2 + 1) * 2048]
                             for j2 in range(2)]
                    for j2 in range(2):
                        j = g * 2 + j2
                        kb.op("gpsimd",
                              lambda e, j=j, d=dslot[j2]: e.indirect_dma_start(
                                  out=d, out_offset=None, in_=f0_down,
                                  in_offset=bass.IndirectOffsetOnAxis(
                                      IDXI[:, j:j + 1], 0)), gdsem, 16)
                        kb.op("gpsimd",
                              lambda e, j=j, u=uslot[j2]: e.indirect_dma_start(
                                  out=u, out_offset=None, in_=f0_up,
                                  in_offset=bass.IndirectOffsetOnAxis(
                                      IDXI[:, j:j + 1], 0)), gdsem, 16)
                    ngd = gdsem.n
                    for j2 in range(2):
                        j = g * 2 + j2
                        nmul = kb.G(lambda e, d=dslot[j2]: e.tensor_tensor(
                            d, d, xn2, ALU.mult),
                            waits=[(gdsem, ngd), (s_t, n_t)])
                        C(lambda e, d=dslot[j2], j=j: e.activation(
                            d, d, ACT.Copy, accum_out=HID[:, j:j + 1]),
                          waits=[(gp, nmul)])
                    nsl = C(lambda e, g=g: e.activation(
                        GG[:, g * 2:(g + 1) * 2],
                        HID[:, g * 2:(g + 1) * 2], ACT.Silu))
                    ngg = V(lambda e, g=g: e.tensor_tensor(
                        GG[:, g * 2:(g + 1) * 2], GG[:, g * 2:(g + 1) * 2],
                        SCOR[:, g * 2:(g + 1) * 2], ALU.mult),
                        waits=[(ac, nsl), (dv, nrel)])
                    for j2 in range(2):
                        j = g * 2 + j2
                        up = uslot[j2]
                        gcol = GG[:, j:j + 1]
                        if j == 0:
                            V(lambda e, u=up, gc=gcol: e.tensor_scalar(
                                outacc, u, gc, None, ALU.mult))
                        else:
                            V(lambda e, u=up, gc=gcol: e.scalar_tensor_tensor(
                                outacc, u, gc, outacc, ALU.mult, ALU.add))
                    gmark[g] = sems["dv"].n
                V(lambda e, t=t: e.tensor_tensor(
                    RES[:, t * D:(t + 1) * D], outacc,
                    RES[:, t * D:(t + 1) * D], ALU.add))

        if phases >= 3:
            kb.barrier()
            nxt = ln_and_transpose()
            kb.wait("tensor", dv, nxt)
            for h in range(NH):
                nq = proj_fm(l1_wqkv, h, PQ)
                qsb = B[:, 0:1024]
                nv = V(lambda e, d=qsb: e.tensor_copy(d, PQ[:]),
                       waits=[(pe, nq)])
                kb.wait("sync", dv, nv)
                STW(dr_qt[h * 128:(h + 1) * 128, :], qsb)
                kb.wait("tensor", dv, nv)
                nk = proj_fm(l1_wqkv, NH + h, PQ)
                ksb = B[:, 1024:2048]
                nv = V(lambda e, d=ksb: e.tensor_copy(d, PQ[:]),
                       waits=[(pe, nk)])
                kb.wait("sync", dv, nv)
                STW(dr_kt1[h * 128:(h + 1) * 128, :], ksb)
                kb.wait("tensor", dv, nv)
            for t in range(NT):
                for dc in range(16):
                    wt, sl, (sm, ln) = wr1.get(l1_wqkv[32 + dc], ch=LDW)
                    for di in range(16):
                        T(lambda e, w=wt[:, di * 128:(di + 1) * 128], dc=dc,
                          x=A[:, di * 1024 + t * 128:
                              di * 1024 + (t + 1) * 128],
                          st=(di == 0), sp=(di == 15):
                          e.matmul(PB[:, dc * 128:(dc + 1) * 128], x, w,
                                   start=st, stop=sp),
                          waits=[(sm, ln)] if di == 0 else [],
                          inc=(di == 15))
                    wr1.used(sl)
                nvt = sems["pe"].n
                vsb = B[:, 2048:4096]
                nv = V(lambda e, d=vsb: e.tensor_copy(d, PB[:]),
                       waits=[(pe, nvt)])
                kb.wait("sync", dv, nv)
                STW(dr_v[t * 128:(t + 1) * 128, :], vsb)
                kb.wait("tensor", dv, nv)
            for (sm_, tt_) in ST.totals():
                kb.wait("gpsimd", sm_, tt_)
            n_cc_k1 = kb.op("gpsimd", lambda e: e.collective_compute(
                "AllGather", ALU.bypass, replica_groups=rg,
                ins=[dr_kt1], outs=[dr_kt1_ag]), cc)
            kb.wait("sync", cc, n_cc_k1)
            rows_acc = B[:, 0:2048]
            for h in range(NH):
                kt_sb = A[:, 0:2048]
                qt_sb = A[:, 4096:5120]
                probs = A[:, 5120:7168]
                probs2 = A[:, 9216:11264]
                kb.wait("sync", pe, sems["pe"].n)
                kb.wait("sync", dv, sems["dv"].n)
                kb.wait("sync", ac, sems["ac"].n)
                LD.load(kt_sb[:, 0:1024], dr_kt1_ag[h * 128:(h + 1) * 128, :])
                LD.load(kt_sb[:, 1024:2048],
                        dr_kt1_ag[D + h * 128:D + (h + 1) * 128, :])
                LD.load(qt_sb[:], dr_qt[h * 128:(h + 1) * 128, :])
                s_h, n_h = LD.close()
                for eng in ("tensor", "vector", "scalar"):
                    kb.wait(eng, s_h, n_h)
                nacc = 0
                nexp = {}
                nsm = {}
                for qt in range(NT + 1):
                    if qt < NT:
                        par = qt % 2
                        pr = probs if par == 0 else probs2
                        w0 = [(ac, nexp[qt - 1])] if qt >= 1 else []
                        nsc = attn_scores(qt_sb, kt_sb, qt, waits0=w0)
                        nexp[qt], nsm[qt] = attn_softmax(pr, nsc, par)
                    if qt == 0:
                        continue
                    qp = qt - 1
                    prp = probs if qp % 2 == 0 else probs2
                    if qp == 0:
                        nacc = V(lambda e, p=prp: e.tensor_copy(rows_acc, p),
                                 waits=[(ac, nsm[qp])])
                    else:
                        nacc = V(lambda e, p=prp: e.tensor_tensor(
                            rows_acc, rows_acc, p, ALU.add),
                            waits=[(ac, nsm[qp])])
                for kc in range(4):
                    ncs = T(lambda e, kc=kc:
                            e.matmul(PT1[0:1, 0:512], ONES128[:],
                                     rows_acc[:, kc * 512:(kc + 1) * 512],
                                     start=True, stop=True),
                            waits=[(dv, nacc)] if kc == 0 else [], inc=True)
                    nve = V(lambda e, kc=kc: e.tensor_copy(
                        SBIAS[0:1, kc * 512:(kc + 1) * 512],
                        PT1[0:1, 0:512]), waits=[(pe, ncs)])
                    kb.wait("tensor", dv, nve)
                kb.wait("sync", dv, sems["dv"].n)
                STW(dr_cs[h:h + 1, :], SBIAS[0:1, :])
            for (sm_, tt_) in ST.totals():
                kb.wait("gpsimd", sm_, tt_)
            n_cc_cs = kb.op("gpsimd", lambda e: e.collective_compute(
                "AllReduce", ALU.add, replica_groups=rg,
                ins=[dr_cs], outs=[dr_cs_ar]), cc)
            csl = B[:, 4096:5120]
            kb.wait("gpsimd", cc, n_cc_cs)
            kb.wait("gpsimd", sems["gd0"], sems["gd0"].n)
            ngsel_sem = sems["gd0"]
            ngsel = kb.op("gpsimd", lambda e: e.indirect_dma_start(
                out=csl[0:16, :], out_offset=None,
                in_=dr_cs_ar.rearrange("h (s t) -> (h s) t", s=2),
                in_offset=bass.IndirectOffsetOnAxis(CSR[:], 0)),
                ngsel_sem, 16)
            for t in range(NT):
                v_sb = B[:, 0:2048]
                kb.wait("sync", pe, sems["pe"].n)
                kb.wait("sync", dv, sems["dv"].n)
                LD.load(v_sb, dr_v[t * 128:(t + 1) * 128, :])
                s_v, n_v = LD.close()
                nct = T(lambda e, s=csl[0:16, t * 128:(t + 1) * 128]:
                        e.transpose(PT1[:, 0:16], s, IDENT[0:16, 0:16]),
                        waits=[(ngsel_sem, ngsel)], inc=True)
                nve = V(lambda e: e.tensor_copy(CSML, PT1[:, 0:16]),
                        waits=[(pe, nct)])
                nsv = V(lambda e: e.tensor_tensor(
                    v_sb.rearrange("p (h d) -> p h d", d=128),
                    v_sb.rearrange("p (h d) -> p h d", d=128),
                    CSML.unsqueeze(2).to_broadcast([128, 16, 128]),
                    ALU.mult), waits=[(s_v, n_v)])
                bank_evac = {0: 0, 1: 0}
                for hc in range(16):
                    bk = hc % 2
                    pb = PT1 if bk == 0 else PT2
                    waits = [(dv, nsv)] if hc < 2 else []
                    if bank_evac[bk]:
                        waits = waits + [(dv, bank_evac[bk])]
                    n = T(lambda e, s=v_sb[:, hc * 128:(hc + 1) * 128], p=pb:
                          e.transpose(p[:, 0:128], s, IDENT[:]),
                          waits=waits, inc=True)
                    bank_evac[bk] = V(
                        lambda e, p=pb, d=A[:, hc * 128:(hc + 1) * 128]:
                        e.tensor_copy(d, p[:, 0:128]), waits=[(pe, n)])
                nvals = sems["dv"].n
                kb.wait("tensor", dv, nvals)
                for hc in range(16):
                    for blk in range(4):
                        wt, sl2, (sm2, ln2) = wr2.get(
                            l1_wo[hc * 128:(hc + 1) * 128,
                                  blk * 512:(blk + 1) * 512])
                        T(lambda e, a=A[:, hc * 128:(hc + 1) * 128], w=wt,
                          blk=blk, st=(hc == 0), sp=(hc == 15):
                          e.matmul(PB[:, blk * 512:(blk + 1) * 512], a, w,
                                   start=st, stop=sp),
                          waits=[(sm2, ln2)], inc=(blk == 3))
                        wr2.used(sl2)
                npj = sems["pe"].n
                nve = V(lambda e, t=t: e.tensor_tensor(
                    RES[:, t * D:(t + 1) * D], PB[:],
                    RES[:, t * D:(t + 1) * D], ALU.add), waits=[(pe, npj)])
                kb.wait("tensor", dv, nve)

        if phases >= 4:
            kb.barrier()
            nxt = ln_and_transpose()
            kb.wait("tensor", dv, nxt)

            def quantize(dst_ap, src_ap, tp, tn, tmp_ap, wl):
                V(lambda e: e.tensor_scalar(dst_ap, src_ap, tp, None,
                                            ALU.is_gt), waits=wl)
                V(lambda e: e.tensor_scalar(tmp_ap, src_ap, tn, None,
                                            ALU.is_lt))
                return V(lambda e: e.tensor_tensor(dst_ap, dst_ap, tmp_ap,
                                                   ALU.subtract))

            QWa = B[:, 4096:6144]
            QWb = B[:, 6144:8192]
            QTMP = B[:, 0:2048]
            qw_mark = {0: 0, 1: 0}
            for c in range(NCH):
                for wi, (wsrc, psm, tp, tn) in enumerate(
                        ((f1_w1t, PQ, T1P, T1N), (f1_w2t, PB, T2P, T2N))):
                    wt, sl, (sm, ln) = wr1.get(wsrc[c], ch=LDW)
                    par = wi
                    qw = QWa if par == 0 else QWb
                    kb.wait("vector", pe, qw_mark[par])
                    nq = quantize(qw, wt, tp, tn, QTMP,
                                  [(sm, ln), (dv, n_gfin)])
                    for di in range(16):
                        for half in range(2):
                            rhs = A[:, di * 1024 + half * 512:
                                    di * 1024 + half * 512 + 512]
                            T(lambda e, p=psm, di=di,
                              w=qw[:, di * 128:(di + 1) * 128], r=rhs,
                              st=(di == 0), sp=(di == 15), hh=half:
                              e.matmul(p[:, hh * 512:(hh + 1) * 512], w, r,
                                       start=st, stop=sp),
                              waits=[(dv, nq)] if di == 0 else [],
                              inc=(half == 1))
                    def _cb(v, par=par):
                        qw_mark[par] = v
                    kb.pe_pending.append(_cb)
                    wr1.used(sl)
                w3raw = QTMP
                w3tmp = B[:, 2048:3072 + 1024][:, 0:1024]
                kb.wait("sync", dv, sems["dv"].n)
                sw3, lw3 = LD.load1(w3raw, f1_w3t[c * 128:(c + 1) * 128, :])
                V(lambda e, w=w3raw: e.tensor_scalar(
                    B[:, 2048:4096], w, T3P, None, ALU.is_gt),
                  waits=[(sw3, lw3), (dv, n_gfin)])
                V(lambda e, w=w3raw: e.tensor_scalar(w, w, T3N, None,
                                                     ALU.is_lt))
                nw3q = V(lambda e, w=w3raw: e.tensor_tensor(
                    w, B[:, 2048:4096], w, ALU.subtract))
                kb.wait("sync", dv, nw3q)
                STW(dr_w3q[c * 128:(c + 1) * 128, :], w3raw)
                npb = sems["pe"].n
                hsb = B[:, 2048:3072]
                na = C(lambda e: e.activation(hsb, PQ[:], ACT.Sigmoid),
                       waits=[(pe, npb), (dv, nw3q)])
                nh = V(lambda e: e.tensor_tensor(hsb, hsb, PB[:, 0:1024],
                                                 ALU.mult), waits=[(ac, na)])
                kb.wait("sync", dv, nh)
                STW(dr_h[c * 128:(c + 1) * 128, :], hsb)
                kb.wait("tensor", dv, nh)
                kb.wait("tensor", ac, na)
            for (sm_, tt_) in ST.totals():
                kb.wait("sync", sm_, tt_)

            htr = WRing(B, 128, 16)
            w3mark = {0: 0, 1: 0}
            for tp_i in range(4):
                t0, t1 = 2 * tp_i, 2 * tp_i + 1
                for c in range(NCH):
                    p3 = c % 2
                    w3raw = B[:, 2048 + p3 * 2048:2048 + (p3 + 1) * 2048]
                    kb.wait("scalar", pe, w3mark[p3])
                    sm3, ln3 = LDW.load1(w3raw,
                                         dr_w3q[c * 128:(c + 1) * 128, :])
                    nq3 = 0
                    hA, slA, (smA, lnA) = htr.get(
                        dr_h[c * 128:(c + 1) * 128, t0 * 128:(t0 + 1) * 128])
                    hB, slB, (smB, lnB) = htr.get(
                        dr_h[c * 128:(c + 1) * 128, t1 * 128:(t1 + 1) * 128])
                    for blk in range(4):
                        T(lambda e, a=hA,
                          w=w3raw[:, blk * 512:(blk + 1) * 512],
                          blk=blk, st=(c == 0), sp=(c == NCH - 1):
                          e.matmul(PB[:, blk * 512:(blk + 1) * 512], a, w,
                                   start=st, stop=sp),
                          waits=[(smA, lnA), (sm3, ln3)] if blk == 0
                          else [], inc=True)
                        pqs = (PQ[:, 0:512], PQ[:, 512:1024],
                               PT1[:, 0:512], PT2[:, 0:512])[blk]
                        T(lambda e, a=hB,
                          w=w3raw[:, blk * 512:(blk + 1) * 512],
                          p=pqs, st=(c == 0), sp=(c == NCH - 1):
                          e.matmul(p, a, w, start=st, stop=sp),
                          waits=[(smB, lnB)] if blk == 0 else [],
                          inc=(blk == 3))
                    def _cb3(v, p3=p3):
                        w3mark[p3] = v
                    kb.pe_pending.append(_cb3)
                    htr.used(slA)
                    htr.used(slB)
                npj = sems["pe"].n
                nv0 = V(lambda e, t=t0: e.tensor_tensor(
                    RES[:, t * D:(t + 1) * D], PB[:],
                    RES[:, t * D:(t + 1) * D], ALU.add), waits=[(pe, npj)])
                V(lambda e, t=t1: e.tensor_tensor(
                    RES[:, t * D:t * D + 512], PQ[:, 0:512],
                    RES[:, t * D:t * D + 512], ALU.add))
                V(lambda e, t=t1: e.tensor_tensor(
                    RES[:, t * D + 512:t * D + 1024], PQ[:, 512:1024],
                    RES[:, t * D + 512:t * D + 1024], ALU.add))
                V(lambda e, t=t1: e.tensor_tensor(
                    RES[:, t * D + 1024:t * D + 1536], PT1[:, 0:512],
                    RES[:, t * D + 1024:t * D + 1536], ALU.add))
                nv1 = V(lambda e, t=t1: e.tensor_tensor(
                    RES[:, t * D + 1536:t * D + 2048], PT2[:, 0:512],
                    RES[:, t * D + 1536:t * D + 2048], ALU.add))
                kb.wait("tensor", dv, nv1)

        kb.barrier()
        for t in range(NT):
            ST.load1(out_d[t * 128:(t + 1) * 128, :], RES[:, t * D:(t + 1) * D])
        for (sm_, tt_) in ST.totals():
            kb.wait("sync", sm_, tt_)

        with nc.Block() as block:
            @block.sync
            def _(e):
                for fn in kb.q["sync"]:
                    fn(e)

            @block.tensor
            def _(e):
                for fn in kb.q["tensor"]:
                    fn(e)

            @block.vector
            def _(e):
                for fn in kb.q["vector"]:
                    fn(e)

            @block.scalar
            def _(e):
                for fn in kb.q["scalar"]:
                    fn(e)

            @block.gpsimd
            def _(e):
                for fn in kb.q["gpsimd"]:
                    fn(e)

    return nc


def _vtile(v):
    v = np.asarray(v, dtype=np.float32)
    ndi, ndo = v.shape[0] // 128, v.shape[1] // 128
    t = v.reshape(ndi, 128, ndo, 128).transpose(2, 1, 0, 3)
    return np.ascontiguousarray(t.reshape(ndo, 128, ndi * 128))


def prep_inputs(kw):
    X = np.asarray(kw["L0_attn_x"], dtype=np.float32)
    keys = np.asarray(kw["L0_ffn_keys"], dtype=np.float32)
    common = {
        "l0_wq": _vtile(np.asarray(kw["L0_attn_Wq"]).T),
        "l0_wk": _vtile(np.asarray(kw["L0_attn_Wk"]).T),
        "l0_wv": _vtile(np.asarray(kw["L0_attn_Wv"]).T),
        "l0_wo": np.ascontiguousarray(np.asarray(kw["L0_attn_Wo"]).T),
        "l0_bq": np.ascontiguousarray(np.asarray(kw["L0_attn_bq"],
                                                 dtype=np.float32)
                                      .reshape(16, 128).T),
        "l0_bk": np.ascontiguousarray(np.asarray(kw["L0_attn_bk"],
                                                 dtype=np.float32)
                                      .reshape(16, 128).T),
        "l0_bv": np.ascontiguousarray(np.asarray(kw["L0_attn_bv"],
                                                 dtype=np.float32)
                                      .reshape(16, 128).T),
        "l0_bo": np.ascontiguousarray(np.asarray(kw["L0_attn_bo"],
                                                 dtype=np.float32)
                                      .reshape(1, D)),
        "f0_wq": _vtile(np.asarray(kw["L0_ffn_Wq"])),
        "f0_k1t": np.ascontiguousarray(keys[:, :, 0, :].transpose(0, 2, 1)),
        "f0_k2t": np.ascontiguousarray(keys[:, :, 1, :].transpose(0, 2, 1)),
        "f0_down": np.ascontiguousarray(np.asarray(kw["L0_ffn_e_down"],
                                                   dtype=np.float32)),
        "f0_up": np.ascontiguousarray(np.asarray(kw["L0_ffn_e_up"],
                                                 dtype=np.float32)),
        "l1_wqkv": _vtile(np.asarray(kw["L1_attn_w_qkv"])),
        "l1_wo": np.ascontiguousarray(np.asarray(kw["L1_attn_w_o"],
                                                 dtype=np.float32)),
        "f1_w1t": _vtile(np.asarray(kw["L1_ffn_w1"]).T),
        "f1_w2t": _vtile(np.asarray(kw["L1_ffn_w2"]).T),
        "f1_w3t": np.ascontiguousarray(np.asarray(kw["L1_ffn_w3"],
                                                  dtype=np.float32).T),
        "c_ident": np.eye(128, dtype=np.float32),
        "c_iota": np.broadcast_to(np.arange(64, dtype=np.float32),
                                  (128, 64)).copy(),
    }
    in_maps = []
    for c in range(8):
        b, half = c // 2, c % 2
        m = dict(common)
        m["x_in"] = np.ascontiguousarray(X[b, half * TLOC:(half + 1) * TLOC, :])
        m["cs_rows"] = np.arange(NH, dtype=np.int32).reshape(NH, 1) * 2 + half
        in_maps.append(m)
    return in_maps


def kernel(**kw):
    in_maps = prep_inputs(kw)
    nc = build_nc(phases=4)
    res = run_bass_kernel_spmd(nc, in_maps, core_ids=list(range(8)))
    out = np.empty((4, 2048, D), dtype=np.float32)
    for c in range(8):
        b, half = c // 2, c % 2
        out[b, half * TLOC:(half + 1) * TLOC, :] = res.results[c]["out"]
    return out
